# version 09
# baseline (speedup 1.0000x reference)
import math
from contextlib import ExitStack

import numpy as np

import concourse.bass as bass
import concourse.mybir as mybir
from concourse.bass_utils import run_bass_kernel_spmd

F32 = mybir.dt.float32
BF16 = mybir.dt.bfloat16
AF = mybir.ActivationFunctionType
ALU = mybir.AluOpType
AX = mybir.AxisListType

T = 2048
D = 1024
HH = 4
KD = 512
VD = 1024
VH = 256
C = 128
CSUB = 32
NT = T // 128
NC_ = D // 128
NCH = T // C
DFF = 4096
NE = 9
EPS = 1e-5
SCALE = 1.0 / math.sqrt(1.25)

A16_N = 75776
A32_N = 14848

ENGS = ("tensor", "vector", "scalar", "sync", "gpsimd")
DMA_ENGS = ("sync", "gpsimd")


class Buf:
    __slots__ = ("name", "last_w", "readers")

    def __init__(self, name):
        self.name = name
        self.last_w = None
        self.readers = []


class View:
    __slots__ = ("arena", "off", "dims", "buf", "nelem")

    def __init__(self, arena, off, dims, name, deps):
        self.arena = arena
        self.off = off
        self.dims = tuple(dims)
        self.nelem = int(np.prod(dims)) if dims else 1
        self.buf = Buf(name)
        self.buf.readers = list(deps)

    def ap(self):
        a = self.arena.h[:, self.off:self.off + self.nelem]
        if len(self.dims) > 1:
            pat = "p (" + " ".join(f"d{i}" for i in range(len(self.dims))) + ") -> p " + \
                  " ".join(f"d{i}" for i in range(len(self.dims)))
            a = a.rearrange(pat, **{f"d{i}": d for i, d in enumerate(self.dims)})
        return a

    def __getitem__(self, idx):
        return self.ap()[idx]


class Arena:
    def __init__(self, handle, n):
        self.h = handle
        self.free_list = [(0, n)]
        self.deps = []

    def alloc(self, dims, name):
        ne = int(np.prod(dims))
        for i, (off, sz) in enumerate(self.free_list):
            aoff = (off + 15) // 16 * 16
            pad = aoff - off
            if sz >= ne + pad:
                rest = sz - ne - pad
                repl = []
                if pad:
                    repl.append((off, pad))
                if rest:
                    repl.append((aoff + ne, rest))
                self.free_list[i:i + 1] = repl
                return View(self, aoff, dims, name, self.deps)
        raise RuntimeError(f"arena OOM for {name} ({ne})")

    def free(self, *views):
        for v in views:
            self.free_list.append((v.off, v.nelem))
            if v.buf.last_w:
                self.deps.append(v.buf.last_w)
            self.deps.extend(v.buf.readers)
        self.free_list.sort()
        merged = []
        for off, sz in self.free_list:
            if merged and merged[-1][0] + merged[-1][1] >= off:
                po, ps_ = merged[-1]
                merged[-1] = (po, max(ps_, off + sz - po))
            else:
                merged.append((off, sz))
        self.free_list = merged


class P:

    def __init__(self, nc):
        self.nc = nc
        self.stack = None
        self.prog = {e: [] for e in ENGS}
        self.cnt = {e: 0 for e in ENGS}
        self.waited = {e: {} for e in ENGS}
        self.sems = {}

    def getsem(self, name):
        if name not in self.sems:
            self.sems[name] = self.stack.enter_context(self.nc.semaphore(name))
            self.cnt[name] = 0
        return name

    def op(self, eng, fn, reads=(), writes=(), war=(), inc=True, dsem=None):
        waits = {}
        self_sync = eng != "tensor"

        def add(sem, cntv):
            if (sem != eng or self_sync) and cntv > waits.get(sem, -1):
                waits[sem] = cntv

        for b in reads:
            if b.last_w:
                add(*b.last_w)
        for b in tuple(writes) + tuple(war):
            if b.last_w:
                add(*b.last_w)
            for r in b.readers:
                add(*r)
        wl = []
        wd = self.waited[eng]
        for sem, v in waits.items():
            if wd.get(sem, -1) < v:
                wd[sem] = v
                wl.append((sem, v))
        if eng in DMA_ENGS:
            if dsem is None:
                assert writes, "DMA op needs writes or dsem"
                dsem = "b_" + writes[0].name
            csem = self.getsem(dsem)
            amt = 16
        else:
            csem = eng
            amt = 1
        if inc:
            self.cnt[csem] += amt
            mark = (csem, self.cnt[csem])
            for b in writes:
                b.last_w = mark
                b.readers = []
            for b in reads:
                b.readers.append(mark)
        else:
            assert not writes
        sems = self.sems

        def run(E, wl=wl, fn=fn, inc=inc, csem=csem, amt=amt):
            for sem, v in wl:
                E.wait_ge(sems[sem], v)
            ins = fn(E)
            if inc:
                ins.then_inc(sems[csem], amt)
        self.prog[eng].append(run)

    def raw_wait(self, eng, sem, val):
        if val <= 0:
            return
        wd = self.waited[eng]
        if sem != eng and wd.get(sem, -1) >= val:
            return
        if sem != eng:
            wd[sem] = val
        sems = self.sems
        self.prog[eng].append(lambda E: E.wait_ge(sems[sem], val))


def host_prep(inp):
    g = {}
    g["w_x1t"] = inp["L0_attn_W_x1"].T
    g["w_x2s"] = inp["L0_attn_W_x2"].reshape(D, 5, 32).transpose(2, 1, 0)
    g["xbias_t"] = inp["L0_attn_x_bias"].reshape(5, NC_, 128).transpose(2, 0, 1)
    g["mu_x_t"] = inp["L0_attn_mu_x"].reshape(NC_, 128).T
    g["w_rt"] = inp["L0_attn_W_r"].T
    g["w_kt"] = inp["L0_attn_W_k"].T
    g["a_wt"] = inp["L0_attn_A_w"].T
    g["b_wt"] = inp["L0_attn_B_w"].T
    g["bw_t"] = inp["L0_attn_b_w"].reshape(4, 128).T
    g["w_vt"] = inp["L0_attn_W_v"].T
    g["w_gt"] = inp["L0_attn_W_g"].T
    g["w_ot"] = inp["L0_attn_W_o"].T
    g["u_t"] = inp["L0_attn_bonus"].T
    g["gnw_rep"] = np.broadcast_to(inp["L0_attn_gn_weight"], (128, D))
    g["gnb_rep"] = np.broadcast_to(inp["L0_attn_gn_bias"], (128, D))
    pm = np.zeros((128, 4, CSUB), np.float32)
    for i in range(4):
        s = np.arange(128)[:, None]
        t = i * CSUB + np.arange(CSUB)[None, :]
        pm[:, i, :] = (s < t) & (s < (i + 1) * CSUB)
    g["pmask"] = pm
    g["ident"] = np.eye(128, dtype=np.float32)
    g["w_routert"] = inp["L0_ffn_w_router"].T
    g["b2ext"] = np.concatenate([inp["L0_ffn_b2"], inp["L0_ffn_bs2"][None]], 0)
    w1 = np.concatenate([inp["L0_ffn_W1"], inp["L0_ffn_Ws1"][None]], 0)
    w2 = np.concatenate([inp["L0_ffn_W2"], inp["L0_ffn_Ws2"][None]], 0)
    b1 = np.concatenate([inp["L0_ffn_b1"], inp["L0_ffn_bs1"][None]], 0)
    g["w1p"] = w1.transpose(0, 2, 1).reshape(NE, D, 32, 128).transpose(0, 2, 1, 3)
    g["w2p"] = w2.transpose(0, 2, 1)
    g["b1t"] = b1.reshape(NE, 32, 128).transpose(2, 0, 1)
    return {k: np.ascontiguousarray(v, dtype=np.float32) for k, v in g.items()}


DRAM_SPECS = [
    ("w_x1t", (D, 160)), ("w_x2s", (32, 5, D)), ("xbias_t", (128, 5, NC_)),
    ("mu_x_t", (128, NC_)), ("w_rt", (D, KD)), ("w_kt", (D, KD)),
    ("a_wt", (D, 64)), ("b_wt", (64, KD)), ("bw_t", (128, 4)),
    ("w_vt", (D, VD)), ("w_gt", (D, VD)), ("w_ot", (D, D)),
    ("u_t", (128, 4)), ("gnw_rep", (128, D)), ("gnb_rep", (128, D)),
    ("pmask", (128, 4, CSUB)), ("ident", (128, 128)),
    ("w_routert", (D, 8)), ("b2ext", (NE, D)),
    ("w1p", (NE, 32, D, 128)), ("w2p", (NE, DFF, D)), ("b1t", (128, NE, 32)),
]

FP_PER_A = 9


def build(nc, dbg=()):
    p = P(nc)
    da = {}
    da["x"] = nc.dram_tensor("x", [T, D], F32, kind="ExternalInput").ap()
    for name, shape in DRAM_SPECS:
        da[name] = nc.dram_tensor(name, list(shape), F32, kind="ExternalInput").ap()
    out_ap = nc.dram_tensor("out", [T, D], F32, kind="ExternalOutput").ap()
    scr = {
        "s": nc.dram_tensor("s_scr", [T, D], F32, kind="Internal").ap(),
        "w1s": nc.dram_tensor("w1s", [NE, 32, D, 128], BF16, kind="Internal").ap(),
        "w2s": nc.dram_tensor("w2s", [NE, DFF, D], BF16, kind="Internal").ap(),
        "rk": nc.dram_tensor("rk_scr", [128, 8, T], BF16, kind="Internal").ap(),
        "ew": nc.dram_tensor("ew_scr", [128, 4, T], F32, kind="Internal").ap(),
        "o": nc.dram_tensor("o_scr", [128, NT, VD], BF16, kind="Internal").ap(),
        "acc": nc.dram_tensor("acc_scr", [2, 128, 8, D], F32, kind="Internal").ap(),
    }
    scrb = {k: Buf("scr_" + k) for k in scr}
    dbg_out = {}
    for name, shape in dbg:
        dbg_out[name] = nc.dram_tensor("dbg_" + name, list(shape), F32,
                                       kind="ExternalOutput").ap()

    with ExitStack() as st:
        a16h = st.enter_context(nc.sbuf_tensor("arena16", [128, A16_N], BF16))
        a32h = st.enter_context(nc.sbuf_tensor("arena32", [128, A32_N], F32))
        A16 = Arena(a16h, A16_N)
        A32 = Arena(a32h, A32_N)
        ps = [st.enter_context(nc.psum_tensor(f"ps{i}", [128, 1024], F32))
              for i in range(4)]
        psb = [Buf(f"ps{i}h{j}") for i in range(4) for j in range(2)]
        p.stack = st
        for e in ENGS:
            p.sems[e] = st.enter_context(nc.semaphore("sem_" + e))
        for e in range(NE):
            p.sems[f"cast{e}"] = st.enter_context(nc.semaphore(f"sem_cast{e}"))
            p.cnt[f"cast{e}"] = 0

        env = dict(da=da, out=out_ap, scr=scr, scrb=scrb, dbg=dbg_out,
                   ps=ps, psb=psb, A16=A16, A32=A32)
        ph_prologue(p, env)
        ph_router(p, env)
        units = make_ffn_units(p, env)
        state = {"i": 0}

        def fill(budget, allow_g):
            while state["i"] < len(units) and budget > 0:
                typ, cost, fn = units[state["i"]]
                if typ == "g" and not allow_g:
                    break
                fn()
                budget -= max(cost, 1)
                state["i"] += 1
        gen_attn(p, env, fill)
        while state["i"] < len(units):
            units[state["i"]][2]()
            state["i"] += 1
        ph_final(p, env)
        for name in dbg_out:
            if name in scr:
                p.op("gpsimd", lambda E, name=name: E.dma_start(
                    out=dbg_out[name], in_=scr[name]),
                    reads=[scrb[name]], dsem="b_dbg")
        if "b_dbg" in p.cnt:
            p.raw_wait("gpsimd", "b_dbg", p.cnt["b_dbg"])

        with nc.Block() as blk:
            for e in ENGS:
                def runner(E, e=e):
                    for fn in p.prog[e]:
                        fn(E)
                getattr(blk, e)(runner)
    return nc


def bankof(env, i, half):
    return (env["ps"][i][:, half * 512:(half + 1) * 512],
            [env["psb"][2 * i + half]])


def apsum(env):
    ps, psb = env["ps"], env["psb"]
    return dict(
        b4=(ps[2][:, 0:512], [psb[4]]), b5=(ps[2][:, 512:1024], [psb[5]]),
        b6=(ps[3][:, 0:512], [psb[6]]), b7=(ps[3][:, 512:1024], [psb[7]]),
        pA=(ps[2][:, 0:CSUB], [psb[4]]), pB=(ps[2][:, 512:512 + CSUB], [psb[5]]),
        dcell=(ps[2][:, 64:65], [psb[4]]), scell=(ps[2][:, 256:512], [psb[4]]),
        tp=(ps[2][:, 640:768], [psb[5]]),
        oev=(ps[3][:, 0:256], [psb[6]]), ood=(ps[3][:, 512:768], [psb[7]]),
    )


def emit_casts(p, env, experts):
    sems = p.sems
    for e in experts:
        def c1(E, e=e):
            E.dma_start(out=env["scr"]["w1s"][e], in_=env["da"]["w1p"][e]).then_inc(
                sems[f"cast{e}"], 16)

        def c2(E, e=e):
            E.dma_start(out=env["scr"]["w2s"][e], in_=env["da"]["w2p"][e]).then_inc(
                sems[f"cast{e}"], 16)
        p.prog["gpsimd"].append(c1)
        p.prog["gpsimd"].append(c2)


def ph_prologue(p, env):
    op, da, A16, A32, ps, psb = p.op, env["da"], env["A16"], env["A32"], env["ps"], env["psb"]

    xnT = A16.alloc((NC_, T), "xnT")
    onecol = A16.alloc((16,), "onecol")
    w_x2s = A16.alloc((5, D), "w_x2s")
    zT3 = A16.alloc((1, T), "zT3")
    zT4 = A16.alloc((1, T), "zT4")
    musb = A16.alloc((512,), "musb")
    musb2 = A16.alloc((512,), "musb2")
    ident = A32.alloc((128,), "ident")
    masks = A32.alloc((4, CSUB), "pmask")
    small32 = A32.alloc((64,), "small32")
    xbias = A32.alloc((5 * NC_,), "xbias")
    ones32 = A32.alloc((128,), "ones32")
    epsb = A32.alloc((2,), "epsb")
    env.update(xnT=xnT, onecol=onecol, w_x2s=w_x2s, zT3=zT3, zT4=zT4, musb=musb,
               musb2=musb2, ident=ident, masks=masks, small32=small32,
               xbias=xbias, ones32=ones32, epsb=epsb)
    env["MU_X"] = lambda c: small32[:, c:c + 1]
    env["BW"] = lambda kb: small32[:, 8 + kb:9 + kb]
    env["UU"] = lambda h: small32[:, 12 + h:13 + h]

    op("sync", lambda E: E.dma_start(out=ident.ap(), in_=da["ident"][:, :]),
       writes=[ident.buf])
    op("sync", lambda E: E.dma_start(out=masks.ap().rearrange("p a b -> p (a b)"),
                                     in_=da["pmask"][:, :, :].rearrange("p a b -> p (a b)")),
       writes=[masks.buf])
    op("sync", lambda E: E.dma_start(out=small32[:, 0:8], in_=da["mu_x_t"][:, :]),
       writes=[small32.buf])
    op("sync", lambda E: E.dma_start(out=small32[:, 8:12], in_=da["bw_t"][:, :]),
       writes=[small32.buf])
    op("sync", lambda E: E.dma_start(out=small32[:, 12:16], in_=da["u_t"][:, :]),
       writes=[small32.buf])
    op("sync", lambda E: E.dma_start(out=xbias.ap().rearrange("p (a b) -> p a b", a=5),
                                     in_=da["xbias_t"][:, :, :]),
       writes=[xbias.buf])
    op("vector", lambda E: E.memset(ones32.ap(), 1.0), writes=[ones32.buf])
    op("vector", lambda E: E.memset(onecol.ap(), 1.0), writes=[onecol.buf])
    op("vector", lambda E: E.memset(epsb.ap(), EPS), writes=[epsb.buf])
    op("gpsimd", lambda E: E.dma_start(
        out=w_x2s[0:32, :, :], in_=da["w_x2s"][:, :, :]), writes=[w_x2s.buf])

    xtile = [A32.alloc((D,), f"xtile{j}") for j in range(2)]
    xcn = A32.alloc((D,), "xcn")
    stat = A32.alloc((8,), "lnstat")
    sqv = A32.alloc((D,), "lnsq")
    for i in range(NT):
        xt = xtile[i % 2]
        op("sync", lambda E, i=i, xt=xt: E.dma_start(
            out=xt.ap(), in_=da["x"][i * 128:(i + 1) * 128, :]), writes=[xt.buf])
        op("vector", lambda E, xt=xt: E.reduce_sum(stat[:, 0:1], xt.ap(), axis=AX.X),
           reads=[xt.buf], writes=[stat.buf])
        op("vector", lambda E: E.tensor_scalar_mul(stat[:, 1:2], stat[:, 0:1], 1.0 / D),
           reads=[stat.buf], writes=[stat.buf])
        op("vector", lambda E, xt=xt: E.tensor_scalar(
            xcn.ap(), xt.ap(), stat[:, 1:2], None, ALU.subtract),
           reads=[xt.buf, stat.buf], writes=[xcn.buf])
        op("scalar", lambda E: E.activation(sqv.ap(), xcn.ap(), AF.Square),
           reads=[xcn.buf], writes=[sqv.buf])
        op("vector", lambda E: E.reduce_sum(stat[:, 2:3], sqv.ap(), axis=AX.X),
           reads=[sqv.buf], writes=[stat.buf])
        op("vector", lambda E: E.tensor_scalar_mul(stat[:, 3:4], stat[:, 2:3], 1.0 / D),
           reads=[stat.buf], writes=[stat.buf])
        op("scalar", lambda E: E.activation(stat[:, 4:5], stat[:, 3:4], AF.Ln, bias=epsb[:, 0:1]),
           reads=[stat.buf, epsb.buf], writes=[stat.buf])
        op("scalar", lambda E: E.activation(stat[:, 5:6], stat[:, 4:5], AF.Exp, scale=-0.5),
           reads=[stat.buf], writes=[stat.buf])
        op("vector", lambda E: E.tensor_scalar(xcn.ap(), xcn.ap(), stat[:, 5:6], None, ALU.mult),
           reads=[stat.buf, xcn.buf], writes=[xcn.buf])
        tp = ps[3][:, 640:768]
        for c in range(NC_):
            op("tensor", lambda E, c=c, tp=tp: E.transpose(
                tp, xcn[:, c * 128:(c + 1) * 128], ident.ap()),
               reads=[xcn.buf, ident.buf], writes=[psb[7]], war=[psb[7]])
            op("vector", lambda E, c=c, i=i, tp=tp: E.tensor_copy(
                xnT[:, c, i * 128:(i + 1) * 128], tp),
               reads=[psb[7]], writes=[xnT.buf])
    A32.free(xtile[0], xtile[1], xcn, stat, sqv)


def ph_router(p, env):
    op, da, A16, A32 = p.op, env["da"], env["A16"], env["A32"]
    ps, psb, xnT = env["ps"], env["psb"], env["xnT"]
    wrt = A16.alloc((NC_, 8), "wrt")
    gatesT = A16.alloc((T,), "gatesT")
    gates = A32.alloc((NT, 8), "gates")
    gf32 = A32.alloc((16,), "gf32")
    gtmp = A32.alloc((16,), "gtmp")
    b1t = A32.alloc((NE * 32,), "b1t")
    env.update(gates=gates, gatesT=gatesT, b1t=b1t)

    op("gpsimd", lambda E: E.dma_start(
        out=wrt.ap(), in_=da["w_routert"].rearrange("(c p) m -> p c m", p=128)),
       writes=[wrt.buf])
    op("sync", lambda E: E.dma_start(
        out=b1t.ap().rearrange("p (a b) -> p a b", a=NE), in_=da["b1t"][:, :, :]),
       writes=[b1t.buf])
    emit_casts(p, env, range(0, 3))

    for i in range(NT):
        t0 = i * 128
        bap, bbs = bankof(env, 0, 0)
        for c in range(NC_):
            op("tensor", lambda E, c=c, bap=bap, t0=t0: E.matmul(
                bap[:, 0:8], xnT[:, c, t0:t0 + 128], wrt[:, c, :],
                start=(c == 0), stop=(c == NC_ - 1)),
               reads=([xnT.buf, wrt.buf] if c in (0, NC_ - 1) else []),
               writes=bbs if c == NC_ - 1 else [],
               war=bbs if c == 0 else [], inc=(c == NC_ - 1))
        op("vector", lambda E, bap=bap: E.reduce_max(gtmp[:, 0:1], bap[:, 0:8], axis=AX.X),
           reads=bbs, writes=[gtmp.buf])
        op("vector", lambda E: E.tensor_scalar_mul(gtmp[:, 1:2], gtmp[:, 0:1], -1.0),
           reads=[gtmp.buf], writes=[gtmp.buf])
        op("scalar", lambda E, bap=bap, i=i: E.activation(
            gates[:, i, :], bap[:, 0:8], AF.Exp, bias=gtmp[:, 1:2]),
           reads=bbs + [gtmp.buf], writes=[gates.buf])
        op("vector", lambda E, i=i: E.reduce_sum(gtmp[:, 2:3], gates[:, i, :], axis=AX.X),
           reads=[gates.buf], writes=[gtmp.buf])
        op("vector", lambda E: E.reciprocal(gtmp[:, 3:4], gtmp[:, 2:3]),
           reads=[gtmp.buf], writes=[gtmp.buf])
        op("vector", lambda E, i=i: E.tensor_scalar(
            gates[:, i, :], gates[:, i, :], gtmp[:, 3:4], None, ALU.mult),
           reads=[gtmp.buf, gates.buf], writes=[gates.buf])
        op("vector", lambda E, i=i: E.reduce_max(gtmp[:, 4:5], gates[:, i, :], axis=AX.X),
           reads=[gates.buf], writes=[gtmp.buf])
        op("vector", lambda E, i=i: E.tensor_scalar(
            gtmp[:, 8:16], gates[:, i, :], gtmp[:, 4:5], None, ALU.is_ge),
           reads=[gates.buf, gtmp.buf], writes=[gtmp.buf])
        op("vector", lambda E, i=i: E.scalar_tensor_tensor(
            gtmp[:, 8:16], gtmp[:, 8:16], -1e30, gates[:, i, :], ALU.mult, ALU.add),
           reads=[gtmp.buf, gates.buf], writes=[gtmp.buf])
        op("vector", lambda E: E.reduce_max(gtmp[:, 5:6], gtmp[:, 8:16], axis=AX.X),
           reads=[gtmp.buf], writes=[gtmp.buf])
        op("vector", lambda E, i=i: E.tensor_scalar(
            gtmp[:, 8:16], gates[:, i, :], gtmp[:, 5:6], None, ALU.is_ge),
           reads=[gates.buf, gtmp.buf], writes=[gtmp.buf])
        op("vector", lambda E, i=i: E.tensor_mul(
            gates[:, i, :], gates[:, i, :], gtmp[:, 8:16]),
           reads=[gtmp.buf, gates.buf], writes=[gates.buf])
        op("vector", lambda E, i=i: E.tensor_copy(gf32[:, 0:8], gates[:, i, :]),
           reads=[gates.buf], writes=[gf32.buf])
        op("vector", lambda E: E.memset(gf32[:, 8:16], 1.0), writes=[gf32.buf])
        tpg, tpg_bb = ps[1][:, 0:128], [psb[2]]
        op("tensor", lambda E, tpg=tpg: E.transpose(
            tpg[0:16, :], gf32.ap(), env["ident"].ap()),
           reads=[gf32.buf, env["ident"].buf], writes=tpg_bb, war=tpg_bb)
        op("vector", lambda E, tpg=tpg, t0=t0: E.tensor_copy(
            gatesT[0:9, t0:t0 + 128], tpg[0:9, :]),
           reads=tpg_bb, writes=[gatesT.buf])
    A32.free(gf32, gtmp)
    A16.free(wrt)


def emit_delta(p, env, dbuf, t0, width):
    op, xnT = p.op, env["xnT"]
    for c in range(NC_):
        if t0 == 0:
            op("vector", lambda E, c=c: E.tensor_scalar_mul(
                dbuf[:, c, 0:1], xnT[:, c, 0:1], -1.0),
               reads=[xnT.buf], writes=[dbuf.buf])
            op("vector", lambda E, c=c, w=width: E.tensor_sub(
                dbuf[:, c, 1:w], xnT[:, c, 0:w - 1], xnT[:, c, 1:w]),
               reads=[xnT.buf], writes=[dbuf.buf])
        else:
            op("vector", lambda E, c=c, t0=t0, w=width: E.tensor_sub(
                dbuf[:, c, 0:w], xnT[:, c, t0 - 1:t0 + w - 1], xnT[:, c, t0:t0 + w]),
               reads=[xnT.buf], writes=[dbuf.buf])


def emit_xmix(p, env, zbuf, zrow, zoff, slot, dbuf, xmix, t0, width, pbanks):
    op, xnT, xbias = p.op, env["xnT"], env["xbias"]
    musb2 = (env["musb"], env["musb2"])
    for c in range(NC_):
        bap, bbs = pbanks[c % 2]
        mb = musb2[c % 2]
        op("tensor", lambda E, c=c, bap=bap, zrow=zrow, slot=slot, zoff=zoff, w=width: E.matmul(
            bap[:, 0:w], env["w_x2s"][0:32, slot, c * 128:(c + 1) * 128],
            zbuf[0:32, zrow, zoff:zoff + w], start=True, stop=True),
           reads=[zbuf.buf, env["w_x2s"].buf], writes=bbs, war=bbs)
        op("vector", lambda E, c=c, bap=bap, slot=slot, w=width, mb=mb: E.scalar_tensor_tensor(
            mb[:, 0:w], bap[:, 0:w], xbias[:, slot * NC_ + c:slot * NC_ + c + 1],
            dbuf[:, c, 0:w], ALU.add, ALU.mult),
           reads=bbs + [xbias.buf, dbuf.buf], writes=[mb.buf])
        op("vector", lambda E, c=c, t0=t0, w=width, mb=mb: E.tensor_add(
            xmix[:, c, 0:w], mb[:, 0:w], xnT[:, c, t0:t0 + w]),
           reads=[mb.buf, xnT.buf], writes=[xmix.buf])


def gen_attn(p, env, fill):
    op, da, A16, A32 = p.op, env["da"], env["A16"], env["A32"]
    xnT = env["xnT"]
    ap_ = apsum(env)
    zT3, zT4 = env["zT3"], env["zT4"]

    dbufA = A16.alloc((NC_, 512), "dbufA")
    xmixA = A16.alloc((NC_, 512), "xmixA")
    zbuf = A16.alloc((1, 512), "zbuf")
    wbuf = A16.alloc((NC_, 512), "wbufA")
    a_wt = A16.alloc((NC_, 64), "a_wt")
    b_wt = A16.alloc((KD,), "b_wt")
    t1w = A16.alloc((512,), "t1w")
    stage = A16.alloc((2, 512), "rkstage")
    wx1v = A16.alloc((NC_, 32), "wx1v")
    ewst = A32.alloc((4, 512), "ewstage")

    def load_wx1(slot):
        op("gpsimd", lambda E, slot=slot: E.dma_start(
            out=wx1v.ap(), in_=da["w_x1t"][:, slot * 32:(slot + 1) * 32]
            .rearrange("(c p) m -> p c m", p=128)),
           writes=[wx1v.buf])

    def emit_z(ts, ztarget, trow, zoff):
        t0 = ts * 512
        emit_delta(p, env, dbufA, t0, 512)
        for c in range(NC_):
            op("vector", lambda E, c=c, t0=t0: E.scalar_tensor_tensor(
                xmixA[:, c, :], dbufA[:, c, :], env["MU_X"](c),
                xnT[:, c, t0:t0 + 512], ALU.mult, ALU.add),
               reads=[dbufA.buf, xnT.buf], writes=[xmixA.buf])
        bap, bbs = ap_["b4"]
        for c in range(NC_):
            op("tensor", lambda E, c=c, bap=bap: E.matmul(
                bap[0:32, :], wx1v[:, c, :], xmixA[:, c, :],
                start=(c == 0), stop=(c == NC_ - 1)),
               reads=([wx1v.buf, xmixA.buf] if c in (0, NC_ - 1) else []),
               writes=bbs if c == NC_ - 1 else [],
               war=bbs if c == 0 else [], inc=(c == NC_ - 1))
        op("scalar", lambda E, bap=bap, ztarget=ztarget, trow=trow, zoff=zoff: E.activation(
            ztarget[0:32, trow, zoff:zoff + 512], bap[0:32, :], AF.Tanh),
           reads=bbs, writes=[ztarget.buf])

    for slot, wname, rkoff in ((0, "w_rt", 0), (2, "w_kt", 4)):
        load_wx1(slot)
        op("gpsimd", lambda E, wname=wname: E.dma_start(
            out=wbuf.ap(), in_=da[wname].rearrange("(c p) m -> p c m", p=128)),
           writes=[wbuf.buf])
        for ts in range(4):
            t0 = ts * 512
            emit_z(ts, zbuf, 0, 0)
            emit_xmix(p, env, zbuf, 0, 0, slot, dbufA, xmixA, t0, 512,
                      (ap_["b4"], ap_["b5"]))
            for kb in range(4):
                bap, bbs = ap_["b6"] if kb % 2 == 0 else ap_["b7"]
                for c in range(NC_):
                    op("tensor", lambda E, kb=kb, c=c, bap=bap: E.matmul(
                        bap, wbuf[:, c, kb * 128:(kb + 1) * 128], xmixA[:, c, :],
                        start=(c == 0), stop=(c == NC_ - 1)),
                       reads=([wbuf.buf, xmixA.buf] if c in (0, NC_ - 1) else []),
                       writes=bbs if c == NC_ - 1 else [],
                       war=bbs if c == 0 else [], inc=(c == NC_ - 1))
                sg = stage[:, kb % 2, :]
                op("vector", lambda E, bap=bap, sg=sg: E.tensor_copy(sg, bap),
                   reads=bbs, writes=[stage.buf])
                op("sync", lambda E, kb=kb, t0=t0, sg=sg, rkoff=rkoff: E.dma_start(
                    out=env["scr"]["rk"][:, rkoff + kb, t0:t0 + 512], in_=sg),
                   reads=[stage.buf], writes=[env["scrb"]["rk"]])
            fill(80, True)
        fill(150, True)

    load_wx1(1)
    op("gpsimd", lambda E: E.dma_start(
        out=a_wt.ap(), in_=da["a_wt"].rearrange("(c p) m -> p c m", p=128)),
       writes=[a_wt.buf])
    op("gpsimd", lambda E: E.dma_start(out=b_wt[0:64, :], in_=da["b_wt"][:, :]),
       writes=[b_wt.buf])
    for ts in range(4):
        t0 = ts * 512
        emit_z(ts, zbuf, 0, 0)
        emit_xmix(p, env, zbuf, 0, 0, 1, dbufA, xmixA, t0, 512,
                  (ap_["b4"], ap_["b5"]))
        bap, bbs = ap_["b4"]
        for c in range(NC_):
            op("tensor", lambda E, c=c, bap=bap: E.matmul(
                bap[0:64, :], a_wt[:, c, :], xmixA[:, c, :],
                start=(c == 0), stop=(c == NC_ - 1)),
               reads=([a_wt.buf, xmixA.buf] if c in (0, NC_ - 1) else []),
               writes=bbs if c == NC_ - 1 else [],
               war=bbs if c == 0 else [], inc=(c == NC_ - 1))
        op("scalar", lambda E, bap=bap: E.activation(t1w[0:64, :], bap[0:64, :], AF.Tanh),
           reads=bbs, writes=[t1w.buf])
        for kb in range(4):
            bap2, bbs2 = ap_["b6"] if kb % 2 == 0 else ap_["b7"]
            op("tensor", lambda E, kb=kb, bap2=bap2: E.matmul(
                bap2, b_wt[0:64, kb * 128:(kb + 1) * 128], t1w[0:64, :],
                start=True, stop=True),
               reads=[t1w.buf, b_wt.buf], writes=bbs2, war=bbs2)
            op("scalar", lambda E, kb=kb, bap2=bap2: E.activation(
                ewst[:, kb, :], bap2, AF.Exp, bias=env["BW"](kb)),
               reads=bbs2, writes=[ewst.buf])
        op("sync", lambda E, t0=t0: E.dma_start(
            out=env["scr"]["ew"][:, :, t0:t0 + 512], in_=ewst.ap()),
           reads=[ewst.buf], writes=[env["scrb"]["ew"]])
        fill(80, True)
    fill(150, True)

    for slot, ztgt in ((3, zT3), (4, zT4)):
        load_wx1(slot)
        for ts in range(4):
            emit_z(ts, ztgt, 0, ts * 512)
            fill(80, True)
    fill(150, True)

    A16.free(dbufA, xmixA, zbuf, wbuf, a_wt, b_wt, t1w, stage, wx1v)
    A32.free(ewst)

    w_vt = A16.alloc((NC_, VD), "w_vt")
    dbufB = A16.alloc((NC_, 128), "dbufB")
    xmixB = A16.alloc((NC_, 128), "xmixB")
    Vc = A16.alloc((VD,), "Vc")
    S_bf = [A16.alloc((VH,), f"S_bf{h}") for h in range(HH)]
    rkc = A16.alloc((8, 128), "rkc")
    ost = A16.alloc((VD,), "ostage")
    eqv = [A16.alloc((C,), f"eqv{j}") for j in range(2)]
    qbf = [A16.alloc((C,), f"qbf{j}") for j in range(2)]
    kbf = [A16.alloc((C,), f"kbf{j}") for j in range(2)]
    Pfull = [A16.alloc((C,), f"Pfull{j}") for j in range(2)]
    khT = [A16.alloc((C,), f"khT{j}") for j in range(2)]
    rku = [A16.alloc((C,), f"rku{j}") for j in range(2)]
    S_fp = [A32.alloc((VH,), f"S_fp{h}") for h in range(HH)]
    negbuf = [A32.alloc((130,), f"negbuf{j}") for j in range(2)]
    negneg = [A32.alloc((C,), f"negneg{j}") for j in range(2)]
    dcol = [A32.alloc((2,), f"dcol{j}") for j in range(2)]
    khat = [A32.alloc((C,), f"khat{j}") for j in range(2)]
    ewc = A32.alloc((4, 128), "ewchunk")
    masks, ident = env["masks"], env["ident"]

    op("gpsimd", lambda E: E.dma_start(
        out=w_vt.ap(), in_=da["w_vt"].rearrange("(c p) m -> p c m", p=128)),
       writes=[w_vt.buf])
    for h in range(HH):
        op("vector", lambda E, h=h: E.memset(S_fp[h].ap(), 0.0), writes=[S_fp[h].buf])
        op("vector", lambda E, h=h: E.memset(S_bf[h].ap(), 0.0), writes=[S_bf[h].buf])
    for j in range(2):
        op("vector", lambda E, j=j: E.memset(negbuf[j][:, 0:2], 0.0),
           writes=[negbuf[j].buf])
        op("vector", lambda E, j=j: E.memset(Pfull[j].ap(), 0.0),
           writes=[Pfull[j].buf])

    for ci in range(NCH):
        t0 = ci * C
        op("sync", lambda E, t0=t0: E.dma_start(
            out=rkc.ap(), in_=env["scr"]["rk"][:, :, t0:t0 + 128]),
           reads=[env["scrb"]["rk"]], writes=[rkc.buf])
        op("gpsimd", lambda E, t0=t0: E.dma_start(
            out=ewc.ap(), in_=env["scr"]["ew"][:, :, t0:t0 + 128]),
           reads=[env["scrb"]["ew"]], writes=[ewc.buf])
        emit_delta(p, env, dbufB, t0, 128)
        emit_xmix(p, env, zT3, 0, t0, 3, dbufB, xmixB, t0, 128,
                  (ap_["b4"], ap_["b5"]))
        for vh in range(2):
            bap, bbs = ap_["b4"] if vh == 0 else ap_["b5"]
            for c in range(NC_):
                def vmm(E, vh=vh, c=c, bap=bap, xmixB=xmixB, w_vt=w_vt):
                    return E.matmul(bap, xmixB[:, c, :],
                                    w_vt[:, c, vh * 512:(vh + 1) * 512],
                                    start=(c == 0), stop=(c == NC_ - 1))
                op("tensor", vmm,
                   reads=([xmixB.buf, w_vt.buf] if c in (0, NC_ - 1) else []),
                   writes=bbs if c == NC_ - 1 else [],
                   war=bbs if c == 0 else [], inc=(c == NC_ - 1))
            op("vector", lambda E, vh=vh, bap=bap: E.tensor_copy(
                Vc[:, vh * 512:(vh + 1) * 512], bap),
               reads=bbs, writes=[Vc.buf])
        if ci == 0 and "xmixB" in env["dbg"] and "zT3" in env["dbg"]:
            op("gpsimd", lambda E: E.dma_start(
                out=env["dbg"]["xmixB"][:, :],
                in_=xmixB.arena.h[:, xmixB.off:xmixB.off + xmixB.nelem]), reads=[xmixB.buf], dsem="b_dbg")
            op("gpsimd", lambda E: E.dma_start(
                out=env["dbg"]["zT3"][:, :], in_=zT3[0:32, 0, :]), reads=[zT3.buf], dsem="b_dbg")
            op("gpsimd", lambda E: E.dma_start(
                out=env["dbg"]["mb6"][:, :], in_=env["musb"][:, 0:128]),
                reads=[env["musb"].buf], dsem="b_dbg")
            op("gpsimd", lambda E: E.dma_start(
                out=env["dbg"]["mb7"][:, :], in_=env["musb2"][:, 0:128]),
                reads=[env["musb2"].buf], dsem="b_dbg")
            op("gpsimd", lambda E: E.dma_start(
                out=env["dbg"]["dbufB"][:, :],
                in_=dbufB.arena.h[:, dbufB.off:dbufB.off + dbufB.nelem]), reads=[dbufB.buf], dsem="b_dbg")

        if ci == 0 and "arena" in env["dbg"]:
            op("gpsimd", lambda E: E.dma_start(
                out=env["dbg"]["arena"][:, :], in_=A16.h[:, 28000:44368]),
                reads=[dbufB.buf, xmixB.buf, Vc.buf], dsem="b_dbg")
        for h in range(HH):
            j = h % 2
            o_ap, o_bb = ap_["oev"] if j == 0 else ap_["ood"]
            sd_ap, sd_bb = ap_["scell"]
            d_ap = ap_["dcell"][0]
            tp_ap, tp_bb = ap_["tp"]
            pm0, pm0_bb = ap_["pA"]
            pm1, pm1_bb = ap_["pB"]
            nbb = negbuf[j]
            nng = negneg[j]
            NB = lambda a, b, nbb=nbb: nbb[:, 2 + a:2 + b]
            NBP = lambda a, b, nbb=nbb: nbb[:, 1 + a:1 + b]
            eq, qb, kb_, Pf, kT_, rk_, dc, kh = (eqv[j], qbf[j], kbf[j], Pfull[j],
                                                 khT[j], rku[j], dcol[j], khat[j])
            vslice = Vc[:, h * VH:(h + 1) * VH]
            rrow = rkc[:, h, :]
            krow = rkc[:, 4 + h, :]
            op("vector", lambda E, h=h, NB=NB: E.tensor_tensor_scan(
                NB(0, 128), env["ones32"][:, 0:128], ewc[:, h, :], 0.0,
                ALU.mult, ALU.add),
               reads=[ewc.buf, env["ones32"].buf], writes=[nbb.buf])
            op("vector", lambda E, nng=nng, NB=NB: E.tensor_scalar_mul(
                nng.ap(), NB(0, 128), -1.0),
               reads=[nbb.buf], writes=[nng.buf])
            op("scalar", lambda E, eq=eq, NBP=NBP: E.activation(
                eq[:, 0:128], NBP(0, 128), AF.Exp, scale=-1.0),
               reads=[nbb.buf], writes=[eq.buf])
            op("vector", lambda E, qb=qb, eq=eq, rrow=rrow: E.tensor_mul(
                qb[:, 0:128], rrow, eq[:, 0:128]),
               reads=[rkc.buf, eq.buf], writes=[qb.buf])
            op("tensor", lambda E, h=h, o_ap=o_ap, qb=qb: E.matmul(
                o_ap, qb[:, 0:128], S_bf[h].ap(), start=True, stop=False),
               reads=[qb.buf, S_bf[h].buf], war=o_bb, inc=False)
            op("vector", lambda E, rk_=rk_, rrow=rrow, krow=krow: E.tensor_mul(
                rk_[:, 0:128], rrow, krow),
               reads=[rkc.buf], writes=[rk_.buf])
            op("vector", lambda E, h=h, rk_=rk_: E.tensor_scalar(
                rk_[:, 0:128], rk_[:, 0:128], env["UU"](h), None, ALU.mult),
               reads=[rk_.buf], writes=[rk_.buf])
            op("tensor", lambda E, d_ap=d_ap, rk_=rk_: E.matmul(
                d_ap, rk_[:, 0:128], env["onecol"][:, 0:1],
                start=True, stop=True),
               reads=[rk_.buf, env["onecol"].buf], writes=sd_bb, war=sd_bb)
            op("vector", lambda E, dc=dc, d_ap=d_ap: E.tensor_copy(dc[:, 0:1], d_ap),
               reads=sd_bb, writes=[dc.buf])
            for i in range(4):
                sw = (i + 1) * CSUB
                pm_ap, pm_bb = (pm0, pm0_bb) if i % 2 == 0 else (pm1, pm1_bb)
                mid = i * CSUB + CSUB // 2 - 1
                op("scalar", lambda E, i=i, mid=mid, eq=eq, NBP=NBP, NB=NB: E.activation(
                    eq[:, 0:CSUB], NBP(i * CSUB, (i + 1) * CSUB), AF.Exp,
                    scale=-1.0, bias=NB(mid, mid + 1)),
                   reads=[nbb.buf], writes=[eq.buf])
                op("vector", lambda E, i=i, qb=qb, eq=eq, rrow=rrow: E.tensor_mul(
                    qb[:, 0:CSUB], rrow[:, i * CSUB:(i + 1) * CSUB], eq[:, 0:CSUB]),
                   reads=[rkc.buf, eq.buf], writes=[qb.buf])
                op("scalar", lambda E, sw=sw, mid=mid, eq=eq, NB=NB, nng=nng: E.activation(
                    eq[:, 0:sw], NB(0, sw), AF.Exp, bias=nng[:, mid:mid + 1]),
                   reads=[nbb.buf, nng.buf], writes=[eq.buf])
                op("vector", lambda E, sw=sw, kb_=kb_, eq=eq, krow=krow: E.tensor_mul(
                    kb_[:, 0:sw], krow[:, 0:sw], eq[:, 0:sw]),
                   reads=[rkc.buf, eq.buf], writes=[kb_.buf])
                op("tensor", lambda E, sw=sw, pm_ap=pm_ap, kb_=kb_, qb=qb: E.matmul(
                    pm_ap[0:sw, :], kb_[:, 0:sw], qb[:, 0:CSUB],
                    start=True, stop=True),
                   reads=[kb_.buf, qb.buf], writes=pm_bb, war=pm_bb)
                op("vector", lambda E, i=i, sw=sw, pm_ap=pm_ap: E.tensor_mul(
                    pm_ap[0:sw, :], pm_ap[0:sw, :], masks[0:sw, i, :]),
                   reads=pm_bb + [masks.buf], writes=pm_bb)
                op("vector", lambda E, i=i, sw=sw, pm_ap=pm_ap, Pf=Pf: E.tensor_copy(
                    Pf[0:sw, i * CSUB:(i + 1) * CSUB], pm_ap[0:sw, :]),
                   reads=pm_bb, writes=[Pf.buf])
            op("tensor", lambda E, o_ap=o_ap, vslice=vslice, Pf=Pf: E.matmul(
                o_ap, Pf.ap(), vslice, start=False, stop=True),
               reads=[Pf.buf, Vc.buf], writes=o_bb, inc=True)
            op("vector", lambda E, h=h, vslice=vslice, o_ap=o_ap, dc=dc: E.scalar_tensor_tensor(
                ost[:, h * VH:(h + 1) * VH], vslice, dc[:, 0:1], o_ap,
                ALU.mult, ALU.add),
               reads=[Vc.buf, dc.buf] + o_bb, writes=[ost.buf])
            op("scalar", lambda E, eq=eq, NB=NB, nng=nng: E.activation(
                eq[:, 0:128], NB(0, 128), AF.Exp, bias=nng[:, 127:128]),
               reads=[nbb.buf, nng.buf], writes=[eq.buf])
            op("vector", lambda E, kh=kh, eq=eq, krow=krow: E.tensor_mul(
                kh.ap(), krow, eq[:, 0:128]),
               reads=[rkc.buf, eq.buf], writes=[kh.buf])
            op("tensor", lambda E, tp_ap=tp_ap, kh=kh: E.transpose(
                tp_ap, kh.ap(), ident.ap()),
               reads=[kh.buf, ident.buf], writes=tp_bb, war=tp_bb)
            op("vector", lambda E, tp_ap=tp_ap, kT_=kT_: E.tensor_copy(kT_.ap(), tp_ap),
               reads=tp_bb, writes=[kT_.buf])
            op("tensor", lambda E, vslice=vslice, sd_ap=sd_ap, kT_=kT_: E.matmul(
                sd_ap, kT_.ap(), vslice, start=True, stop=True),
               reads=[kT_.buf, Vc.buf], writes=sd_bb, war=sd_bb)
            op("scalar", lambda E, dc=dc, NB=NB: E.activation(
                dc[:, 1:2], NB(127, 128), AF.Exp, scale=-1.0),
               reads=[nbb.buf], writes=[dc.buf])
            op("vector", lambda E, h=h, dc=dc, sd_ap=sd_ap: E.scalar_tensor_tensor(
                S_fp[h].ap(), S_fp[h].ap(), dc[:, 1:2], sd_ap,
                ALU.mult, ALU.add),
               reads=[S_fp[h].buf, dc.buf] + sd_bb, writes=[S_fp[h].buf])
            op("vector", lambda E, h=h: E.tensor_copy(S_bf[h].ap(), S_fp[h].ap()),
               reads=[S_fp[h].buf], writes=[S_bf[h].buf])
            fill(12, False)
            if ci == 0 and h == 0 and "Pf" in env["dbg"]:
                op("gpsimd", lambda E, Pf=Pf: E.dma_start(
                    out=env["dbg"]["Pf"][:, :], in_=Pf.ap()), reads=[Pf.buf], dsem="b_dbg")
                op("gpsimd", lambda E: E.dma_start(
                    out=env["dbg"]["Vc"][:, :], in_=Vc.ap()), reads=[Vc.buf], dsem="b_dbg")
                op("sync", lambda E, nbb=nbb: E.dma_start(
                    out=env["dbg"]["nbb"][:, :], in_=nbb.ap()), reads=[nbb.buf], dsem="b_dbg")
                op("sync", lambda E, dc=dc: E.dma_start(
                    out=env["dbg"]["dc"][:, :], in_=dc.ap()), reads=[dc.buf], dsem="b_dbg")
                op("gpsimd", lambda E: E.dma_start(
                    out=env["dbg"]["ost"][:, :], in_=ost[:, 0:256]), reads=[ost.buf], dsem="b_dbg")
        op("gpsimd", lambda E, ci=ci: E.dma_start(
            out=env["scr"]["o"][:, ci, :], in_=ost.ap()),
           reads=[ost.buf], writes=[env["scrb"]["o"]])
        fill(220, True)

    A16.free(w_vt, dbufB, xmixB, Vc, *S_bf, rkc, ost, *eqv, *qbf, *kbf,
             *Pfull, *khT, *rku, zT3)
    A32.free(*S_fp, *negbuf, *negneg, *dcol, *khat, ewc)

    ordb = A16.alloc((VD,), "ord")
    stats = A32.alloc((NT, HH, 2), "gnstats")
    sq = A32.alloc((VH,), "gnsq")
    st8 = A32.alloc((8,), "gnst8")

    for i in range(NT):
        op("sync", lambda E, i=i: E.dma_start(
            out=ordb.ap(), in_=env["scr"]["o"][:, i, :]),
           reads=[env["scrb"]["o"]], writes=[ordb.buf])
        for h in range(HH):
            osl = ordb[:, h * VH:(h + 1) * VH]
            op("scalar", lambda E, osl=osl: E.activation(
                sq.ap(), osl, AF.Identity, accum_out=st8[:, 0:1]),
               reads=[ordb.buf], writes=[sq.buf, st8.buf])
            op("scalar", lambda E, osl=osl: E.activation(
                sq.ap(), osl, AF.Square, accum_out=st8[:, 1:2]),
               reads=[ordb.buf], writes=[sq.buf, st8.buf])
            op("vector", lambda E, i=i, h=h: E.tensor_scalar_mul(
                stats[:, i, h, 0:1], st8[:, 0:1], 1.0 / VH),
               reads=[st8.buf], writes=[stats.buf])
            op("vector", lambda E: E.tensor_scalar_mul(st8[:, 2:3], st8[:, 1:2], 1.0 / VH),
               reads=[st8.buf], writes=[st8.buf])
            op("vector", lambda E, i=i, h=h: E.tensor_mul(
                st8[:, 3:4], stats[:, i, h, 0:1], stats[:, i, h, 0:1]),
               reads=[stats.buf], writes=[st8.buf])
            op("vector", lambda E: E.tensor_sub(st8[:, 4:5], st8[:, 2:3], st8[:, 3:4]),
               reads=[st8.buf], writes=[st8.buf])
            op("scalar", lambda E: E.activation(st8[:, 5:6], st8[:, 4:5], AF.Ln,
                                                bias=env["epsb"][:, 0:1]),
               reads=[st8.buf, env["epsb"].buf], writes=[st8.buf])
            op("scalar", lambda E, i=i, h=h: E.activation(
                stats[:, i, h, 1:2], st8[:, 5:6], AF.Exp, scale=-0.5),
               reads=[st8.buf], writes=[stats.buf])
        fill(30, False)
    fill(150, True)

    wgo = A16.alloc((NC_, VD), "wgo")
    dbufB2 = A16.alloc((NC_, 128), "dbufB2")
    xmixB2 = A16.alloc((NC_, 128), "xmixB2")
    gnw = A16.alloc((D,), "gnw16")
    gnb = A16.alloc((D,), "gnb16")
    gsil = [A16.alloc((VD,), f"gsil{j}") for j in range(2)]
    o_n = A16.alloc((VD,), "o_n")
    sc16 = A16.alloc((VD,), "sc16")
    og = A32.alloc((VD,), "og")
    s_sb = A32.alloc((D,), "s_sb")

    op("gpsimd", lambda E: E.dma_start(out=gnw.ap(), in_=da["gnw_rep"][:, :]),
       writes=[gnw.buf])
    op("gpsimd", lambda E: E.dma_start(out=gnb.ap(), in_=da["gnb_rep"][:, :]),
       writes=[gnb.buf])

    for ip in range(NT // 2):
        tiles = (2 * ip, 2 * ip + 1)
        op("gpsimd", lambda E: E.dma_start(
            out=wgo.ap(), in_=da["w_gt"].rearrange("(c p) m -> p c m", p=128)),
           writes=[wgo.buf])
        for jj, i in enumerate(tiles):
            t0 = i * 128
            gs = gsil[jj]
            emit_delta(p, env, dbufB2, t0, 128)
            emit_xmix(p, env, zT4, 0, t0, 4, dbufB2, xmixB2, t0, 128,
                      (ap_["b6"], ap_["b7"]))
            for vh in range(2):
                bap, bbs = ap_["b6"] if vh == 0 else ap_["b7"]
                for c in range(NC_):
                    op("tensor", lambda E, vh=vh, c=c, bap=bap, xmixB2=xmixB2, wgo=wgo: E.matmul(
                        bap, xmixB2[:, c, :], wgo[:, c, vh * 512:(vh + 1) * 512],
                        start=(c == 0), stop=(c == NC_ - 1)),
                       reads=([xmixB2.buf, wgo.buf] if c in (0, NC_ - 1) else []),
                       writes=bbs if c == NC_ - 1 else [],
                       war=bbs if c == 0 else [], inc=(c == NC_ - 1))
                op("scalar", lambda E, vh=vh, bap=bap: E.activation(
                    sc16[:, vh * 512:(vh + 1) * 512], bap, AF.Tanh, scale=0.5),
                   reads=bbs, writes=[sc16.buf])
                op("vector", lambda E, vh=vh: E.tensor_scalar(
                    sc16[:, vh * 512:(vh + 1) * 512], sc16[:, vh * 512:(vh + 1) * 512],
                    0.5, 0.5, ALU.mult, ALU.add),
                   reads=[sc16.buf], writes=[sc16.buf])
                op("vector", lambda E, vh=vh, bap=bap, gs=gs: E.tensor_mul(
                    gs[:, vh * 512:(vh + 1) * 512], sc16[:, vh * 512:(vh + 1) * 512],
                    bap),
                   reads=[sc16.buf] + bbs, writes=[gs.buf])
        op("gpsimd", lambda E: E.dma_start(
            out=wgo.ap(), in_=da["w_ot"].rearrange("(c p) m -> p c m", p=128)),
           writes=[wgo.buf])
        for jj, i in enumerate(tiles):
            gs = gsil[jj]
            op("sync", lambda E, i=i: E.dma_start(
                out=ordb.ap(), in_=env["scr"]["o"][:, i, :]),
               reads=[env["scrb"]["o"]], writes=[ordb.buf])
            for h in range(HH):
                op("vector", lambda E, i=i, h=h: E.tensor_scalar(
                    o_n[:, h * VH:(h + 1) * VH], ordb[:, h * VH:(h + 1) * VH],
                    stats[:, i, h, 0:1], stats[:, i, h, 1:2],
                    ALU.subtract, ALU.mult),
                   reads=[ordb.buf, stats.buf], writes=[o_n.buf])
            op("vector", lambda E: E.tensor_mul(o_n.ap(), o_n.ap(), gnw.ap()),
               reads=[gnw.buf, o_n.buf], writes=[o_n.buf])
            op("vector", lambda E: E.tensor_add(o_n.ap(), o_n.ap(), gnb.ap()),
               reads=[gnb.buf, o_n.buf], writes=[o_n.buf])
            op("vector", lambda E, gs=gs: E.tensor_mul(og.ap(), o_n.ap(), gs.ap()),
               reads=[gs.buf, o_n.buf], writes=[og.buf])
            tp_ap, tp_bb = ap_["tp"]
            for c in range(NC_):
                op("tensor", lambda E, c=c, tp_ap=tp_ap: E.transpose(
                    tp_ap, og[:, c * 128:(c + 1) * 128], env["ident"].ap()),
                   reads=[og.buf, env["ident"].buf], writes=tp_bb, war=tp_bb)
                op("vector", lambda E, c=c, tp_ap=tp_ap: E.tensor_copy(
                    sc16[:, c * 128:(c + 1) * 128], tp_ap),
                   reads=tp_bb, writes=[sc16.buf])
            for oh in range(2):
                bap, bbs = ap_["b6"] if oh == 0 else ap_["b7"]
                for c in range(NC_):
                    op("tensor", lambda E, oh=oh, c=c, bap=bap: E.matmul(
                        bap, sc16[:, c * 128:(c + 1) * 128],
                        wgo[:, c, oh * 512:(oh + 1) * 512],
                        start=(c == 0), stop=(c == NC_ - 1)),
                       reads=([sc16.buf, wgo.buf] if c in (0, NC_ - 1) else []),
                       writes=bbs if c == NC_ - 1 else [],
                       war=bbs if c == 0 else [], inc=(c == NC_ - 1))
                op("vector", lambda E, oh=oh, bap=bap: E.tensor_copy(
                    s_sb[:, oh * 512:(oh + 1) * 512], bap),
                   reads=bbs, writes=[s_sb.buf])
            op("sync", lambda E, i=i: E.dma_start(
                out=env["scr"]["s"][i * 128:(i + 1) * 128, :], in_=s_sb.ap()),
               reads=[s_sb.buf], writes=[env["scrb"]["s"]])
        fill(120, True)
    A16.free(ordb, wgo, dbufB2, xmixB2, gnw, gnb, *gsil, o_n, sc16,
             env["zT4"], env["w_x2s"], env["musb"], env["musb2"])
    A32.free(stats, sq, st8, og, s_sb, env["xbias"])


def make_ffn_units(p, env):
    op, da, A16, A32 = p.op, env["da"], env["A16"], env["A32"]
    ps, psb, xnT = env["ps"], env["psb"], env["xnT"]
    gates, gatesT, b1t = env["gates"], env["gatesT"], env["b1t"]

    Hsb = A16.alloc((32, 512), "Hsb")
    w1b = [A16.alloc((4, NC_, 128), f"w1b{j}") for j in range(2)]
    w2b = [A16.alloc((2, 1024), f"w2b{j}") for j in range(2)]
    b2e = A16.alloc((D,), "b2e")
    acc = A32.alloc((8, D), "acc")
    env["acc_v"] = acc

    units = []

    def u_b2e():
        op("gpsimd", lambda E: E.dma_start(out=b2e[0:NE, :], in_=da["b2ext"][:, :]),
           writes=[b2e.buf])
    units.append(("q", 0, u_b2e))

    for th in range(2):
        t0 = th * 1024

        def u_init(th=th, t0=t0):
            for tb in range(8):
                tt = t0 + tb * 128
                for dh in range(2):
                    bap, bbs = bankof(env, 1, dh)
                    op("tensor", lambda E, bap=bap, tt=tt, dh=dh: E.matmul(
                        bap, gatesT[0:9, tt:tt + 128],
                        b2e[0:9, dh * 512:(dh + 1) * 512],
                        start=True, stop=True),
                       reads=[gatesT.buf, b2e.buf], writes=bbs, war=bbs)
                    op("vector", lambda E, tb=tb, dh=dh, bap=bap: E.tensor_copy(
                        acc[:, tb, dh * 512:(dh + 1) * 512], bap),
                       reads=bbs, writes=[acc.buf])
        units.append(("q", 32, u_init))

        for e in range(NE):
            if th == 0:
                nxt = [x for x in (3 + e * 2, 4 + e * 2) if x < NE]
                if nxt:
                    def u_cast(nxt=nxt):
                        emit_casts(p, env, nxt)
                    units.append(("q", 0, u_cast))
            for q in range(2):
                tq = t0 + q * 512
                for pnl in range(8):
                    def u_p1(e=e, pnl=pnl, tq=tq):
                        wb = w1b[pnl % 2]
                        qeng = "sync" if pnl % 2 == 0 else "gpsimd"
                        p.raw_wait(qeng, f"cast{e}", 32)
                        op(qeng, lambda E, e=e, pnl=pnl, wb=wb: E.dma_start(
                            out=wb.ap(),
                            in_=env["scr"]["w1s"][e, pnl * 4:(pnl + 1) * 4]
                            .rearrange("f (c p) m -> p f c m", p=128)),
                           writes=[wb.buf])
                        for fi in range(4):
                            ffb = pnl * 4 + fi
                            bap, bbs = bankof(env, (ffb % 2), (ffb // 2) % 2)
                            for c in range(NC_):
                                op("tensor", lambda E, wb=wb, fi=fi, c=c,
                                   bap=bap, tq=tq: E.matmul(
                                    bap, wb[:, fi, c, :], xnT[:, c, tq:tq + 512],
                                    start=(c == 0), stop=(c == NC_ - 1)),
                                   reads=([wb.buf, xnT.buf] if c in (0, NC_ - 1) else []),
                                   writes=bbs if c == NC_ - 1 else [],
                                   war=bbs if c == 0 else [], inc=(c == NC_ - 1))
                            op("scalar", lambda E, e=e, ffb=ffb, bap=bap: E.activation(
                                Hsb[:, ffb, :], bap, AF.Gelu_apprx_tanh,
                                bias=b1t[:, e * 32 + ffb:e * 32 + ffb + 1]),
                               reads=bbs + [b1t.buf], writes=[Hsb.buf])
                    units.append(("g", 32, u_p1))
                for pp in range(2):
                    for fp_ in range(16):
                        def u_p2(e=e, q=q, pp=pp, fp_=fp_):
                            wb2 = w2b[fp_ % 2]
                            qeng = "sync" if fp_ % 2 == 0 else "gpsimd"
                            p.raw_wait(qeng, f"cast{e}", 32)
                            op(qeng, lambda E, e=e, fp_=fp_, wb2=wb2: E.dma_start(
                                out=wb2.ap(),
                                in_=env["scr"]["w2s"][e, fp_ * 256:(fp_ + 1) * 256, :]
                                .rearrange("(f p) d -> p f d", p=128)),
                               writes=[wb2.buf])
                            for sub in range(2):
                                ffc = fp_ * 2 + sub
                                for tl in range(2):
                                    tloc = pp * 2 + tl
                                    for dh in range(2):
                                        first = (sub == 0 and tl == 0 and dh == 0)
                                        last = (sub == 1 and tl == 1 and dh == 1)
                                        op("tensor", lambda E, wb2=wb2, sub=sub,
                                           tl=tl, dh=dh, tloc=tloc, ffc=ffc: E.matmul(
                                            ps[tl][:, dh * 512:(dh + 1) * 512],
                                            Hsb[:, ffc, tloc * 128:(tloc + 1) * 128],
                                            wb2[:, sub, dh * 512:(dh + 1) * 512],
                                            start=(ffc == 0), stop=(ffc == 31)),
                                           reads=([wb2.buf, Hsb.buf] if (first or last) else []),
                                           writes=[psb[2 * tl + dh]] if ffc == 31 else [],
                                           war=[psb[2 * tl + dh]] if ffc == 0 else [],
                                           inc=(ffc == 31 or last))
                        units.append(("q", 16, u_p2))

                    def u_ev(e=e, q=q, pp=pp, th=th):
                        for tl in range(2):
                            tbg = q * 4 + pp * 2 + tl
                            ti = th * 8 + tbg
                            if e < 8:
                                op("vector", lambda E, tl=tl, tbg=tbg, ti=ti, e=e: E.scalar_tensor_tensor(
                                    acc[:, tbg, :], ps[tl][:, :], gates[:, ti, e:e + 1],
                                    acc[:, tbg, :], ALU.mult, ALU.add),
                                   reads=[psb[2 * tl], psb[2 * tl + 1], gates.buf, acc.buf],
                                   writes=[acc.buf])
                            else:
                                op("vector", lambda E, tl=tl, tbg=tbg: E.tensor_add(
                                    acc[:, tbg, :], acc[:, tbg, :], ps[tl][:, :]),
                                   reads=[psb[2 * tl], psb[2 * tl + 1], acc.buf],
                                   writes=[acc.buf])
                    units.append(("q", 8, u_ev))

        def u_store(th=th):
            op("sync", lambda E, th=th: E.dma_start(
                out=env["scr"]["acc"][th], in_=acc.ap()),
               reads=[acc.buf], writes=[env["scrb"]["acc"]])
        units.append(("q", 0, u_store))
    return units


def ph_final(p, env):
    op, da, A32 = p.op, env["da"], env["A32"]
    A32.free(env["acc_v"])
    xres = [A32.alloc((D,), f"xres{j}") for j in range(2)]
    sres = [A32.alloc((D,), f"sres{j}") for j in range(2)]
    ares = [A32.alloc((D,), f"ares{j}") for j in range(2)]
    gfin = A32.alloc((D,), "gfin")

    for i in range(NT):
        th, tb = i // 8, i % 8
        xr, sr, ar = xres[i % 2], sres[i % 2], ares[i % 2]
        op("sync", lambda E, i=i, xr=xr: E.dma_start(
            out=xr.ap(), in_=da["x"][i * 128:(i + 1) * 128, :]), writes=[xr.buf])
        op("sync", lambda E, i=i, sr=sr: E.dma_start(
            out=sr.ap(), in_=env["scr"]["s"][i * 128:(i + 1) * 128, :]),
           reads=[env["scrb"]["s"]], writes=[sr.buf])
        op("sync", lambda E, th=th, tb=tb, ar=ar: E.dma_start(
            out=ar.ap(), in_=env["scr"]["acc"][th, :, tb, :]),
           reads=[env["scrb"]["acc"]], writes=[ar.buf])
        op("scalar", lambda E, ar=ar: E.activation(
            gfin.ap(), ar.ap(), AF.Tanh, scale=SCALE / 2),
           reads=[ar.buf], writes=[gfin.buf])
        op("vector", lambda E: E.tensor_scalar(
            gfin.ap(), gfin.ap(), 0.5, 0.5, ALU.mult, ALU.add),
           reads=[gfin.buf], writes=[gfin.buf])
        op("vector", lambda E, ar=ar: E.scalar_tensor_tensor(
            gfin.ap(), ar.ap(), SCALE, gfin.ap(), ALU.mult, ALU.mult),
           reads=[ar.buf, gfin.buf], writes=[gfin.buf])
        op("vector", lambda E, sr=sr: E.tensor_mul(sr.ap(), sr.ap(), gfin.ap()),
           reads=[sr.buf, gfin.buf], writes=[sr.buf])
        op("vector", lambda E, xr=xr, sr=sr: E.tensor_add(sr.ap(), sr.ap(), xr.ap()),
           reads=[xr.buf, sr.buf], writes=[sr.buf])
        op("sync", lambda E, i=i, sr=sr: E.dma_start(
            out=env["out"][i * 128:(i + 1) * 128, :], in_=sr.ap()),
           reads=[sr.buf], dsem="b_out")
    p.raw_wait("sync", "b_out", p.cnt.get("b_out", 0))


def kernel(**inputs):
    nc = bass.Bass("TRN2", target_bir_lowering=False)
    build(nc)
    g = host_prep(inputs)
    x = np.ascontiguousarray(inputs["L0_attn_hidden_states"], dtype=np.float32)
    in_maps = []
    for c in range(8):
        m = dict(g)
        m["x"] = np.ascontiguousarray(x[c])
        in_maps.append(m)
    res = run_bass_kernel_spmd(nc, in_maps, core_ids=list(range(8)))
    return np.stack([res.results[c]["out"] for c in range(8)], axis=0)


# version 13
# speedup vs baseline: 1.3054x; 1.3054x over previous
import math
from contextlib import ExitStack

import numpy as np

import concourse.bass as bass
import concourse.mybir as mybir
from concourse.bass_utils import run_bass_kernel_spmd

F32 = mybir.dt.float32
BF16 = mybir.dt.bfloat16
AF = mybir.ActivationFunctionType
ALU = mybir.AluOpType
AX = mybir.AxisListType

T = 2048
D = 1024
HH = 4
KD = 512
VD = 1024
VH = 256
C = 128
CSUB = 32
NT = T // 128
NC_ = D // 128
NCH = T // C
DFF = 4096
NE = 9
EPS = 1e-5
SCALE = 1.0 / math.sqrt(1.25)

A16_N = 75776
A32_N = 14848

ENGS = ("tensor", "vector", "scalar", "sync", "gpsimd")
DMA_ENGS = ("sync", "gpsimd")


class Buf:
    __slots__ = ("name", "last_w", "readers")

    def __init__(self, name):
        self.name = name
        self.last_w = None
        self.readers = []


class View:
    __slots__ = ("arena", "off", "dims", "buf", "nelem")

    def __init__(self, arena, off, dims, name):
        self.arena = arena
        self.off = off
        self.dims = tuple(dims)
        self.nelem = int(np.prod(dims)) if dims else 1
        self.buf = Buf(name)

    def ap(self):
        a = self.arena.h[:, self.off:self.off + self.nelem]
        if len(self.dims) > 1:
            pat = "p (" + " ".join(f"d{i}" for i in range(len(self.dims))) + ") -> p " + \
                  " ".join(f"d{i}" for i in range(len(self.dims)))
            a = a.rearrange(pat, **{f"d{i}": d for i, d in enumerate(self.dims)})
        return a

    def __getitem__(self, idx):
        return self.ap()[idx]


class Arena:

    def __init__(self, handle, n):
        self.h = handle
        self.free_list = [(0, n)]

    def alloc(self, dims, name):
        ne = int(np.prod(dims))
        for i, (off, sz) in enumerate(self.free_list):
            aoff = (off + 15) // 16 * 16
            pad = aoff - off
            if sz >= ne + pad:
                rest = sz - ne - pad
                repl = []
                if pad:
                    repl.append((off, pad))
                if rest:
                    repl.append((aoff + ne, rest))
                self.free_list[i:i + 1] = repl
                return View(self, aoff, dims, name)
        raise RuntimeError(f"arena OOM for {name} ({ne})")

    def free(self, *views):
        for v in views:
            self.free_list.append((v.off, v.nelem))
        self.free_list.sort()
        merged = []
        for off, sz in self.free_list:
            if merged and merged[-1][0] + merged[-1][1] >= off:
                po, ps_ = merged[-1]
                merged[-1] = (po, max(ps_, off + sz - po))
            else:
                merged.append((off, sz))
        self.free_list = merged


class P:

    def __init__(self, nc):
        self.nc = nc
        self.stack = None
        self.prog = {e: [] for e in ENGS}
        self.cnt = {e: 0 for e in ENGS}
        self.waited = {e: {} for e in ENGS}
        self.sems = {}

    def getsem(self, name):
        if name not in self.sems:
            self.sems[name] = self.stack.enter_context(self.nc.semaphore(name))
            self.cnt[name] = 0
        return name

    def op(self, eng, fn, reads=(), writes=(), war=(), inc=True, dsem=None):
        waits = {}
        self_sync = eng != "tensor"

        def add(sem, cntv):
            if (sem != eng or self_sync) and cntv > waits.get(sem, -1):
                waits[sem] = cntv

        for b in reads:
            if b.last_w:
                add(*b.last_w)
        for b in tuple(writes) + tuple(war):
            if b.last_w:
                add(*b.last_w)
            for r in b.readers:
                add(*r)
        wl = []
        wd = self.waited[eng]
        for sem, v in waits.items():
            if wd.get(sem, -1) < v:
                wd[sem] = v
                wl.append((sem, v))
        if eng in DMA_ENGS or dsem is not None:
            if dsem is None:
                assert writes, "DMA op needs writes or dsem"
                dsem = "b_" + writes[0].name
            csem = self.getsem(dsem)
            amt = 16
        else:
            csem = eng
            amt = 1
        if inc:
            self.cnt[csem] += amt
            mark = (csem, self.cnt[csem])
            for b in writes:
                b.last_w = mark
                b.readers = []
            for b in reads:
                b.readers.append(mark)
        else:
            assert not writes
        sems = self.sems

        def run(E, wl=wl, fn=fn, inc=inc, csem=csem, amt=amt):
            for sem, v in wl:
                E.wait_ge(sems[sem], v)
            ins = fn(E)
            if inc:
                ins.then_inc(sems[csem], amt)
        self.prog[eng].append(run)

    def raw_wait(self, eng, sem, val):
        if val <= 0:
            return
        wd = self.waited[eng]
        if sem != eng and wd.get(sem, -1) >= val:
            return
        if sem != eng:
            wd[sem] = val
        sems = self.sems
        self.prog[eng].append(lambda E: E.wait_ge(sems[sem], val))

    def barrier(self):
        snap = dict(self.cnt)
        for e in ENGS:
            for o, v in snap.items():
                if o != e and o not in DMA_ENGS:
                    self.raw_wait(e, o, v)


def host_prep(inp):
    g = {}
    g["w_x1t"] = inp["L0_attn_W_x1"].T
    g["w_x2s"] = inp["L0_attn_W_x2"].reshape(D, 5, 32).transpose(2, 1, 0)
    g["xbias_t"] = inp["L0_attn_x_bias"].reshape(5, NC_, 128).transpose(2, 0, 1)
    g["mu_x_t"] = inp["L0_attn_mu_x"].reshape(NC_, 128).T
    g["w_rt"] = inp["L0_attn_W_r"].T
    g["w_kt"] = inp["L0_attn_W_k"].T
    g["a_wt"] = inp["L0_attn_A_w"].T
    g["b_wt"] = inp["L0_attn_B_w"].T
    g["bw_t"] = inp["L0_attn_b_w"].reshape(4, 128).T
    g["w_vt"] = inp["L0_attn_W_v"].T
    g["w_gt"] = inp["L0_attn_W_g"].T
    g["w_ot"] = inp["L0_attn_W_o"].T
    g["u_t"] = inp["L0_attn_bonus"].T
    g["gnw_rep"] = np.broadcast_to(inp["L0_attn_gn_weight"], (128, D))
    g["gnb_rep"] = np.broadcast_to(inp["L0_attn_gn_bias"], (128, D))
    pm = np.zeros((128, 4, CSUB), np.float32)
    for i in range(4):
        s = np.arange(128)[:, None]
        t = i * CSUB + np.arange(CSUB)[None, :]
        pm[:, i, :] = (s < t) & (s < (i + 1) * CSUB)
    g["pmask"] = pm
    g["ident"] = np.eye(128, dtype=np.float32)
    g["w_routert"] = inp["L0_ffn_w_router"].T
    g["b2ext"] = np.concatenate([inp["L0_ffn_b2"], inp["L0_ffn_bs2"][None]], 0)
    w1 = np.concatenate([inp["L0_ffn_W1"], inp["L0_ffn_Ws1"][None]], 0)
    w2 = np.concatenate([inp["L0_ffn_W2"], inp["L0_ffn_Ws2"][None]], 0)
    b1 = np.concatenate([inp["L0_ffn_b1"], inp["L0_ffn_bs1"][None]], 0)
    g["w1p"] = w1.transpose(0, 2, 1).reshape(NE, D, 32, 128).transpose(0, 2, 1, 3)
    g["w2p"] = w2.transpose(0, 2, 1)
    g["b1t"] = b1.reshape(NE, 32, 128).transpose(2, 0, 1)
    return {k: np.ascontiguousarray(v, dtype=np.float32) for k, v in g.items()}


DRAM_SPECS = [
    ("w_x1t", (D, 160)), ("w_x2s", (32, 5, D)), ("xbias_t", (128, 5, NC_)),
    ("mu_x_t", (128, NC_)), ("w_rt", (D, KD)), ("w_kt", (D, KD)),
    ("a_wt", (D, 64)), ("b_wt", (64, KD)), ("bw_t", (128, 4)),
    ("w_vt", (D, VD)), ("w_gt", (D, VD)), ("w_ot", (D, D)),
    ("u_t", (128, 4)), ("gnw_rep", (128, D)), ("gnb_rep", (128, D)),
    ("pmask", (128, 4, CSUB)), ("ident", (128, 128)),
    ("w_routert", (D, 8)), ("b2ext", (NE, D)),
    ("w1p", (NE, 32, D, 128)), ("w2p", (NE, DFF, D)), ("b1t", (128, NE, 32)),
]


def build(nc, dbg=()):
    p = P(nc)
    da = {}
    da["x"] = nc.dram_tensor("x", [T, D], F32, kind="ExternalInput").ap()
    for name, shape in DRAM_SPECS:
        da[name] = nc.dram_tensor(name, list(shape), F32, kind="ExternalInput").ap()
    out_ap = nc.dram_tensor("out", [T, D], F32, kind="ExternalOutput").ap()
    s_scr = nc.dram_tensor("s_scr", [T, D], F32, kind="Internal").ap()
    w1s = nc.dram_tensor("w1s", [NE, 32, D, 128], BF16, kind="Internal").ap()
    w2s = nc.dram_tensor("w2s", [NE, DFF, D], BF16, kind="Internal").ap()
    dbg_out = {}
    for name, shape in dbg:
        dbg_out[name] = nc.dram_tensor("dbg_" + name, list(shape), F32,
                                       kind="ExternalOutput").ap()

    with ExitStack() as st:
        a16h = st.enter_context(nc.sbuf_tensor("arena16", [128, A16_N], BF16))
        a32h = st.enter_context(nc.sbuf_tensor("arena32", [128, A32_N], F32))
        A16 = Arena(a16h, A16_N)
        A32 = Arena(a32h, A32_N)
        ps = [st.enter_context(nc.psum_tensor(f"ps{i}", [128, 1024], F32))
              for i in range(4)]
        psb = [Buf(f"ps{i}h{j}") for i in range(4) for j in range(2)]
        p.stack = st
        for e in ENGS:
            p.sems[e] = st.enter_context(nc.semaphore("sem_" + e))
        for e in range(NE):
            p.sems[f"cast{e}"] = st.enter_context(nc.semaphore(f"sem_cast{e}"))
            p.cnt[f"cast{e}"] = 0

        env = dict(da=da, out=out_ap, s_scr=s_scr, s_scr_buf=Buf("s_scr"),
                   w1s=w1s, w2s=w2s,
                   dbg=dbg_out, ps=ps, psb=psb, A16=A16, A32=A32)
        ph_prologue(p, env)
        ph_A(p, env)
        ph_B(p, env)
        ph_B2(p, env)
        ph_ffn(p, env)

        with nc.Block() as blk:
            for e in ENGS:
                def runner(E, e=e):
                    for fn in p.prog[e]:
                        fn(E)
                getattr(blk, e)(runner)
    return nc


def bankof(env, i, half):
    if half is None:
        return env["ps"][i][:, :], [env["psb"][2 * i], env["psb"][2 * i + 1]]
    return (env["ps"][i][:, half * 512:(half + 1) * 512],
            [env["psb"][2 * i + half]])


def dump16(p, env, view, name):
    if name not in env["dbg"]:
        return
    p.op("gpsimd", lambda E: E.dma_start(
        out=env["dbg"][name][:, :], in_=view.ap().rearrange(
            "p " + " ".join(f"d{i}" for i in range(len(view.dims))) +
            " -> p (" + " ".join(f"d{i}" for i in range(len(view.dims))) + ")")
        if len(view.dims) > 1 else view.ap()),
        reads=[view.buf], dsem="b_dbg")


def dump32(p, env, view, name):
    if name not in env["dbg"]:
        return
    p.op("sync", lambda E: E.dma_start(
        out=env["dbg"][name][:, :], in_=view.ap().rearrange(
            "p " + " ".join(f"d{i}" for i in range(len(view.dims))) +
            " -> p (" + " ".join(f"d{i}" for i in range(len(view.dims))) + ")")
        if len(view.dims) > 1 else view.ap()),
        reads=[view.buf], dsem="b_dbg")


def ph_prologue(p, env):
    op, da, A16, A32, ps, psb = p.op, env["da"], env["A16"], env["A32"], env["ps"], env["psb"]

    xnT = A16.alloc((NC_, T), "xnT")
    onecol = A16.alloc((8,), "onecol")
    w_x2s = A16.alloc((5, D), "w_x2s")
    zTb = A16.alloc((2, T), "zTb")
    musb = A16.alloc((512,), "musb")
    musb2 = A16.alloc((512,), "musb2")
    ident = A32.alloc((128,), "ident")
    masks = A32.alloc((4, CSUB), "pmask")
    small32 = A32.alloc((64,), "small32")
    xbias = A32.alloc((5 * NC_,), "xbias")
    ones32 = A32.alloc((128,), "ones32")
    epsb = A32.alloc((2,), "epsb")
    env.update(xnT=xnT, onecol=onecol, w_x2s=w_x2s, zTb=zTb, musb=musb,
               musb2=musb2, ident=ident, masks=masks, small32=small32,
               xbias=xbias, ones32=ones32, epsb=epsb)
    env["MU_X"] = lambda c: small32[:, c:c + 1]
    env["BW"] = lambda kb: small32[:, 8 + kb:9 + kb]
    env["UU"] = lambda h: small32[:, 12 + h:13 + h]

    op("sync", lambda E: E.dma_start(out=ident.ap(), in_=da["ident"][:, :]),
       writes=[ident.buf])
    op("sync", lambda E: E.dma_start(out=masks.ap().rearrange("p a b -> p (a b)"),
                                     in_=da["pmask"][:, :, :].rearrange("p a b -> p (a b)")),
       writes=[masks.buf])
    op("sync", lambda E: E.dma_start(out=small32[:, 0:8], in_=da["mu_x_t"][:, :]),
       writes=[small32.buf])
    op("sync", lambda E: E.dma_start(out=small32[:, 8:12], in_=da["bw_t"][:, :]),
       writes=[small32.buf])
    op("sync", lambda E: E.dma_start(out=small32[:, 12:16], in_=da["u_t"][:, :]),
       writes=[small32.buf])
    op("sync", lambda E: E.dma_start(out=xbias.ap().rearrange("p (a b) -> p a b", a=5),
                                     in_=da["xbias_t"][:, :, :]),
       writes=[xbias.buf])
    op("vector", lambda E: E.memset(ones32.ap(), 1.0), writes=[ones32.buf])
    op("vector", lambda E: E.memset(epsb.ap(), EPS), writes=[epsb.buf])
    op("vector", lambda E: E.memset(onecol.ap(), 1.0), writes=[onecol.buf])
    op("gpsimd", lambda E: E.dma_start(
        out=w_x2s[0:32, :, :], in_=da["w_x2s"][:, :, :]), writes=[w_x2s.buf])

    xtile = [A32.alloc((D,), f"xtile{j}") for j in range(2)]
    xcn = A32.alloc((D,), "xcn")
    stat = A32.alloc((8,), "lnstat")
    sqv = A32.alloc((D,), "lnsq")
    for i in range(NT):
        xt = xtile[i % 2]
        op("sync", lambda E, i=i, xt=xt: E.dma_start(
            out=xt.ap(), in_=da["x"][i * 128:(i + 1) * 128, :]), writes=[xt.buf])
        op("vector", lambda E, xt=xt: E.reduce_sum(stat[:, 0:1], xt.ap(), axis=AX.X),
           reads=[xt.buf], writes=[stat.buf])
        op("vector", lambda E: E.tensor_scalar_mul(stat[:, 1:2], stat[:, 0:1], 1.0 / D),
           reads=[stat.buf], writes=[stat.buf])
        op("vector", lambda E, xt=xt: E.tensor_scalar(
            xcn.ap(), xt.ap(), stat[:, 1:2], None, ALU.subtract),
           reads=[xt.buf, stat.buf], writes=[xcn.buf])
        op("scalar", lambda E: E.activation(sqv.ap(), xcn.ap(), AF.Square),
           reads=[xcn.buf], writes=[sqv.buf])
        op("vector", lambda E: E.reduce_sum(stat[:, 2:3], sqv.ap(), axis=AX.X),
           reads=[sqv.buf], writes=[stat.buf])
        op("vector", lambda E: E.tensor_scalar_mul(stat[:, 3:4], stat[:, 2:3], 1.0 / D),
           reads=[stat.buf], writes=[stat.buf])
        op("scalar", lambda E: E.activation(stat[:, 4:5], stat[:, 3:4], AF.Ln, bias=epsb[:, 0:1]),
           reads=[stat.buf, epsb.buf], writes=[stat.buf])
        op("scalar", lambda E: E.activation(stat[:, 5:6], stat[:, 4:5], AF.Exp, scale=-0.5),
           reads=[stat.buf], writes=[stat.buf])
        op("vector", lambda E: E.tensor_scalar(xcn.ap(), xcn.ap(), stat[:, 5:6], None, ALU.mult),
           reads=[stat.buf, xcn.buf], writes=[xcn.buf])
        tp = ps[3][:, 0:128]
        for c in range(NC_):
            op("tensor", lambda E, c=c, tp=tp: E.transpose(
                tp, xcn[:, c * 128:(c + 1) * 128], ident.ap()),
               reads=[xcn.buf, ident.buf], writes=[psb[6]], war=[psb[6]])
            op("vector", lambda E, c=c, i=i, tp=tp: E.tensor_copy(
                xnT[:, c, i * 128:(i + 1) * 128], tp),
               reads=[psb[6]], writes=[xnT.buf])
    p.barrier()
    A32.free(xtile[0], xtile[1], xcn, stat, sqv)
    dump16(p, env, xnT, "xnT")


def emit_casts(p, env, experts):
    sems = p.sems
    for e in experts:
        def c1(E, e=e):
            E.dma_start(out=env["w1s"][e], in_=env["da"]["w1p"][e]).then_inc(
                sems[f"cast{e}"], 16)
        def c2(E, e=e):
            E.dma_start(out=env["w2s"][e], in_=env["da"]["w2p"][e]).then_inc(
                sems[f"cast{e}"], 16)
        p.prog["gpsimd"].append(c1)
        p.prog["gpsimd"].append(c2)


def emit_delta(p, env, dbuf, t0, width):
    op, xnT = p.op, env["xnT"]
    for c in range(NC_):
        if t0 == 0:
            op("vector", lambda E, c=c: E.tensor_scalar_mul(
                dbuf[:, c, 0:1], xnT[:, c, 0:1], -1.0),
               reads=[xnT.buf], writes=[dbuf.buf])
            op("vector", lambda E, c=c, w=width: E.tensor_sub(
                dbuf[:, c, 1:w], xnT[:, c, 0:w - 1], xnT[:, c, 1:w]),
               reads=[xnT.buf], writes=[dbuf.buf])
        else:
            op("vector", lambda E, c=c, t0=t0, w=width: E.tensor_sub(
                dbuf[:, c, 0:w], xnT[:, c, t0 - 1:t0 + w - 1], xnT[:, c, t0:t0 + w]),
               reads=[xnT.buf], writes=[dbuf.buf])


def emit_xmix(p, env, zview, zrow, slot, dbuf, xmix, t0, width, pbanks):
    op, xnT, xbias = p.op, env["xnT"], env["xbias"]
    musb2 = (env["musb"], env["musb2"])
    for c in range(NC_):
        bap, bbs = pbanks[c % 2]
        mb = musb2[c % 2]
        op("tensor", lambda E, c=c, bap=bap, zrow=zrow, slot=slot, t0=t0, w=width: E.matmul(
            bap[:, 0:w], env["w_x2s"][0:32, slot, c * 128:(c + 1) * 128],
            zview[0:32, zrow, t0:t0 + w], start=True, stop=True),
           reads=[zview.buf, env["w_x2s"].buf], writes=bbs, war=bbs)
        op("vector", lambda E, c=c, bap=bap, slot=slot, w=width, mb=mb: E.scalar_tensor_tensor(
            mb[:, 0:w], bap[:, 0:w], xbias[:, slot * NC_ + c:slot * NC_ + c + 1],
            dbuf[:, c, 0:w], ALU.add, ALU.mult),
           reads=bbs + [xbias.buf, dbuf.buf], writes=[mb.buf])
        op("vector", lambda E, c=c, t0=t0, w=width, mb=mb: E.tensor_add(
            xmix[:, c, 0:w], mb[:, 0:w], xnT[:, c, t0:t0 + w]),
           reads=[mb.buf, xnT.buf], writes=[xmix.buf])


def emit_proj(p, env, xmix, wview, width, nkb, pbank_fn, evac_fn):
    op = p.op
    for kb in range(nkb):
        bap, bbs = pbank_fn(kb)
        for c in range(NC_):
            op("tensor", lambda E, kb=kb, c=c, bap=bap, w=width: E.matmul(
                bap, wview[:, c, kb * 128:(kb + 1) * 128], xmix[:, c, 0:w],
                start=(c == 0), stop=(c == NC_ - 1)),
               reads=([wview.buf, xmix.buf] if c in (0, NC_ - 1) else []),
               writes=bbs if c == NC_ - 1 else [],
               war=bbs if c == 0 else [],
               inc=(c == NC_ - 1))
        evac_fn(kb, bap, bbs)


def ph_A(p, env):
    op, da, A16, A32 = p.op, env["da"], env["A16"], env["A32"]
    ps, psb, xnT = env["ps"], env["psb"], env["xnT"]

    rT = A16.alloc((4, T), "rT")
    kT = A16.alloc((4, T), "kT")
    zTa = A16.alloc((3, T), "zTa")
    dbufA = A16.alloc((NC_, 512), "dbufA")
    xmixA = A16.alloc((NC_, 512), "xmixA")
    t1w = A16.alloc((512,), "t1w")
    w_x1t = A16.alloc((NC_, 160), "w_x1t")
    w_rt = A16.alloc((NC_, KD), "w_rt")
    w_kt = A16.alloc((NC_, KD), "w_kt")
    a_wt = A16.alloc((NC_, 64), "a_wt")
    b_wt = A16.alloc((KD,), "b_wt")
    ew = A32.alloc((4, T), "ew")
    env.update(rT=rT, kT=kT, ew=ew)
    zTb, musb, xbias = env["zTb"], env["musb"], env["xbias"]

    for name, view in [("w_x1t", w_x1t), ("w_rt", w_rt), ("w_kt", w_kt),
                       ("a_wt", a_wt)]:
        op("gpsimd", lambda E, name=name, view=view: E.dma_start(
            out=view.ap(), in_=da[name].rearrange("(c p) m -> p c m", p=128)),
           writes=[view.buf])
    op("gpsimd", lambda E: E.dma_start(out=b_wt[0:64, :], in_=da["b_wt"][:, :]),
       writes=[b_wt.buf])

    for ts in range(4):
        t0 = ts * 512
        emit_delta(p, env, dbufA, t0, 512)
        for c in range(NC_):
            op("vector", lambda E, c=c, t0=t0: E.scalar_tensor_tensor(
                xmixA[:, c, :], dbufA[:, c, :], env["MU_X"](c),
                xnT[:, c, t0:t0 + 512], ALU.mult, ALU.add),
               reads=[dbufA.buf, xnT.buf], writes=[xmixA.buf])
        zbank = [bankof(env, 0, 0), bankof(env, 0, 1), bankof(env, 1, 0),
                 bankof(env, 1, 1), bankof(env, 2, 0)]
        for n in range(5):
            bap, bbs = zbank[n]
            for c in range(NC_):
                op("tensor", lambda E, n=n, c=c, bap=bap: E.matmul(
                    bap[0:32, :], w_x1t[:, c, n * 32:(n + 1) * 32], xmixA[:, c, :],
                    start=(c == 0), stop=(c == NC_ - 1)),
                   reads=([w_x1t.buf, xmixA.buf] if c in (0, NC_ - 1) else []),
                   writes=bbs if c == NC_ - 1 else [],
                   war=bbs if c == 0 else [], inc=(c == NC_ - 1))
        for n in range(5):
            bap, bbs = zbank[n]
            tgt = (zTa, n) if n < 3 else (zTb, n - 3)
            op("scalar", lambda E, bap=bap, tgt=tgt, t0=t0: E.activation(
                tgt[0][0:32, tgt[1], t0:t0 + 512], bap[0:32, :], AF.Tanh),
               reads=bbs, writes=[tgt[0].buf])

        emit_xmix(p, env, zTa, 0, 0, dbufA, xmixA, t0, 512,
                  (bankof(env, 2, 1), bankof(env, 2, 0)))
        def evac_r(kb, bap, bbs, t0=t0):
            op("vector", lambda E, kb=kb, bap=bap: E.tensor_copy(
                rT[:, kb, t0:t0 + 512], bap),
               reads=bbs, writes=[rT.buf])
        emit_proj(p, env, xmixA, w_rt, 512, 4, lambda kb: bankof(env, 3, kb % 2), evac_r)

        emit_xmix(p, env, zTa, 1, 1, dbufA, xmixA, t0, 512,
                  (bankof(env, 2, 1), bankof(env, 2, 0)))
        bap, bbs = bankof(env, 0, 0)
        for c in range(NC_):
            op("tensor", lambda E, c=c, bap=bap: E.matmul(
                bap[0:64, :], a_wt[:, c, :], xmixA[:, c, :],
                start=(c == 0), stop=(c == NC_ - 1)),
               reads=([a_wt.buf, xmixA.buf] if c in (0, NC_ - 1) else []),
               writes=bbs if c == NC_ - 1 else [],
               war=bbs if c == 0 else [], inc=(c == NC_ - 1))
        op("scalar", lambda E, bap=bap: E.activation(t1w[0:64, :], bap[0:64, :], AF.Tanh),
           reads=bbs, writes=[t1w.buf])
        for kb in range(4):
            bap2, bbs2 = bankof(env, 1, kb % 2)
            op("tensor", lambda E, kb=kb, bap2=bap2: E.matmul(
                bap2, b_wt[0:64, kb * 128:(kb + 1) * 128], t1w[0:64, :],
                start=True, stop=True),
               reads=[t1w.buf, b_wt.buf], writes=bbs2, war=bbs2)
            op("scalar", lambda E, kb=kb, bap2=bap2, t0=t0: E.activation(
                ew[:, kb, t0:t0 + 512], bap2, AF.Exp, bias=env["BW"](kb)),
               reads=bbs2, writes=[ew.buf])

        emit_xmix(p, env, zTa, 2, 2, dbufA, xmixA, t0, 512,
                  (bankof(env, 2, 1), bankof(env, 2, 0)))
        def evac_k(kb, bap, bbs, t0=t0):
            op("vector", lambda E, kb=kb, bap=bap: E.tensor_copy(
                kT[:, kb, t0:t0 + 512], bap),
               reads=bbs, writes=[kT.buf])
        emit_proj(p, env, xmixA, w_kt, 512, 4, lambda kb: bankof(env, 3, kb % 2), evac_k)

    p.barrier()
    A16.free(zTa, dbufA, xmixA, t1w, w_x1t, w_rt, w_kt, a_wt, b_wt)
    dump16(p, env, rT, "rT")
    dump16(p, env, kT, "kT")
    dump32(p, env, ew, "ew")
    dump16(p, env, zTb, "zTb")


def ph_B(p, env):
    op, da, A16, A32 = p.op, env["da"], env["A16"], env["A32"]
    ps, psb, xnT = env["ps"], env["psb"], env["xnT"]
    rT, kT, ew, zTb, masks, ident = env["rT"], env["kT"], env["ew"], env["zTb"], env["masks"], env["ident"]

    w_vt = A16.alloc((NC_, VD), "w_vt")
    o_bf = A16.alloc((NT, VD), "o_bf")
    Vc = A16.alloc((VD,), "Vc")
    S_bf = [A16.alloc((VH,), f"S_bf{h}") for h in range(HH)]
    dbufB = A16.alloc((NC_, 128), "dbufB")
    xmixB = A16.alloc((NC_, 128), "xmixB")
    eqv = [A16.alloc((C,), f"eqv{j}") for j in range(2)]
    qbf = [A16.alloc((C,), f"qbf{j}") for j in range(2)]
    kbf = [A16.alloc((C,), f"kbf{j}") for j in range(2)]
    Pfull = [A16.alloc((C,), f"Pfull{j}") for j in range(2)]
    khT = [A16.alloc((C,), f"khT{j}") for j in range(2)]
    S_fp = [A32.alloc((VH,), f"S_fp{h}") for h in range(HH)]
    negbuf = [A32.alloc((130,), f"negbuf{j}") for j in range(2)]
    negneg = [A32.alloc((C,), f"negneg{j}") for j in range(2)]
    rku = [A16.alloc((C,), f"rku{j}") for j in range(2)]
    dcol = [A32.alloc((2,), f"dcol{j}") for j in range(2)]
    khat = [A32.alloc((C,), f"khat{j}") for j in range(2)]
    env.update(o_bf=o_bf, dbufB=dbufB, xmixB=xmixB)

    op("gpsimd", lambda E: E.dma_start(
        out=w_vt.ap(), in_=da["w_vt"].rearrange("(c p) m -> p c m", p=128)),
       writes=[w_vt.buf])
    emit_casts(p, env, range(0, 5))
    for h in range(HH):
        op("vector", lambda E, h=h: E.memset(S_fp[h].ap(), 0.0), writes=[S_fp[h].buf])
        op("vector", lambda E, h=h: E.memset(S_bf[h].ap(), 0.0), writes=[S_bf[h].buf])
    for j in range(2):
        op("vector", lambda E, j=j: E.memset(negbuf[j][:, 0:2], 0.0),
           writes=[negbuf[j].buf])
        op("vector", lambda E, j=j: E.memset(Pfull[j].ap(), 0.0),
           writes=[Pfull[j].buf])

    for ci in range(NCH):
        t0 = ci * C
        emit_delta(p, env, dbufB, t0, 128)
        emit_xmix(p, env, zTb, 0, 3, dbufB, xmixB, t0, 128,
                  (bankof(env, 2, 1), bankof(env, 2, 0)))
        for vh in range(2):
            bap, bbs = bankof(env, 0, vh)
            for c in range(NC_):
                op("tensor", lambda E, vh=vh, c=c, bap=bap: E.matmul(
                    bap, xmixB[:, c, :], w_vt[:, c, vh * 512:(vh + 1) * 512],
                    start=(c == 0), stop=(c == NC_ - 1)),
                   reads=([xmixB.buf, w_vt.buf] if c in (0, NC_ - 1) else []),
                   writes=bbs if c == NC_ - 1 else [],
                   war=bbs if c == 0 else [], inc=(c == NC_ - 1))
            op("vector", lambda E, vh=vh, bap=bap: E.tensor_copy(
                Vc[:, vh * 512:(vh + 1) * 512], bap),
               reads=bbs, writes=[Vc.buf])

        for h in range(HH):
            j = h % 2
            if j == 0:
                o_ap, o_bb = ps[1][:, 0:256], [psb[2]]
                sd_ap, d_ap, sd_bb = ps[3][:, 0:256], ps[3][:, 256:257], [psb[6]]
                tp_ap, tp_bb = ps[3][:, 640:768], [psb[7]]
            else:
                o_ap, o_bb = ps[1][:, 512:768], [psb[3]]
                sd_ap, d_ap, sd_bb = ps[0][:, 0:256], ps[0][:, 256:257], [psb[0]]
                tp_ap, tp_bb = ps[0][:, 640:768], [psb[1]]
            nbb = negbuf[j]
            nng = negneg[j]
            NB = lambda a, b, nbb=nbb: nbb[:, 2 + a:2 + b]
            NBP = lambda a, b, nbb=nbb: nbb[:, 1 + a:1 + b]
            eq, qb, kb, Pf, kT_, rk, dc, kh = (eqv[j], qbf[j], kbf[j], Pfull[j],
                                               khT[j], rku[j], dcol[j], khat[j])
            vslice = Vc[:, h * VH:(h + 1) * VH]
            op("vector", lambda E, h=h, t0=t0, NB=NB: E.tensor_tensor_scan(
                NB(0, 128), env["ones32"][:, 0:128], ew[:, h, t0:t0 + 128], 0.0,
                ALU.mult, ALU.add),
               reads=[ew.buf, env["ones32"].buf], writes=[nbb.buf])
            op("vector", lambda E, nng=nng, NB=NB: E.tensor_scalar_mul(nng.ap(), NB(0, 128), -1.0),
               reads=[nbb.buf], writes=[nng.buf])
            op("scalar", lambda E, eq=eq, NBP=NBP: E.activation(eq[:, 0:128], NBP(0, 128), AF.Exp, scale=-1.0),
               reads=[nbb.buf], writes=[eq.buf])
            op("vector", lambda E, h=h, t0=t0, qb=qb, eq=eq: E.tensor_mul(
                qb[:, 0:128], rT[:, h, t0:t0 + 128], eq[:, 0:128]),
               reads=[rT.buf, eq.buf], writes=[qb.buf])
            op("tensor", lambda E, h=h, o_ap=o_ap, qb=qb: E.matmul(
                o_ap, qb[:, 0:128], S_bf[h].ap(), start=True, stop=False),
               reads=[qb.buf, S_bf[h].buf], war=o_bb, inc=False)
            op("vector", lambda E, h=h, t0=t0, rk=rk: E.tensor_mul(
                rk[:, 0:128], rT[:, h, t0:t0 + 128], kT[:, h, t0:t0 + 128]),
               reads=[rT.buf, kT.buf], writes=[rk.buf])
            op("vector", lambda E, h=h, rk=rk: E.tensor_scalar(
                rk[:, 0:128], rk[:, 0:128], env["UU"](h), None, ALU.mult),
               reads=[rk.buf], writes=[rk.buf])
            op("tensor", lambda E, d_ap=d_ap, rk=rk: E.matmul(
                d_ap, rk[:, 0:128], env["onecol"][:, 0:1],
                start=True, stop=True),
               reads=[rk.buf, env["onecol"].buf], writes=sd_bb, war=sd_bb)
            op("vector", lambda E, dc=dc, d_ap=d_ap: E.tensor_copy(dc[:, 0:1], d_ap),
               reads=sd_bb, writes=[dc.buf])
            for i in range(4):
                sw = (i + 1) * CSUB
                pm_ap, pm_bb = ((ps[2][:, 0:CSUB], [psb[4]]) if i % 2 == 0
                                else (ps[2][:, 512:512 + CSUB], [psb[5]]))
                mid = i * CSUB + CSUB // 2 - 1
                op("scalar", lambda E, i=i, mid=mid, eq=eq, NBP=NBP, NB=NB: E.activation(
                    eq[:, 0:CSUB], NBP(i * CSUB, (i + 1) * CSUB), AF.Exp,
                    scale=-1.0, bias=NB(mid, mid + 1)),
                   reads=[nbb.buf], writes=[eq.buf])
                op("vector", lambda E, i=i, h=h, t0=t0, qb=qb, eq=eq: E.tensor_mul(
                    qb[:, 0:CSUB], rT[:, h, t0 + i * CSUB:t0 + (i + 1) * CSUB],
                    eq[:, 0:CSUB]),
                   reads=[rT.buf, eq.buf], writes=[qb.buf])
                op("scalar", lambda E, sw=sw, mid=mid, eq=eq, NB=NB, nng=nng: E.activation(
                    eq[:, 0:sw], NB(0, sw), AF.Exp, bias=nng[:, mid:mid + 1]),
                   reads=[nbb.buf, nng.buf], writes=[eq.buf])
                op("vector", lambda E, h=h, t0=t0, sw=sw, kb=kb, eq=eq: E.tensor_mul(
                    kb[:, 0:sw], kT[:, h, t0:t0 + sw], eq[:, 0:sw]),
                   reads=[kT.buf, eq.buf], writes=[kb.buf])
                op("tensor", lambda E, sw=sw, pm_ap=pm_ap, kb=kb, qb=qb: E.matmul(
                    pm_ap[0:sw, :], kb[:, 0:sw], qb[:, 0:CSUB],
                    start=True, stop=True),
                   reads=[kb.buf, qb.buf], writes=pm_bb, war=pm_bb)
                op("vector", lambda E, i=i, sw=sw, pm_ap=pm_ap: E.tensor_mul(
                    pm_ap[0:sw, :], pm_ap[0:sw, :], masks[0:sw, i, :]),
                   reads=pm_bb + [masks.buf], writes=pm_bb)
                op("vector", lambda E, i=i, sw=sw, pm_ap=pm_ap, Pf=Pf: E.tensor_copy(
                    Pf[0:sw, i * CSUB:(i + 1) * CSUB], pm_ap[0:sw, :]),
                   reads=pm_bb, writes=[Pf.buf])
            op("tensor", lambda E, o_ap=o_ap, vslice=vslice, Pf=Pf: E.matmul(
                o_ap, Pf.ap(), vslice, start=False, stop=True),
               reads=[Pf.buf, Vc.buf], writes=o_bb, inc=True)
            op("vector", lambda E, h=h, ci=ci, vslice=vslice, o_ap=o_ap, dc=dc: E.scalar_tensor_tensor(
                o_bf[:, ci, h * VH:(h + 1) * VH], vslice, dc[:, 0:1], o_ap,
                ALU.mult, ALU.add),
               reads=[Vc.buf, dc.buf] + o_bb, writes=[o_bf.buf])
            op("scalar", lambda E, eq=eq, NB=NB, nng=nng: E.activation(
                eq[:, 0:128], NB(0, 128), AF.Exp, bias=nng[:, 127:128]),
               reads=[nbb.buf, nng.buf], writes=[eq.buf])
            op("vector", lambda E, h=h, t0=t0, kh=kh, eq=eq: E.tensor_mul(
                kh.ap(), kT[:, h, t0:t0 + 128], eq[:, 0:128]),
               reads=[kT.buf, eq.buf], writes=[kh.buf])
            op("tensor", lambda E, tp_ap=tp_ap, kh=kh: E.transpose(tp_ap, kh.ap(), ident.ap()),
               reads=[kh.buf, ident.buf], writes=tp_bb, war=tp_bb)
            op("vector", lambda E, tp_ap=tp_ap, kT_=kT_: E.tensor_copy(kT_.ap(), tp_ap),
               reads=tp_bb, writes=[kT_.buf])
            op("tensor", lambda E, vslice=vslice, sd_ap=sd_ap, kT_=kT_: E.matmul(
                sd_ap, kT_.ap(), vslice, start=True, stop=True),
               reads=[kT_.buf, Vc.buf], writes=sd_bb, war=sd_bb)
            op("scalar", lambda E, dc=dc, NB=NB: E.activation(
                dc[:, 1:2], NB(127, 128), AF.Exp, scale=-1.0),
               reads=[nbb.buf], writes=[dc.buf])
            op("vector", lambda E, h=h, dc=dc, sd_ap=sd_ap: E.scalar_tensor_tensor(
                S_fp[h].ap(), S_fp[h].ap(), dc[:, 1:2], sd_ap,
                ALU.mult, ALU.add),
               reads=[S_fp[h].buf, dc.buf] + sd_bb, writes=[S_fp[h].buf])
            op("vector", lambda E, h=h: E.tensor_copy(S_bf[h].ap(), S_fp[h].ap()),
               reads=[S_fp[h].buf], writes=[S_bf[h].buf])
    p.barrier()
    A16.free(w_vt, Vc, *S_bf, *eqv, *qbf, *kbf, *Pfull, *khT, *rku,
             env["rT"], env["kT"])
    A32.free(*S_fp, *negbuf, *negneg, *dcol, *khat, env["ew"])
    dump16(p, env, o_bf, "o_bf")


def ph_B2(p, env):
    op, da, A16, A32 = p.op, env["da"], env["A16"], env["A32"]
    ps, psb, xnT = env["ps"], env["psb"], env["xnT"]
    o_bf, zTb, dbufB, xmixB = env["o_bf"], env["zTb"], env["dbufB"], env["xmixB"]

    w_gt = A16.alloc((NC_, VD), "w_gt")
    w_ot = A16.alloc((NC_, D), "w_ot")
    gnw = A16.alloc((D,), "gnw16")
    gnb = A16.alloc((D,), "gnb16")
    gsil = [A16.alloc((VD,), f"gsil{j}") for j in range(2)]
    o_n = [A16.alloc((VD,), f"o_n{j}") for j in range(2)]
    ogT = [A16.alloc((NC_, 128), f"ogT{j}") for j in range(2)]
    stats = A32.alloc((NT, HH, 2), "gnstats")
    sq = A32.alloc((VH,), "gnsq")
    st8 = A32.alloc((8,), "gnst8")
    og = [A32.alloc((VD,), f"og{j}") for j in range(2)]
    s_sb = [A32.alloc((D,), f"s_sb{j}") for j in range(2)]

    op("gpsimd", lambda E: E.dma_start(
        out=w_gt.ap(), in_=da["w_gt"].rearrange("(c p) m -> p c m", p=128)),
       writes=[w_gt.buf])
    op("gpsimd", lambda E: E.dma_start(
        out=w_ot.ap(), in_=da["w_ot"].rearrange("(c p) m -> p c m", p=128)),
       writes=[w_ot.buf])
    op("gpsimd", lambda E: E.dma_start(out=gnw.ap(), in_=da["gnw_rep"][:, :]),
       writes=[gnw.buf])
    op("gpsimd", lambda E: E.dma_start(out=gnb.ap(), in_=da["gnb_rep"][:, :]),
       writes=[gnb.buf])
    emit_casts(p, env, range(5, NE))

    for i in range(NT):
        for h in range(HH):
            osl = o_bf[:, i, h * VH:(h + 1) * VH]
            op("scalar", lambda E, osl=osl: E.activation(
                sq.ap(), osl, AF.Identity, accum_out=st8[:, 0:1]),
               reads=[o_bf.buf], writes=[sq.buf, st8.buf])
            op("scalar", lambda E, osl=osl: E.activation(
                sq.ap(), osl, AF.Square, accum_out=st8[:, 1:2]),
               reads=[o_bf.buf], writes=[sq.buf, st8.buf])
            op("vector", lambda E, i=i, h=h: E.tensor_scalar_mul(
                stats[:, i, h, 0:1], st8[:, 0:1], 1.0 / VH),
               reads=[st8.buf], writes=[stats.buf])
            op("vector", lambda E: E.tensor_scalar_mul(st8[:, 2:3], st8[:, 1:2], 1.0 / VH),
               reads=[st8.buf], writes=[st8.buf])
            op("vector", lambda E, i=i, h=h: E.tensor_mul(
                st8[:, 3:4], stats[:, i, h, 0:1], stats[:, i, h, 0:1]),
               reads=[stats.buf], writes=[st8.buf])
            op("vector", lambda E: E.tensor_sub(st8[:, 4:5], st8[:, 2:3], st8[:, 3:4]),
               reads=[st8.buf], writes=[st8.buf])
            op("scalar", lambda E: E.activation(st8[:, 5:6], st8[:, 4:5], AF.Ln, bias=env["epsb"][:, 0:1]),
               reads=[st8.buf, env["epsb"].buf], writes=[st8.buf])
            op("scalar", lambda E, i=i, h=h: E.activation(
                stats[:, i, h, 1:2], st8[:, 5:6], AF.Exp, scale=-0.5),
               reads=[st8.buf], writes=[stats.buf])

    for i in range(NT):
        t0 = i * 128
        pj = i % 2
        gs, on_, ogj, ogTj, ssb = gsil[pj], o_n[pj], og[pj], ogT[pj], s_sb[pj]
        emit_delta(p, env, dbufB, t0, 128)
        emit_xmix(p, env, zTb, 1, 4, dbufB, xmixB, t0, 128,
                  (bankof(env, 2, 1), bankof(env, 2, 0)))
        for vh in range(2):
            bap, bbs = bankof(env, 0, vh)
            for c in range(NC_):
                op("tensor", lambda E, vh=vh, c=c, bap=bap: E.matmul(
                    bap, xmixB[:, c, :], w_gt[:, c, vh * 512:(vh + 1) * 512],
                    start=(c == 0), stop=(c == NC_ - 1)),
                   reads=([xmixB.buf, w_gt.buf] if c in (0, NC_ - 1) else []),
                   writes=bbs if c == NC_ - 1 else [],
                   war=bbs if c == 0 else [], inc=(c == NC_ - 1))
            op("scalar", lambda E, vh=vh, bap=bap, gs=gs: E.activation(
                gs[:, vh * 512:(vh + 1) * 512], bap, AF.Silu),
               reads=bbs, writes=[gs.buf])
        for h in range(HH):
            op("vector", lambda E, i=i, h=h, on_=on_: E.tensor_scalar(
                on_[:, h * VH:(h + 1) * VH], o_bf[:, i, h * VH:(h + 1) * VH],
                stats[:, i, h, 0:1], stats[:, i, h, 1:2], ALU.subtract, ALU.mult),
               reads=[o_bf.buf, stats.buf], writes=[on_.buf])
        op("vector", lambda E, on_=on_: E.tensor_mul(on_.ap(), on_.ap(), gnw.ap()),
           reads=[gnw.buf, on_.buf], writes=[on_.buf])
        op("vector", lambda E, on_=on_: E.tensor_add(on_.ap(), on_.ap(), gnb.ap()),
           reads=[gnb.buf, on_.buf], writes=[on_.buf])
        op("vector", lambda E, on_=on_, gs=gs, ogj=ogj: E.tensor_mul(ogj.ap(), on_.ap(), gs.ap()),
           reads=[gs.buf, on_.buf], writes=[ogj.buf])
        tp_ap, tp_bb = (ps[3][:, 640:768], [psb[7]]) if pj == 0 else (ps[0][:, 640:768], [psb[1]])
        for c in range(NC_):
            op("tensor", lambda E, c=c, tp_ap=tp_ap, ogj=ogj: E.transpose(
                tp_ap, ogj[:, c * 128:(c + 1) * 128], env["ident"].ap()),
               reads=[ogj.buf, env["ident"].buf], writes=tp_bb, war=tp_bb)
            op("vector", lambda E, c=c, tp_ap=tp_ap, ogTj=ogTj: E.tensor_copy(ogTj[:, c, :], tp_ap),
               reads=tp_bb, writes=[ogTj.buf])
        for oh in range(2):
            bap, bbs = bankof(env, 1, oh) if pj == 0 else bankof(env, 2, oh)
            for c in range(NC_):
                op("tensor", lambda E, oh=oh, c=c, bap=bap, ogTj=ogTj: E.matmul(
                    bap, ogTj[:, c, :], w_ot[:, c, oh * 512:(oh + 1) * 512],
                    start=(c == 0), stop=(c == NC_ - 1)),
                   reads=([ogTj.buf, w_ot.buf] if c in (0, NC_ - 1) else []),
                   writes=bbs if c == NC_ - 1 else [],
                   war=bbs if c == 0 else [], inc=(c == NC_ - 1))
            op("vector", lambda E, oh=oh, bap=bap, ssb=ssb: E.tensor_copy(
                ssb[:, oh * 512:(oh + 1) * 512], bap),
               reads=bbs, writes=[ssb.buf])
        op("sync", lambda E, i=i, ssb=ssb: E.dma_start(
            out=env["s_scr"][i * 128:(i + 1) * 128, :], in_=ssb.ap()),
           reads=[ssb.buf], writes=[env["s_scr_buf"]])
        if "s" in env["dbg"]:
            op("sync", lambda E, i=i, ssb=ssb: E.dma_start(
                out=env["dbg"]["s"][i * 128:(i + 1) * 128, :], in_=ssb.ap()),
               reads=[ssb.buf], dsem="b_dbg")
    p.barrier()
    A16.free(w_gt, w_ot, gnw, gnb, *gsil, *o_n, *ogT, o_bf, env["zTb"],
             env["w_x2s"], env["musb"], env["musb2"], dbufB, xmixB)
    A32.free(stats, sq, st8, *og, *s_sb, env["xbias"])


def ph_ffn(p, env):
    op, da, A16, A32 = p.op, env["da"], env["A16"], env["A32"]
    ps, psb, xnT = env["ps"], env["psb"], env["xnT"]

    Hsb = A16.alloc((32, 1024), "Hsb")
    w1b = [A16.alloc((8, NC_, 128), f"w1b{j}") for j in range(2)]
    w2b = [A16.alloc((1, 1024), f"w2b{j}") for j in range(6)]
    wrt = A16.alloc((NC_, 8), "wrt")
    b2e = A16.alloc((D,), "b2e")
    gatesT = A16.alloc((T,), "gatesT")
    b1t = A32.alloc((NE * 32,), "b1t")
    gates = A32.alloc((NT, 8), "gates")
    gf32 = A32.alloc((16,), "gf32")
    gtmp = A32.alloc((16,), "gtmp")
    acc = A32.alloc((8, D), "acc")
    xres = [A32.alloc((D,), f"xres{j}") for j in range(2)]
    sres = [A32.alloc((D,), f"sres{j}") for j in range(2)]
    gfin = A32.alloc((D,), "gfin")

    op("gpsimd", lambda E: E.dma_start(
        out=wrt.ap(), in_=da["w_routert"].rearrange("(c p) m -> p c m", p=128)),
       writes=[wrt.buf])
    op("gpsimd", lambda E: E.dma_start(out=b2e[0:NE, :], in_=da["b2ext"][:, :]),
       writes=[b2e.buf])
    op("sync", lambda E: E.dma_start(
        out=b1t.ap().rearrange("p (a b) -> p a b", a=NE), in_=da["b1t"][:, :, :]),
       writes=[b1t.buf])

    for i in range(NT):
        t0 = i * 128
        bap, bbs = bankof(env, 2, 0)
        for c in range(NC_):
            op("tensor", lambda E, c=c, bap=bap, t0=t0: E.matmul(
                bap[:, 0:8], xnT[:, c, t0:t0 + 128], wrt[:, c, :],
                start=(c == 0), stop=(c == NC_ - 1)),
               reads=([xnT.buf, wrt.buf] if c in (0, NC_ - 1) else []),
               writes=bbs if c == NC_ - 1 else [],
               war=bbs if c == 0 else [], inc=(c == NC_ - 1))
        op("vector", lambda E, bap=bap: E.reduce_max(gtmp[:, 0:1], bap[:, 0:8], axis=AX.X),
           reads=bbs, writes=[gtmp.buf])
        op("vector", lambda E: E.tensor_scalar_mul(gtmp[:, 1:2], gtmp[:, 0:1], -1.0),
           reads=[gtmp.buf], writes=[gtmp.buf])
        op("scalar", lambda E, bap=bap, i=i: E.activation(
            gates[:, i, :], bap[:, 0:8], AF.Exp, bias=gtmp[:, 1:2]),
           reads=bbs + [gtmp.buf], writes=[gates.buf])
        op("vector", lambda E, i=i: E.reduce_sum(gtmp[:, 2:3], gates[:, i, :], axis=AX.X),
           reads=[gates.buf], writes=[gtmp.buf])
        op("vector", lambda E: E.reciprocal(gtmp[:, 3:4], gtmp[:, 2:3]),
           reads=[gtmp.buf], writes=[gtmp.buf])
        op("vector", lambda E, i=i: E.tensor_scalar(
            gates[:, i, :], gates[:, i, :], gtmp[:, 3:4], None, ALU.mult),
           reads=[gtmp.buf, gates.buf], writes=[gates.buf])
        op("vector", lambda E, i=i: E.reduce_max(gtmp[:, 4:5], gates[:, i, :], axis=AX.X),
           reads=[gates.buf], writes=[gtmp.buf])
        op("vector", lambda E, i=i: E.tensor_scalar(
            gtmp[:, 8:16], gates[:, i, :], gtmp[:, 4:5], None, ALU.is_ge),
           reads=[gates.buf, gtmp.buf], writes=[gtmp.buf])
        op("vector", lambda E, i=i: E.scalar_tensor_tensor(
            gtmp[:, 8:16], gtmp[:, 8:16], -1e30, gates[:, i, :], ALU.mult, ALU.add),
           reads=[gtmp.buf, gates.buf], writes=[gtmp.buf])
        op("vector", lambda E: E.reduce_max(gtmp[:, 5:6], gtmp[:, 8:16], axis=AX.X),
           reads=[gtmp.buf], writes=[gtmp.buf])
        op("vector", lambda E, i=i: E.tensor_scalar(
            gtmp[:, 8:16], gates[:, i, :], gtmp[:, 5:6], None, ALU.is_ge),
           reads=[gates.buf, gtmp.buf], writes=[gtmp.buf])
        op("vector", lambda E, i=i: E.tensor_mul(
            gates[:, i, :], gates[:, i, :], gtmp[:, 8:16]),
           reads=[gtmp.buf, gates.buf], writes=[gates.buf])
        op("vector", lambda E, i=i: E.tensor_copy(gf32[:, 0:8], gates[:, i, :]),
           reads=[gates.buf], writes=[gf32.buf])
        op("vector", lambda E: E.memset(gf32[:, 8:16], 1.0), writes=[gf32.buf])
        tpg, tpg_bb = ps[3][:, 0:128], [psb[6]]
        op("tensor", lambda E, tpg=tpg: E.transpose(
            tpg[0:16, :], gf32.ap(), env["ident"].ap()),
           reads=[gf32.buf, env["ident"].buf], writes=tpg_bb, war=tpg_bb)
        op("vector", lambda E, tpg=tpg, t0=t0: E.tensor_copy(
            gatesT[0:9, t0:t0 + 128], tpg[0:9, :]),
           reads=tpg_bb, writes=[gatesT.buf])
    if "gates" in env["dbg"]:
        p.op("sync", lambda E: E.dma_start(
            out=env["dbg"]["gates"][:, :],
            in_=gates.ap().rearrange("p a b -> p (a b)")),
            reads=[gates.buf], dsem="b_dbg")

    pend = []

    def mk_combine(th):
        its = []
        for tb in range(8):
            i = th * 8 + tb

            def it(tb=tb, i=i):
                xr, sr = xres[tb % 2], sres[tb % 2]
                op("scalar", lambda E, i=i, xr=xr: E.dma_start(
                    out=xr.ap(), in_=da["x"][i * 128:(i + 1) * 128, :]),
                   writes=[xr.buf], dsem="b_" + xr.buf.name)
                op("scalar", lambda E, i=i, sr=sr: E.dma_start(
                    out=sr.ap(), in_=env["s_scr"][i * 128:(i + 1) * 128, :]),
                   reads=[env["s_scr_buf"]], writes=[sr.buf],
                   dsem="b_" + sr.buf.name)
                op("scalar", lambda E, tb=tb: E.activation(
                    gfin.ap(), acc[:, tb, :], AF.Tanh, scale=SCALE / 2),
                   reads=[acc.buf], writes=[gfin.buf])
                op("vector", lambda E: E.tensor_scalar(
                    gfin.ap(), gfin.ap(), 0.5, 0.5, ALU.mult, ALU.add),
                   reads=[gfin.buf], writes=[gfin.buf])
                op("vector", lambda E, tb=tb: E.scalar_tensor_tensor(
                    gfin.ap(), acc[:, tb, :], SCALE, gfin.ap(), ALU.mult,
                    ALU.mult),
                   reads=[acc.buf, gfin.buf], writes=[gfin.buf])
                op("vector", lambda E, sr=sr: E.tensor_mul(
                    sr.ap(), sr.ap(), gfin.ap()),
                   reads=[sr.buf, gfin.buf], writes=[sr.buf])
                op("vector", lambda E, xr=xr, sr=sr: E.tensor_add(
                    sr.ap(), sr.ap(), xr.ap()),
                   reads=[xr.buf, sr.buf], writes=[sr.buf])
                op("scalar", lambda E, i=i, sr=sr: E.dma_start(
                    out=env["out"][i * 128:(i + 1) * 128, :], in_=sr.ap()),
                   reads=[sr.buf], dsem="b_out")
            its.append(it)
        return its

    for th in range(2):
        t0 = th * 1024
        cmb = mk_combine(th)
        for e in range(NE):
            idx = th * NE + e
            for q in range(4):
                wb = w1b[q % 2]
                if not (q < 2 and idx > 0):
                    qeng = "sync" if q % 2 == 0 else "gpsimd"
                    p.raw_wait(qeng, f"cast{e}", 32)
                    op(qeng, lambda E, e=e, q=q, wb=wb: E.dma_start(
                        out=wb.ap(), in_=env["w1s"][e, q * 8:(q + 1) * 8].rearrange(
                            "f (c p) m -> p f c m", p=128)),
                       writes=[wb.buf])
                for fi in range(8):
                    ffb = q * 8 + fi
                    banks = [bankof(env, (ffb % 2), tc) for tc in range(2)]
                    for c in range(NC_):
                        for tc in range(2):
                            bap, bbs = banks[tc]
                            op("tensor", lambda E, wb=wb, fi=fi, c=c, bap=bap,
                               tt=t0 + tc * 512: E.matmul(
                                bap, wb[:, fi, c, :], xnT[:, c, tt:tt + 512],
                                start=(c == 0), stop=(c == NC_ - 1)),
                               reads=([wb.buf, xnT.buf]
                                      if (c == 0 and tc == 0) or (c == NC_ - 1 and tc == 1)
                                      else []),
                               writes=bbs if c == NC_ - 1 else [],
                               war=bbs if c == 0 else [],
                               inc=(c == NC_ - 1))
                    for tc in range(2):
                        bap, bbs = banks[tc]
                        op("scalar", lambda E, e=e, ffb=ffb, tc=tc, bap=bap: E.activation(
                            Hsb[:, ffb, tc * 512:(tc + 1) * 512], bap,
                            AF.Gelu_apprx_tanh,
                            bias=b1t[:, e * 32 + ffb:e * 32 + ffb + 1]),
                           reads=bbs + [b1t.buf], writes=[Hsb.buf])
                    if e == 0 and pend and ffb in (8, 12, 16, 20):
                        pend.pop(0)()
            if e == 0:
                for tb in range(8):
                    tt = t0 + tb * 128
                    for dh in range(2):
                        bap, bbs = bankof(env, 2, dh)
                        op("tensor", lambda E, bap=bap, tt=tt, dh=dh: E.matmul(
                            bap, gatesT[0:9, tt:tt + 128],
                            b2e[0:9, dh * 512:(dh + 1) * 512],
                            start=True, stop=True),
                           reads=[gatesT.buf, b2e.buf], writes=bbs, war=bbs)
                        op("vector", lambda E, tb=tb, dh=dh, bap=bap: E.tensor_copy(
                            acc[:, tb, dh * 512:(dh + 1) * 512], bap),
                           reads=bbs, writes=[acc.buf])
            for qq in range(2):
                for ffc in range(32):
                    pc = qq * 32 + ffc
                    wb2 = w2b[pc % 6]
                    qeng = "sync" if pc % 2 == 0 else "gpsimd"
                    p.raw_wait(qeng, f"cast{e}", 32)
                    op(qeng, lambda E, e=e, ffc=ffc, wb2=wb2: E.dma_start(
                        out=wb2.ap(), in_=env["w2s"][e, ffc * 128:(ffc + 1) * 128, :]
                        .rearrange("(f p) d -> p f d", p=128)),
                       writes=[wb2.buf])
                    for tb in range(4):
                        tbg = qq * 4 + tb
                        for dh in range(2):
                            first = (tb == 0 and dh == 0)
                            last = (tb == 3 and dh == 1)
                            op("tensor", lambda E, wb2=wb2, tb=tb, dh=dh,
                               tbg=tbg, ffc=ffc: E.matmul(
                                ps[tb][:, dh * 512:(dh + 1) * 512],
                                Hsb[:, ffc, tbg * 128:(tbg + 1) * 128],
                                wb2[:, 0, dh * 512:(dh + 1) * 512],
                                start=(ffc == 0), stop=(ffc == 31)),
                               reads=([wb2.buf, Hsb.buf] if (first or last) else []),
                               writes=[psb[2 * tb + dh]] if ffc == 31 else [],
                               war=[psb[2 * tb + dh]] if ffc == 0 else [],
                               inc=(ffc == 31 or last))
                    if e == NE - 1 and qq == 1 and pend and ffc in (4, 12, 20, 28):
                        pend.pop(0)()
                if qq == 0 and idx + 1 < 2 * NE:
                    th2, e2 = divmod(idx + 1, NE)
                    for q2, peng in ((0, "scalar"), (1, "scalar")):
                        wb = w1b[q2]
                        p.raw_wait(peng, f"cast{e2}", 32)
                        op(peng, lambda E, e2=e2, q2=q2, wb=wb: E.dma_start(
                            out=wb.ap(), in_=env["w1s"][e2, q2 * 8:(q2 + 1) * 8]
                            .rearrange("f (c p) m -> p f c m", p=128)),
                           writes=[wb.buf], dsem="b_" + wb.buf.name)
                for tb in range(4):
                    tbg = qq * 4 + tb
                    ti = th * 8 + tbg
                    for dh in range(2):
                        sl = slice(dh * 512, (dh + 1) * 512)
                        if e < 8:
                            op("vector", lambda E, tb=tb, tbg=tbg, ti=ti, e=e, sl=sl: E.scalar_tensor_tensor(
                                acc[:, tbg, sl], ps[tb][:, sl], gates[:, ti, e:e + 1],
                                acc[:, tbg, sl], ALU.mult, ALU.add),
                               reads=[psb[2 * tb + dh], gates.buf, acc.buf],
                               writes=[acc.buf])
                        else:
                            op("vector", lambda E, tb=tb, tbg=tbg, sl=sl: E.tensor_add(
                                acc[:, tbg, sl], acc[:, tbg, sl], ps[tb][:, sl]),
                               reads=[psb[2 * tb + dh], acc.buf],
                               writes=[acc.buf])
                if e == NE - 1:
                    pend.extend(cmb[0:4] if qq == 0 else cmb[4:8])
        accname = f"acc{th}"
        if accname in env["dbg"]:
            p.op("sync", lambda E, accname=accname: E.dma_start(
                out=env["dbg"][accname][:, :],
                in_=acc.ap().rearrange("p a b -> p (a b)")),
                reads=[acc.buf], dsem="b_dbg")
    while pend:
        pend.pop(0)()
    p.raw_wait("sync", "b_out", p.cnt.get("b_out", 0))


def kernel(**inputs):
    nc = bass.Bass("TRN2", target_bir_lowering=False)
    build(nc)
    g = host_prep(inputs)
    x = np.ascontiguousarray(inputs["L0_attn_hidden_states"], dtype=np.float32)
    in_maps = []
    for c in range(8):
        m = dict(g)
        m["x"] = np.ascontiguousarray(x[c])
        in_maps.append(m)
    res = run_bass_kernel_spmd(nc, in_maps, core_ids=list(range(8)))
    return np.stack([res.results[c]["out"] for c in range(8)], axis=0)




# version 15
# speedup vs baseline: 1.5045x; 1.1525x over previous
import math
from contextlib import ExitStack

import numpy as np

import concourse.bass as bass
import concourse.mybir as mybir
from concourse.bass_utils import run_bass_kernel_spmd

F32 = mybir.dt.float32
BF16 = mybir.dt.bfloat16
AF = mybir.ActivationFunctionType
ALU = mybir.AluOpType
AX = mybir.AxisListType

T = 2048
D = 1024
HH = 4
KD = 512
VD = 1024
VH = 256
C = 128
CSUB = 32
NT = T // 128
NC_ = D // 128
NCH = T // C
DFF = 4096
NE = 9
EPS = 1e-5
SCALE = 1.0 / math.sqrt(1.25)

A16_N = 75776
A32_N = 14848

ENGS = ("tensor", "vector", "scalar", "sync", "gpsimd")
DMA_ENGS = ("sync", "gpsimd")


class Buf:
    __slots__ = ("name", "last_w", "readers")

    def __init__(self, name):
        self.name = name
        self.last_w = None
        self.readers = []


class View:
    __slots__ = ("arena", "off", "dims", "buf", "nelem")

    def __init__(self, arena, off, dims, name):
        self.arena = arena
        self.off = off
        self.dims = tuple(dims)
        self.nelem = int(np.prod(dims)) if dims else 1
        self.buf = Buf(name)

    def ap(self):
        a = self.arena.h[:, self.off:self.off + self.nelem]
        if len(self.dims) > 1:
            pat = "p (" + " ".join(f"d{i}" for i in range(len(self.dims))) + ") -> p " + \
                  " ".join(f"d{i}" for i in range(len(self.dims)))
            a = a.rearrange(pat, **{f"d{i}": d for i, d in enumerate(self.dims)})
        return a

    def __getitem__(self, idx):
        return self.ap()[idx]


class Arena:

    def __init__(self, handle, n):
        self.h = handle
        self.free_list = [(0, n)]

    def alloc(self, dims, name):
        ne = int(np.prod(dims))
        for i, (off, sz) in enumerate(self.free_list):
            aoff = (off + 15) // 16 * 16
            pad = aoff - off
            if sz >= ne + pad:
                rest = sz - ne - pad
                repl = []
                if pad:
                    repl.append((off, pad))
                if rest:
                    repl.append((aoff + ne, rest))
                self.free_list[i:i + 1] = repl
                return View(self, aoff, dims, name)
        raise RuntimeError(f"arena OOM for {name} ({ne})")

    def free(self, *views):
        for v in views:
            self.free_list.append((v.off, v.nelem))
        self.free_list.sort()
        merged = []
        for off, sz in self.free_list:
            if merged and merged[-1][0] + merged[-1][1] >= off:
                po, ps_ = merged[-1]
                merged[-1] = (po, max(ps_, off + sz - po))
            else:
                merged.append((off, sz))
        self.free_list = merged


class P:

    def __init__(self, nc):
        self.nc = nc
        self.stack = None
        self.prog = {e: [] for e in ENGS}
        self.cnt = {e: 0 for e in ENGS}
        self.waited = {e: {} for e in ENGS}
        self.sems = {}

    def getsem(self, name):
        if name not in self.sems:
            self.sems[name] = self.stack.enter_context(self.nc.semaphore(name))
            self.cnt[name] = 0
        return name

    def op(self, eng, fn, reads=(), writes=(), war=(), inc=True, dsem=None):
        waits = {}
        self_sync = eng != "tensor"

        def add(sem, cntv):
            if (sem != eng or self_sync) and cntv > waits.get(sem, -1):
                waits[sem] = cntv

        for b in reads:
            if b.last_w:
                add(*b.last_w)
        for b in tuple(writes) + tuple(war):
            if b.last_w:
                add(*b.last_w)
            for r in b.readers:
                add(*r)
        wl = []
        wd = self.waited[eng]
        for sem, v in waits.items():
            if wd.get(sem, -1) < v:
                wd[sem] = v
                wl.append((sem, v))
        if eng in DMA_ENGS or dsem is not None:
            if dsem is None:
                assert writes, "DMA op needs writes or dsem"
                dsem = "b_" + writes[0].name
            csem = self.getsem(dsem)
            amt = 16
        else:
            csem = eng
            amt = 1
        if inc:
            self.cnt[csem] += amt
            mark = (csem, self.cnt[csem])
            for b in writes:
                b.last_w = mark
                b.readers = []
            for b in reads:
                b.readers.append(mark)
        else:
            assert not writes
        sems = self.sems

        def run(E, wl=wl, fn=fn, inc=inc, csem=csem, amt=amt):
            for sem, v in wl:
                E.wait_ge(sems[sem], v)
            ins = fn(E)
            if inc:
                ins.then_inc(sems[csem], amt)
        self.prog[eng].append(run)

    def raw_wait(self, eng, sem, val):
        if val <= 0:
            return
        wd = self.waited[eng]
        if sem != eng and wd.get(sem, -1) >= val:
            return
        if sem != eng:
            wd[sem] = val
        sems = self.sems
        self.prog[eng].append(lambda E: E.wait_ge(sems[sem], val))

    def barrier(self):
        snap = dict(self.cnt)
        for e in ENGS:
            for o, v in snap.items():
                if o != e and o not in DMA_ENGS:
                    self.raw_wait(e, o, v)


def host_prep(inp):
    g = {}
    g["w_x1t"] = inp["L0_attn_W_x1"].T
    g["w_x2s"] = inp["L0_attn_W_x2"].reshape(D, 5, 32).transpose(2, 1, 0)
    g["xbias_t"] = inp["L0_attn_x_bias"].reshape(5, NC_, 128).transpose(2, 0, 1)
    g["mu_x_t"] = inp["L0_attn_mu_x"].reshape(NC_, 128).T
    g["w_rt"] = inp["L0_attn_W_r"].T
    g["w_kt"] = inp["L0_attn_W_k"].T
    g["a_wt"] = inp["L0_attn_A_w"].T
    g["b_wt"] = inp["L0_attn_B_w"].T
    g["bw_t"] = inp["L0_attn_b_w"].reshape(4, 128).T
    g["w_vt"] = inp["L0_attn_W_v"].T
    g["w_gt"] = inp["L0_attn_W_g"].T
    g["w_ot"] = inp["L0_attn_W_o"].T
    g["u_t"] = inp["L0_attn_bonus"].T
    g["gnw_rep"] = np.broadcast_to(inp["L0_attn_gn_weight"], (128, D))
    g["gnb_rep"] = np.broadcast_to(inp["L0_attn_gn_bias"], (128, D))
    pm = np.zeros((128, 4, CSUB), np.float32)
    for i in range(4):
        s = np.arange(128)[:, None]
        t = i * CSUB + np.arange(CSUB)[None, :]
        pm[:, i, :] = (s < t) & (s < (i + 1) * CSUB)
    g["pmask"] = pm
    g["ident"] = np.eye(128, dtype=np.float32)
    g["w_routert"] = inp["L0_ffn_w_router"].T
    g["b2ext"] = np.concatenate([inp["L0_ffn_b2"], inp["L0_ffn_bs2"][None]], 0)
    w1 = np.concatenate([inp["L0_ffn_W1"], inp["L0_ffn_Ws1"][None]], 0)
    w2 = np.concatenate([inp["L0_ffn_W2"], inp["L0_ffn_Ws2"][None]], 0)
    b1 = np.concatenate([inp["L0_ffn_b1"], inp["L0_ffn_bs1"][None]], 0)
    g["w1p"] = w1.transpose(0, 2, 1).reshape(NE, D, 32, 128).transpose(0, 2, 1, 3)
    g["w2p"] = w2.transpose(0, 2, 1)
    g["b1t"] = b1.reshape(NE, 32, 128).transpose(2, 0, 1)
    return {k: np.ascontiguousarray(v, dtype=np.float32) for k, v in g.items()}


DRAM_SPECS = [
    ("w_x1t", (D, 160)), ("w_x2s", (32, 5, D)), ("xbias_t", (128, 5, NC_)),
    ("mu_x_t", (128, NC_)), ("w_rt", (D, KD)), ("w_kt", (D, KD)),
    ("a_wt", (D, 64)), ("b_wt", (64, KD)), ("bw_t", (128, 4)),
    ("w_vt", (D, VD)), ("w_gt", (D, VD)), ("w_ot", (D, D)),
    ("u_t", (128, 4)), ("gnw_rep", (128, D)), ("gnb_rep", (128, D)),
    ("pmask", (128, 4, CSUB)), ("ident", (128, 128)),
    ("w_routert", (D, 8)), ("b2ext", (NE, D)),
    ("w1p", (NE, 32, D, 128)), ("w2p", (NE, DFF, D)), ("b1t", (128, NE, 32)),
]


def build(nc, dbg=()):
    p = P(nc)
    da = {}
    da["x"] = nc.dram_tensor("x", [T, D], F32, kind="ExternalInput").ap()
    for name, shape in DRAM_SPECS:
        da[name] = nc.dram_tensor(name, list(shape), F32, kind="ExternalInput").ap()
    out_ap = nc.dram_tensor("out", [T, D], F32, kind="ExternalOutput").ap()
    s_scr = nc.dram_tensor("s_scr", [T, D], F32, kind="Internal").ap()
    w1s = nc.dram_tensor("w1s", [NE, 32, D, 128], BF16, kind="Internal").ap()
    w2s = nc.dram_tensor("w2s", [NE, DFF, D], BF16, kind="Internal").ap()
    dbg_out = {}
    for name, shape in dbg:
        dbg_out[name] = nc.dram_tensor("dbg_" + name, list(shape), F32,
                                       kind="ExternalOutput").ap()

    with ExitStack() as st:
        a16h = st.enter_context(nc.sbuf_tensor("arena16", [128, A16_N], BF16))
        a32h = st.enter_context(nc.sbuf_tensor("arena32", [128, A32_N], F32))
        A16 = Arena(a16h, A16_N)
        A32 = Arena(a32h, A32_N)
        ps = [st.enter_context(nc.psum_tensor(f"ps{i}", [128, 1024], F32))
              for i in range(4)]
        psb = [Buf(f"ps{i}h{j}") for i in range(4) for j in range(2)]
        p.stack = st
        for e in ENGS:
            p.sems[e] = st.enter_context(nc.semaphore("sem_" + e))
        for e in range(NE):
            p.sems[f"cast{e}"] = st.enter_context(nc.semaphore(f"sem_cast{e}"))
            p.cnt[f"cast{e}"] = 0

        env = dict(da=da, out=out_ap, s_scr=s_scr, s_scr_buf=Buf("s_scr"),
                   w1s=w1s, w2s=w2s,
                   dbg=dbg_out, ps=ps, psb=psb, A16=A16, A32=A32)
        ph_prologue(p, env)
        ph_A(p, env)
        ph_B(p, env)
        ph_B2(p, env)
        ph_ffn(p, env)

        with nc.Block() as blk:
            for e in ENGS:
                def runner(E, e=e):
                    for fn in p.prog[e]:
                        fn(E)
                getattr(blk, e)(runner)
    return nc


def bankof(env, i, half):
    if half is None:
        return env["ps"][i][:, :], [env["psb"][2 * i], env["psb"][2 * i + 1]]
    return (env["ps"][i][:, half * 512:(half + 1) * 512],
            [env["psb"][2 * i + half]])


def dump16(p, env, view, name):
    if name not in env["dbg"]:
        return
    p.op("gpsimd", lambda E: E.dma_start(
        out=env["dbg"][name][:, :], in_=view.ap().rearrange(
            "p " + " ".join(f"d{i}" for i in range(len(view.dims))) +
            " -> p (" + " ".join(f"d{i}" for i in range(len(view.dims))) + ")")
        if len(view.dims) > 1 else view.ap()),
        reads=[view.buf], dsem="b_dbg")


def dump32(p, env, view, name):
    if name not in env["dbg"]:
        return
    p.op("sync", lambda E: E.dma_start(
        out=env["dbg"][name][:, :], in_=view.ap().rearrange(
            "p " + " ".join(f"d{i}" for i in range(len(view.dims))) +
            " -> p (" + " ".join(f"d{i}" for i in range(len(view.dims))) + ")")
        if len(view.dims) > 1 else view.ap()),
        reads=[view.buf], dsem="b_dbg")


def ph_prologue(p, env):
    op, da, A16, A32, ps, psb = p.op, env["da"], env["A16"], env["A32"], env["ps"], env["psb"]

    xnT = A16.alloc((NC_, T), "xnT")
    onecol = A16.alloc((8,), "onecol")
    w_x2s = A16.alloc((5, D), "w_x2s")
    zTb = A16.alloc((2, T), "zTb")
    musb = A16.alloc((512,), "musb")
    musb2 = A16.alloc((512,), "musb2")
    ident = A32.alloc((128,), "ident")
    masks = A32.alloc((4, CSUB), "pmask")
    small32 = A32.alloc((64,), "small32")
    xbias = A32.alloc((5 * NC_,), "xbias")
    ones32 = A32.alloc((128,), "ones32")
    epsb = A32.alloc((2,), "epsb")
    env.update(xnT=xnT, onecol=onecol, w_x2s=w_x2s, zTb=zTb, musb=musb,
               musb2=musb2, ident=ident, masks=masks, small32=small32,
               xbias=xbias, ones32=ones32, epsb=epsb)
    env["MU_X"] = lambda c: small32[:, c:c + 1]
    env["BW"] = lambda kb: small32[:, 8 + kb:9 + kb]
    env["UU"] = lambda h: small32[:, 12 + h:13 + h]

    op("sync", lambda E: E.dma_start(out=ident.ap(), in_=da["ident"][:, :]),
       writes=[ident.buf])
    op("sync", lambda E: E.dma_start(out=masks.ap().rearrange("p a b -> p (a b)"),
                                     in_=da["pmask"][:, :, :].rearrange("p a b -> p (a b)")),
       writes=[masks.buf])
    op("sync", lambda E: E.dma_start(out=small32[:, 0:8], in_=da["mu_x_t"][:, :]),
       writes=[small32.buf])
    op("sync", lambda E: E.dma_start(out=small32[:, 8:12], in_=da["bw_t"][:, :]),
       writes=[small32.buf])
    op("sync", lambda E: E.dma_start(out=small32[:, 12:16], in_=da["u_t"][:, :]),
       writes=[small32.buf])
    op("sync", lambda E: E.dma_start(out=xbias.ap().rearrange("p (a b) -> p a b", a=5),
                                     in_=da["xbias_t"][:, :, :]),
       writes=[xbias.buf])
    op("vector", lambda E: E.memset(ones32.ap(), 1.0), writes=[ones32.buf])
    op("vector", lambda E: E.memset(epsb.ap(), EPS), writes=[epsb.buf])
    op("vector", lambda E: E.memset(onecol.ap(), 1.0), writes=[onecol.buf])
    op("gpsimd", lambda E: E.dma_start(
        out=w_x2s[0:32, :, :], in_=da["w_x2s"][:, :, :]), writes=[w_x2s.buf])

    xtile = [A32.alloc((D,), f"xtile{j}") for j in range(2)]
    xcn = A32.alloc((D,), "xcn")
    stat = A32.alloc((8,), "lnstat")
    sqv = A32.alloc((D,), "lnsq")
    for i in range(NT):
        xt = xtile[i % 2]
        op("sync", lambda E, i=i, xt=xt: E.dma_start(
            out=xt.ap(), in_=da["x"][i * 128:(i + 1) * 128, :]), writes=[xt.buf])
        op("vector", lambda E, xt=xt: E.reduce_sum(stat[:, 0:1], xt.ap(), axis=AX.X),
           reads=[xt.buf], writes=[stat.buf])
        op("vector", lambda E: E.tensor_scalar_mul(stat[:, 1:2], stat[:, 0:1], 1.0 / D),
           reads=[stat.buf], writes=[stat.buf])
        op("vector", lambda E, xt=xt: E.tensor_scalar(
            xcn.ap(), xt.ap(), stat[:, 1:2], None, ALU.subtract),
           reads=[xt.buf, stat.buf], writes=[xcn.buf])
        op("scalar", lambda E: E.activation(sqv.ap(), xcn.ap(), AF.Square),
           reads=[xcn.buf], writes=[sqv.buf])
        op("vector", lambda E: E.reduce_sum(stat[:, 2:3], sqv.ap(), axis=AX.X),
           reads=[sqv.buf], writes=[stat.buf])
        op("vector", lambda E: E.tensor_scalar_mul(stat[:, 3:4], stat[:, 2:3], 1.0 / D),
           reads=[stat.buf], writes=[stat.buf])
        op("scalar", lambda E: E.activation(stat[:, 4:5], stat[:, 3:4], AF.Ln, bias=epsb[:, 0:1]),
           reads=[stat.buf, epsb.buf], writes=[stat.buf])
        op("scalar", lambda E: E.activation(stat[:, 5:6], stat[:, 4:5], AF.Exp, scale=-0.5),
           reads=[stat.buf], writes=[stat.buf])
        op("vector", lambda E: E.tensor_scalar(xcn.ap(), xcn.ap(), stat[:, 5:6], None, ALU.mult),
           reads=[stat.buf, xcn.buf], writes=[xcn.buf])
        tp = ps[3][:, 0:128]
        for c in range(NC_):
            op("tensor", lambda E, c=c, tp=tp: E.transpose(
                tp, xcn[:, c * 128:(c + 1) * 128], ident.ap()),
               reads=[xcn.buf, ident.buf], writes=[psb[6]], war=[psb[6]])
            op("vector", lambda E, c=c, i=i, tp=tp: E.tensor_copy(
                xnT[:, c, i * 128:(i + 1) * 128], tp),
               reads=[psb[6]], writes=[xnT.buf])
    p.barrier()
    A32.free(xtile[0], xtile[1], xcn, stat, sqv)
    dump16(p, env, xnT, "xnT")


def emit_casts(p, env, experts):
    sems = p.sems
    for e in experts:
        def c1(E, e=e):
            E.dma_start(out=env["w1s"][e], in_=env["da"]["w1p"][e]).then_inc(
                sems[f"cast{e}"], 16)
        def c2(E, e=e):
            E.dma_start(out=env["w2s"][e], in_=env["da"]["w2p"][e]).then_inc(
                sems[f"cast{e}"], 16)
        p.prog["gpsimd"].append(c1)
        p.prog["gpsimd"].append(c2)


def emit_delta(p, env, dbuf, t0, width):
    op, xnT = p.op, env["xnT"]
    for c in range(NC_):
        if t0 == 0:
            op("vector", lambda E, c=c: E.tensor_scalar_mul(
                dbuf[:, c, 0:1], xnT[:, c, 0:1], -1.0),
               reads=[xnT.buf], writes=[dbuf.buf])
            op("vector", lambda E, c=c, w=width: E.tensor_sub(
                dbuf[:, c, 1:w], xnT[:, c, 0:w - 1], xnT[:, c, 1:w]),
               reads=[xnT.buf], writes=[dbuf.buf])
        else:
            op("vector", lambda E, c=c, t0=t0, w=width: E.tensor_sub(
                dbuf[:, c, 0:w], xnT[:, c, t0 - 1:t0 + w - 1], xnT[:, c, t0:t0 + w]),
               reads=[xnT.buf], writes=[dbuf.buf])


def emit_xmix(p, env, zview, zrow, slot, dbuf, xmix, t0, width, pbanks):
    op, xnT, xbias = p.op, env["xnT"], env["xbias"]
    musb2 = (env["musb"], env["musb2"])
    for c in range(NC_):
        bap, bbs = pbanks[c % 2]
        mb = musb2[c % 2]
        op("tensor", lambda E, c=c, bap=bap, zrow=zrow, slot=slot, t0=t0, w=width: E.matmul(
            bap[:, 0:w], env["w_x2s"][0:32, slot, c * 128:(c + 1) * 128],
            zview[0:32, zrow, t0:t0 + w], start=True, stop=True),
           reads=[zview.buf, env["w_x2s"].buf], writes=bbs, war=bbs)
        op("vector", lambda E, c=c, bap=bap, slot=slot, w=width, mb=mb: E.scalar_tensor_tensor(
            mb[:, 0:w], bap[:, 0:w], xbias[:, slot * NC_ + c:slot * NC_ + c + 1],
            dbuf[:, c, 0:w], ALU.add, ALU.mult),
           reads=bbs + [xbias.buf, dbuf.buf], writes=[mb.buf])
        op("vector", lambda E, c=c, t0=t0, w=width, mb=mb: E.tensor_add(
            xmix[:, c, 0:w], mb[:, 0:w], xnT[:, c, t0:t0 + w]),
           reads=[mb.buf, xnT.buf], writes=[xmix.buf])


def emit_proj(p, env, xmix, wview, width, nkb, pbank_fn, evac_fn):
    op = p.op
    for kb in range(nkb):
        bap, bbs = pbank_fn(kb)
        for c in range(NC_):
            op("tensor", lambda E, kb=kb, c=c, bap=bap, w=width: E.matmul(
                bap, wview[:, c, kb * 128:(kb + 1) * 128], xmix[:, c, 0:w],
                start=(c == 0), stop=(c == NC_ - 1)),
               reads=([wview.buf, xmix.buf] if c in (0, NC_ - 1) else []),
               writes=bbs if c == NC_ - 1 else [],
               war=bbs if c == 0 else [],
               inc=(c == NC_ - 1))
        evac_fn(kb, bap, bbs)


def ph_A(p, env):
    op, da, A16, A32 = p.op, env["da"], env["A16"], env["A32"]
    ps, psb, xnT = env["ps"], env["psb"], env["xnT"]

    rT = A16.alloc((4, T), "rT")
    kT = A16.alloc((4, T), "kT")
    zTa = A16.alloc((3, T), "zTa")
    dbufA = A16.alloc((NC_, 512), "dbufA")
    xmixA = A16.alloc((NC_, 512), "xmixA")
    t1w = A16.alloc((512,), "t1w")
    w_x1t = A16.alloc((NC_, 160), "w_x1t")
    w_rt = A16.alloc((NC_, KD), "w_rt")
    w_kt = A16.alloc((NC_, KD), "w_kt")
    a_wt = A16.alloc((NC_, 64), "a_wt")
    b_wt = A16.alloc((KD,), "b_wt")
    ew = A32.alloc((4, T), "ew")
    env.update(rT=rT, kT=kT, ew=ew)
    zTb, musb, xbias = env["zTb"], env["musb"], env["xbias"]

    for name, view in [("w_x1t", w_x1t), ("w_rt", w_rt), ("w_kt", w_kt),
                       ("a_wt", a_wt)]:
        op("gpsimd", lambda E, name=name, view=view: E.dma_start(
            out=view.ap(), in_=da[name].rearrange("(c p) m -> p c m", p=128)),
           writes=[view.buf])
    op("gpsimd", lambda E: E.dma_start(out=b_wt[0:64, :], in_=da["b_wt"][:, :]),
       writes=[b_wt.buf])

    for ts in range(4):
        t0 = ts * 512
        emit_delta(p, env, dbufA, t0, 512)
        for c in range(NC_):
            op("vector", lambda E, c=c, t0=t0: E.scalar_tensor_tensor(
                xmixA[:, c, :], dbufA[:, c, :], env["MU_X"](c),
                xnT[:, c, t0:t0 + 512], ALU.mult, ALU.add),
               reads=[dbufA.buf, xnT.buf], writes=[xmixA.buf])
        zbank = [bankof(env, 0, 0), bankof(env, 0, 1), bankof(env, 1, 0),
                 bankof(env, 1, 1), bankof(env, 2, 0)]
        for n in range(5):
            bap, bbs = zbank[n]
            for c in range(NC_):
                op("tensor", lambda E, n=n, c=c, bap=bap: E.matmul(
                    bap[0:32, :], w_x1t[:, c, n * 32:(n + 1) * 32], xmixA[:, c, :],
                    start=(c == 0), stop=(c == NC_ - 1)),
                   reads=([w_x1t.buf, xmixA.buf] if c in (0, NC_ - 1) else []),
                   writes=bbs if c == NC_ - 1 else [],
                   war=bbs if c == 0 else [], inc=(c == NC_ - 1))
        for n in range(5):
            bap, bbs = zbank[n]
            tgt = (zTa, n) if n < 3 else (zTb, n - 3)
            op("scalar", lambda E, bap=bap, tgt=tgt, t0=t0: E.activation(
                tgt[0][0:32, tgt[1], t0:t0 + 512], bap[0:32, :], AF.Tanh),
               reads=bbs, writes=[tgt[0].buf])

        emit_xmix(p, env, zTa, 0, 0, dbufA, xmixA, t0, 512,
                  (bankof(env, 2, 1), bankof(env, 2, 0)))
        def evac_r(kb, bap, bbs, t0=t0):
            op("vector", lambda E, kb=kb, bap=bap: E.tensor_copy(
                rT[:, kb, t0:t0 + 512], bap),
               reads=bbs, writes=[rT.buf])
        emit_proj(p, env, xmixA, w_rt, 512, 4, lambda kb: bankof(env, 3, kb % 2), evac_r)

        emit_xmix(p, env, zTa, 1, 1, dbufA, xmixA, t0, 512,
                  (bankof(env, 2, 1), bankof(env, 2, 0)))
        bap, bbs = bankof(env, 0, 0)
        for c in range(NC_):
            op("tensor", lambda E, c=c, bap=bap: E.matmul(
                bap[0:64, :], a_wt[:, c, :], xmixA[:, c, :],
                start=(c == 0), stop=(c == NC_ - 1)),
               reads=([a_wt.buf, xmixA.buf] if c in (0, NC_ - 1) else []),
               writes=bbs if c == NC_ - 1 else [],
               war=bbs if c == 0 else [], inc=(c == NC_ - 1))
        op("scalar", lambda E, bap=bap: E.activation(t1w[0:64, :], bap[0:64, :], AF.Tanh),
           reads=bbs, writes=[t1w.buf])
        for kb in range(4):
            bap2, bbs2 = bankof(env, 1, kb % 2)
            op("tensor", lambda E, kb=kb, bap2=bap2: E.matmul(
                bap2, b_wt[0:64, kb * 128:(kb + 1) * 128], t1w[0:64, :],
                start=True, stop=True),
               reads=[t1w.buf, b_wt.buf], writes=bbs2, war=bbs2)
            op("scalar", lambda E, kb=kb, bap2=bap2, t0=t0: E.activation(
                ew[:, kb, t0:t0 + 512], bap2, AF.Exp, bias=env["BW"](kb)),
               reads=bbs2, writes=[ew.buf])

        emit_xmix(p, env, zTa, 2, 2, dbufA, xmixA, t0, 512,
                  (bankof(env, 2, 1), bankof(env, 2, 0)))
        def evac_k(kb, bap, bbs, t0=t0):
            op("vector", lambda E, kb=kb, bap=bap: E.tensor_copy(
                kT[:, kb, t0:t0 + 512], bap),
               reads=bbs, writes=[kT.buf])
        emit_proj(p, env, xmixA, w_kt, 512, 4, lambda kb: bankof(env, 3, kb % 2), evac_k)

    p.barrier()
    A16.free(zTa, dbufA, xmixA, t1w, w_x1t, w_rt, w_kt, a_wt, b_wt)
    dump16(p, env, rT, "rT")
    dump16(p, env, kT, "kT")
    dump32(p, env, ew, "ew")
    dump16(p, env, zTb, "zTb")


def ph_B(p, env):
    op, da, A16, A32 = p.op, env["da"], env["A16"], env["A32"]
    ps, psb, xnT = env["ps"], env["psb"], env["xnT"]
    rT, kT, ew, zTb, masks, ident = env["rT"], env["kT"], env["ew"], env["zTb"], env["masks"], env["ident"]

    w_vt = A16.alloc((NC_, VD), "w_vt")
    o_bf = A16.alloc((NT, VD), "o_bf")
    Vc = A16.alloc((VD,), "Vc")
    S_bf = [A16.alloc((VH,), f"S_bf{h}") for h in range(HH)]
    dbufB = A16.alloc((NC_, 128), "dbufB")
    xmixB = A16.alloc((NC_, 128), "xmixB")
    eqv = [A16.alloc((C,), f"eqv{j}") for j in range(2)]
    qbf = [A16.alloc((C,), f"qbf{j}") for j in range(2)]
    kbf = [A16.alloc((C,), f"kbf{j}") for j in range(2)]
    Pfull = [A16.alloc((C,), f"Pfull{j}") for j in range(2)]
    khT = [A16.alloc((C,), f"khT{j}") for j in range(2)]
    S_fp = [A32.alloc((VH,), f"S_fp{h}") for h in range(HH)]
    negbuf = [A32.alloc((130,), f"negbuf{j}") for j in range(2)]
    negneg = [A32.alloc((C,), f"negneg{j}") for j in range(2)]
    rku = [A16.alloc((C,), f"rku{j}") for j in range(2)]
    dcol = [A32.alloc((2,), f"dcol{j}") for j in range(2)]
    khat = [A32.alloc((C,), f"khat{j}") for j in range(2)]
    env.update(o_bf=o_bf, dbufB=dbufB, xmixB=xmixB)

    op("gpsimd", lambda E: E.dma_start(
        out=w_vt.ap(), in_=da["w_vt"].rearrange("(c p) m -> p c m", p=128)),
       writes=[w_vt.buf])
    emit_casts(p, env, range(0, 5))
    for h in range(HH):
        op("vector", lambda E, h=h: E.memset(S_fp[h].ap(), 0.0), writes=[S_fp[h].buf])
        op("vector", lambda E, h=h: E.memset(S_bf[h].ap(), 0.0), writes=[S_bf[h].buf])
    for j in range(2):
        op("vector", lambda E, j=j: E.memset(negbuf[j][:, 0:2], 0.0),
           writes=[negbuf[j].buf])
        op("vector", lambda E, j=j: E.memset(Pfull[j].ap(), 0.0),
           writes=[Pfull[j].buf])

    for ci in range(NCH):
        t0 = ci * C
        emit_delta(p, env, dbufB, t0, 128)
        emit_xmix(p, env, zTb, 0, 3, dbufB, xmixB, t0, 128,
                  (bankof(env, 2, 1), bankof(env, 2, 0)))
        for vh in range(2):
            bap, bbs = bankof(env, 0, vh)
            for c in range(NC_):
                op("tensor", lambda E, vh=vh, c=c, bap=bap: E.matmul(
                    bap, xmixB[:, c, :], w_vt[:, c, vh * 512:(vh + 1) * 512],
                    start=(c == 0), stop=(c == NC_ - 1)),
                   reads=([xmixB.buf, w_vt.buf] if c in (0, NC_ - 1) else []),
                   writes=bbs if c == NC_ - 1 else [],
                   war=bbs if c == 0 else [], inc=(c == NC_ - 1))
            op("vector", lambda E, vh=vh, bap=bap: E.tensor_copy(
                Vc[:, vh * 512:(vh + 1) * 512], bap),
               reads=bbs, writes=[Vc.buf])

        for h in range(HH):
            j = h % 2
            if j == 0:
                o_ap, o_bb = ps[1][:, 0:256], [psb[2]]
                sd_ap, d_ap, sd_bb = ps[3][:, 0:256], ps[3][:, 256:257], [psb[6]]
                tp_ap, tp_bb = ps[3][:, 640:768], [psb[7]]
            else:
                o_ap, o_bb = ps[1][:, 512:768], [psb[3]]
                sd_ap, d_ap, sd_bb = ps[0][:, 0:256], ps[0][:, 256:257], [psb[0]]
                tp_ap, tp_bb = ps[0][:, 640:768], [psb[1]]
            nbb = negbuf[j]
            nng = negneg[j]
            NB = lambda a, b, nbb=nbb: nbb[:, 2 + a:2 + b]
            NBP = lambda a, b, nbb=nbb: nbb[:, 1 + a:1 + b]
            eq, qb, kb, Pf, kT_, rk, dc, kh = (eqv[j], qbf[j], kbf[j], Pfull[j],
                                               khT[j], rku[j], dcol[j], khat[j])
            vslice = Vc[:, h * VH:(h + 1) * VH]
            op("vector", lambda E, h=h, t0=t0, NB=NB: E.tensor_tensor_scan(
                NB(0, 128), env["ones32"][:, 0:128], ew[:, h, t0:t0 + 128], 0.0,
                ALU.mult, ALU.add),
               reads=[ew.buf, env["ones32"].buf], writes=[nbb.buf])
            op("vector", lambda E, nng=nng, NB=NB: E.tensor_scalar_mul(nng.ap(), NB(0, 128), -1.0),
               reads=[nbb.buf], writes=[nng.buf])
            op("scalar", lambda E, eq=eq, NBP=NBP: E.activation(eq[:, 0:128], NBP(0, 128), AF.Exp, scale=-1.0),
               reads=[nbb.buf], writes=[eq.buf])
            op("vector", lambda E, h=h, t0=t0, qb=qb, eq=eq: E.tensor_mul(
                qb[:, 0:128], rT[:, h, t0:t0 + 128], eq[:, 0:128]),
               reads=[rT.buf, eq.buf], writes=[qb.buf])
            op("tensor", lambda E, h=h, o_ap=o_ap, qb=qb: E.matmul(
                o_ap, qb[:, 0:128], S_bf[h].ap(), start=True, stop=False),
               reads=[qb.buf, S_bf[h].buf], war=o_bb, inc=False)
            op("vector", lambda E, h=h, t0=t0, rk=rk: E.tensor_mul(
                rk[:, 0:128], rT[:, h, t0:t0 + 128], kT[:, h, t0:t0 + 128]),
               reads=[rT.buf, kT.buf], writes=[rk.buf])
            op("vector", lambda E, h=h, rk=rk: E.tensor_scalar(
                rk[:, 0:128], rk[:, 0:128], env["UU"](h), None, ALU.mult),
               reads=[rk.buf], writes=[rk.buf])
            op("tensor", lambda E, d_ap=d_ap, rk=rk: E.matmul(
                d_ap, rk[:, 0:128], env["onecol"][:, 0:1],
                start=True, stop=True),
               reads=[rk.buf, env["onecol"].buf], writes=sd_bb, war=sd_bb)
            op("vector", lambda E, dc=dc, d_ap=d_ap: E.tensor_copy(dc[:, 0:1], d_ap),
               reads=sd_bb, writes=[dc.buf])
            for i in range(4):
                sw = (i + 1) * CSUB
                pm_ap, pm_bb = ((ps[2][:, 0:CSUB], [psb[4]]) if i % 2 == 0
                                else (ps[2][:, 512:512 + CSUB], [psb[5]]))
                mid = i * CSUB + CSUB // 2 - 1
                op("scalar", lambda E, i=i, mid=mid, eq=eq, NBP=NBP, NB=NB: E.activation(
                    eq[:, 0:CSUB], NBP(i * CSUB, (i + 1) * CSUB), AF.Exp,
                    scale=-1.0, bias=NB(mid, mid + 1)),
                   reads=[nbb.buf], writes=[eq.buf])
                op("vector", lambda E, i=i, h=h, t0=t0, qb=qb, eq=eq: E.tensor_mul(
                    qb[:, 0:CSUB], rT[:, h, t0 + i * CSUB:t0 + (i + 1) * CSUB],
                    eq[:, 0:CSUB]),
                   reads=[rT.buf, eq.buf], writes=[qb.buf])
                op("scalar", lambda E, sw=sw, mid=mid, eq=eq, NB=NB, nng=nng: E.activation(
                    eq[:, 0:sw], NB(0, sw), AF.Exp, bias=nng[:, mid:mid + 1]),
                   reads=[nbb.buf, nng.buf], writes=[eq.buf])
                op("vector", lambda E, h=h, t0=t0, sw=sw, kb=kb, eq=eq: E.tensor_mul(
                    kb[:, 0:sw], kT[:, h, t0:t0 + sw], eq[:, 0:sw]),
                   reads=[kT.buf, eq.buf], writes=[kb.buf])
                op("tensor", lambda E, sw=sw, pm_ap=pm_ap, kb=kb, qb=qb: E.matmul(
                    pm_ap[0:sw, :], kb[:, 0:sw], qb[:, 0:CSUB],
                    start=True, stop=True),
                   reads=[kb.buf, qb.buf], writes=pm_bb, war=pm_bb)
                op("vector", lambda E, i=i, sw=sw, pm_ap=pm_ap: E.tensor_mul(
                    pm_ap[0:sw, :], pm_ap[0:sw, :], masks[0:sw, i, :]),
                   reads=pm_bb + [masks.buf], writes=pm_bb)
                op("vector", lambda E, i=i, sw=sw, pm_ap=pm_ap, Pf=Pf: E.tensor_copy(
                    Pf[0:sw, i * CSUB:(i + 1) * CSUB], pm_ap[0:sw, :]),
                   reads=pm_bb, writes=[Pf.buf])
            op("tensor", lambda E, o_ap=o_ap, vslice=vslice, Pf=Pf: E.matmul(
                o_ap, Pf.ap(), vslice, start=False, stop=True),
               reads=[Pf.buf, Vc.buf], writes=o_bb, inc=True)
            op("vector", lambda E, h=h, ci=ci, vslice=vslice, o_ap=o_ap, dc=dc: E.scalar_tensor_tensor(
                o_bf[:, ci, h * VH:(h + 1) * VH], vslice, dc[:, 0:1], o_ap,
                ALU.mult, ALU.add),
               reads=[Vc.buf, dc.buf] + o_bb, writes=[o_bf.buf])
            op("scalar", lambda E, eq=eq, NB=NB, nng=nng: E.activation(
                eq[:, 0:128], NB(0, 128), AF.Exp, bias=nng[:, 127:128]),
               reads=[nbb.buf, nng.buf], writes=[eq.buf])
            op("vector", lambda E, h=h, t0=t0, kh=kh, eq=eq: E.tensor_mul(
                kh.ap(), kT[:, h, t0:t0 + 128], eq[:, 0:128]),
               reads=[kT.buf, eq.buf], writes=[kh.buf])
            op("tensor", lambda E, tp_ap=tp_ap, kh=kh: E.transpose(tp_ap, kh.ap(), ident.ap()),
               reads=[kh.buf, ident.buf], writes=tp_bb, war=tp_bb)
            op("vector", lambda E, tp_ap=tp_ap, kT_=kT_: E.tensor_copy(kT_.ap(), tp_ap),
               reads=tp_bb, writes=[kT_.buf])
            op("tensor", lambda E, vslice=vslice, sd_ap=sd_ap, kT_=kT_: E.matmul(
                sd_ap, kT_.ap(), vslice, start=True, stop=True),
               reads=[kT_.buf, Vc.buf], writes=sd_bb, war=sd_bb)
            op("scalar", lambda E, dc=dc, NB=NB: E.activation(
                dc[:, 1:2], NB(127, 128), AF.Exp, scale=-1.0),
               reads=[nbb.buf], writes=[dc.buf])
            op("vector", lambda E, h=h, dc=dc, sd_ap=sd_ap: E.scalar_tensor_tensor(
                S_fp[h].ap(), S_fp[h].ap(), dc[:, 1:2], sd_ap,
                ALU.mult, ALU.add),
               reads=[S_fp[h].buf, dc.buf] + sd_bb, writes=[S_fp[h].buf])
            op("vector", lambda E, h=h: E.tensor_copy(S_bf[h].ap(), S_fp[h].ap()),
               reads=[S_fp[h].buf], writes=[S_bf[h].buf])
    p.barrier()
    A16.free(w_vt, Vc, *S_bf, *eqv, *qbf, *kbf, *Pfull, *khT, *rku,
             env["rT"], env["kT"])
    A32.free(*S_fp, *negbuf, *negneg, *dcol, *khat, env["ew"])
    dump16(p, env, o_bf, "o_bf")


def ph_B2(p, env):
    op, da, A16, A32 = p.op, env["da"], env["A16"], env["A32"]
    ps, psb, xnT = env["ps"], env["psb"], env["xnT"]
    o_bf, zTb, dbufB, xmixB = env["o_bf"], env["zTb"], env["dbufB"], env["xmixB"]

    w_gt = A16.alloc((NC_, VD), "w_gt")
    w_ot = A16.alloc((NC_, D), "w_ot")
    gnw = A16.alloc((D,), "gnw16")
    gnb = A16.alloc((D,), "gnb16")
    gsil = [A16.alloc((VD,), f"gsil{j}") for j in range(2)]
    o_n = [A16.alloc((VD,), f"o_n{j}") for j in range(2)]
    ogT = [A16.alloc((NC_, 128), f"ogT{j}") for j in range(2)]
    stats = A32.alloc((NT, HH, 2), "gnstats")
    sq = A32.alloc((VH,), "gnsq")
    st8 = A32.alloc((8,), "gnst8")
    og = [A32.alloc((VD,), f"og{j}") for j in range(2)]
    s_sb = [A32.alloc((D,), f"s_sb{j}") for j in range(2)]

    op("gpsimd", lambda E: E.dma_start(
        out=w_gt.ap(), in_=da["w_gt"].rearrange("(c p) m -> p c m", p=128)),
       writes=[w_gt.buf])
    op("gpsimd", lambda E: E.dma_start(
        out=w_ot.ap(), in_=da["w_ot"].rearrange("(c p) m -> p c m", p=128)),
       writes=[w_ot.buf])
    op("gpsimd", lambda E: E.dma_start(out=gnw.ap(), in_=da["gnw_rep"][:, :]),
       writes=[gnw.buf])
    op("gpsimd", lambda E: E.dma_start(out=gnb.ap(), in_=da["gnb_rep"][:, :]),
       writes=[gnb.buf])
    emit_casts(p, env, range(5, NE))

    for i in range(NT):
        for h in range(HH):
            osl = o_bf[:, i, h * VH:(h + 1) * VH]
            op("scalar", lambda E, osl=osl: E.activation(
                sq.ap(), osl, AF.Identity, accum_out=st8[:, 0:1]),
               reads=[o_bf.buf], writes=[sq.buf, st8.buf])
            op("scalar", lambda E, osl=osl: E.activation(
                sq.ap(), osl, AF.Square, accum_out=st8[:, 1:2]),
               reads=[o_bf.buf], writes=[sq.buf, st8.buf])
            op("vector", lambda E, i=i, h=h: E.tensor_scalar_mul(
                stats[:, i, h, 0:1], st8[:, 0:1], 1.0 / VH),
               reads=[st8.buf], writes=[stats.buf])
            op("vector", lambda E: E.tensor_scalar_mul(st8[:, 2:3], st8[:, 1:2], 1.0 / VH),
               reads=[st8.buf], writes=[st8.buf])
            op("vector", lambda E, i=i, h=h: E.tensor_mul(
                st8[:, 3:4], stats[:, i, h, 0:1], stats[:, i, h, 0:1]),
               reads=[stats.buf], writes=[st8.buf])
            op("vector", lambda E: E.tensor_sub(st8[:, 4:5], st8[:, 2:3], st8[:, 3:4]),
               reads=[st8.buf], writes=[st8.buf])
            op("scalar", lambda E: E.activation(st8[:, 5:6], st8[:, 4:5], AF.Ln, bias=env["epsb"][:, 0:1]),
               reads=[st8.buf, env["epsb"].buf], writes=[st8.buf])
            op("scalar", lambda E, i=i, h=h: E.activation(
                stats[:, i, h, 1:2], st8[:, 5:6], AF.Exp, scale=-0.5),
               reads=[st8.buf], writes=[stats.buf])

    for i in range(NT):
        t0 = i * 128
        pj = i % 2
        gs, on_, ogj, ogTj, ssb = gsil[pj], o_n[pj], og[pj], ogT[pj], s_sb[pj]
        emit_delta(p, env, dbufB, t0, 128)
        emit_xmix(p, env, zTb, 1, 4, dbufB, xmixB, t0, 128,
                  (bankof(env, 2, 1), bankof(env, 2, 0)))
        for vh in range(2):
            bap, bbs = bankof(env, 0, vh)
            for c in range(NC_):
                op("tensor", lambda E, vh=vh, c=c, bap=bap: E.matmul(
                    bap, xmixB[:, c, :], w_gt[:, c, vh * 512:(vh + 1) * 512],
                    start=(c == 0), stop=(c == NC_ - 1)),
                   reads=([xmixB.buf, w_gt.buf] if c in (0, NC_ - 1) else []),
                   writes=bbs if c == NC_ - 1 else [],
                   war=bbs if c == 0 else [], inc=(c == NC_ - 1))
            op("scalar", lambda E, vh=vh, bap=bap, gs=gs: E.activation(
                gs[:, vh * 512:(vh + 1) * 512], bap, AF.Silu),
               reads=bbs, writes=[gs.buf])
        for h in range(HH):
            op("vector", lambda E, i=i, h=h, on_=on_: E.tensor_scalar(
                on_[:, h * VH:(h + 1) * VH], o_bf[:, i, h * VH:(h + 1) * VH],
                stats[:, i, h, 0:1], stats[:, i, h, 1:2], ALU.subtract, ALU.mult),
               reads=[o_bf.buf, stats.buf], writes=[on_.buf])
        op("vector", lambda E, on_=on_: E.tensor_mul(on_.ap(), on_.ap(), gnw.ap()),
           reads=[gnw.buf, on_.buf], writes=[on_.buf])
        op("vector", lambda E, on_=on_: E.tensor_add(on_.ap(), on_.ap(), gnb.ap()),
           reads=[gnb.buf, on_.buf], writes=[on_.buf])
        op("vector", lambda E, on_=on_, gs=gs, ogj=ogj: E.tensor_mul(ogj.ap(), on_.ap(), gs.ap()),
           reads=[gs.buf, on_.buf], writes=[ogj.buf])
        tp_ap, tp_bb = (ps[3][:, 640:768], [psb[7]]) if pj == 0 else (ps[0][:, 640:768], [psb[1]])
        for c in range(NC_):
            op("tensor", lambda E, c=c, tp_ap=tp_ap, ogj=ogj: E.transpose(
                tp_ap, ogj[:, c * 128:(c + 1) * 128], env["ident"].ap()),
               reads=[ogj.buf, env["ident"].buf], writes=tp_bb, war=tp_bb)
            op("vector", lambda E, c=c, tp_ap=tp_ap, ogTj=ogTj: E.tensor_copy(ogTj[:, c, :], tp_ap),
               reads=tp_bb, writes=[ogTj.buf])
        for oh in range(2):
            bap, bbs = bankof(env, 1, oh) if pj == 0 else bankof(env, 2, oh)
            for c in range(NC_):
                op("tensor", lambda E, oh=oh, c=c, bap=bap, ogTj=ogTj: E.matmul(
                    bap, ogTj[:, c, :], w_ot[:, c, oh * 512:(oh + 1) * 512],
                    start=(c == 0), stop=(c == NC_ - 1)),
                   reads=([ogTj.buf, w_ot.buf] if c in (0, NC_ - 1) else []),
                   writes=bbs if c == NC_ - 1 else [],
                   war=bbs if c == 0 else [], inc=(c == NC_ - 1))
            op("vector", lambda E, oh=oh, bap=bap, ssb=ssb: E.tensor_copy(
                ssb[:, oh * 512:(oh + 1) * 512], bap),
               reads=bbs, writes=[ssb.buf])
        op("sync", lambda E, i=i, ssb=ssb: E.dma_start(
            out=env["s_scr"][i * 128:(i + 1) * 128, :], in_=ssb.ap()),
           reads=[ssb.buf], writes=[env["s_scr_buf"]])
        if "s" in env["dbg"]:
            op("sync", lambda E, i=i, ssb=ssb: E.dma_start(
                out=env["dbg"]["s"][i * 128:(i + 1) * 128, :], in_=ssb.ap()),
               reads=[ssb.buf], dsem="b_dbg")
    p.barrier()
    A16.free(w_gt, w_ot, gnw, gnb, *gsil, *o_n, *ogT, o_bf, env["zTb"],
             env["w_x2s"], env["musb"], env["musb2"], dbufB, xmixB)
    A32.free(stats, sq, st8, *og, *s_sb, env["xbias"])


def ph_ffn(p, env):
    op, da, A16, A32 = p.op, env["da"], env["A16"], env["A32"]
    ps, psb, xnT = env["ps"], env["psb"], env["xnT"]

    Hsb = A16.alloc((32, 1024), "Hsb")
    w1b = [A16.alloc((8, NC_, 128), f"w1b{j}") for j in range(2)]
    w2b = [A16.alloc((1, 1024), f"w2b{j}") for j in range(6)]
    wrt = A16.alloc((NC_, 8), "wrt")
    b2e = A16.alloc((D,), "b2e")
    gatesT = A16.alloc((T,), "gatesT")
    b1t = A32.alloc((NE * 32,), "b1t")
    gates = A32.alloc((NT, 8), "gates")
    gf32 = A32.alloc((16,), "gf32")
    gtmp = A32.alloc((16,), "gtmp")
    acc = A32.alloc((8, D), "acc")
    xres = [A32.alloc((D,), f"xres{j}") for j in range(2)]
    sres = [A32.alloc((D,), f"sres{j}") for j in range(2)]
    gfin = A32.alloc((D,), "gfin")

    op("gpsimd", lambda E: E.dma_start(
        out=wrt.ap(), in_=da["w_routert"].rearrange("(c p) m -> p c m", p=128)),
       writes=[wrt.buf])
    op("gpsimd", lambda E: E.dma_start(out=b2e[0:NE, :], in_=da["b2ext"][:, :]),
       writes=[b2e.buf])
    op("sync", lambda E: E.dma_start(
        out=b1t.ap().rearrange("p (a b) -> p a b", a=NE), in_=da["b1t"][:, :, :]),
       writes=[b1t.buf])

    for i in range(NT):
        t0 = i * 128
        bap, bbs = bankof(env, 2, 0)
        for c in range(NC_):
            op("tensor", lambda E, c=c, bap=bap, t0=t0: E.matmul(
                bap[:, 0:8], xnT[:, c, t0:t0 + 128], wrt[:, c, :],
                start=(c == 0), stop=(c == NC_ - 1)),
               reads=([xnT.buf, wrt.buf] if c in (0, NC_ - 1) else []),
               writes=bbs if c == NC_ - 1 else [],
               war=bbs if c == 0 else [], inc=(c == NC_ - 1))
        op("vector", lambda E, bap=bap: E.reduce_max(gtmp[:, 0:1], bap[:, 0:8], axis=AX.X),
           reads=bbs, writes=[gtmp.buf])
        op("vector", lambda E: E.tensor_scalar_mul(gtmp[:, 1:2], gtmp[:, 0:1], -1.0),
           reads=[gtmp.buf], writes=[gtmp.buf])
        op("scalar", lambda E, bap=bap, i=i: E.activation(
            gates[:, i, :], bap[:, 0:8], AF.Exp, bias=gtmp[:, 1:2]),
           reads=bbs + [gtmp.buf], writes=[gates.buf])
        op("vector", lambda E, i=i: E.reduce_sum(gtmp[:, 2:3], gates[:, i, :], axis=AX.X),
           reads=[gates.buf], writes=[gtmp.buf])
        op("vector", lambda E: E.reciprocal(gtmp[:, 3:4], gtmp[:, 2:3]),
           reads=[gtmp.buf], writes=[gtmp.buf])
        op("vector", lambda E, i=i: E.tensor_scalar(
            gates[:, i, :], gates[:, i, :], gtmp[:, 3:4], None, ALU.mult),
           reads=[gtmp.buf, gates.buf], writes=[gates.buf])
        op("vector", lambda E, i=i: E.reduce_max(gtmp[:, 4:5], gates[:, i, :], axis=AX.X),
           reads=[gates.buf], writes=[gtmp.buf])
        op("vector", lambda E, i=i: E.tensor_scalar(
            gtmp[:, 8:16], gates[:, i, :], gtmp[:, 4:5], None, ALU.is_ge),
           reads=[gates.buf, gtmp.buf], writes=[gtmp.buf])
        op("vector", lambda E, i=i: E.scalar_tensor_tensor(
            gtmp[:, 8:16], gtmp[:, 8:16], -1e30, gates[:, i, :], ALU.mult, ALU.add),
           reads=[gtmp.buf, gates.buf], writes=[gtmp.buf])
        op("vector", lambda E: E.reduce_max(gtmp[:, 5:6], gtmp[:, 8:16], axis=AX.X),
           reads=[gtmp.buf], writes=[gtmp.buf])
        op("vector", lambda E, i=i: E.tensor_scalar(
            gtmp[:, 8:16], gates[:, i, :], gtmp[:, 5:6], None, ALU.is_ge),
           reads=[gates.buf, gtmp.buf], writes=[gtmp.buf])
        op("vector", lambda E, i=i: E.tensor_mul(
            gates[:, i, :], gates[:, i, :], gtmp[:, 8:16]),
           reads=[gtmp.buf, gates.buf], writes=[gates.buf])
        op("vector", lambda E, i=i: E.tensor_copy(gf32[:, 0:8], gates[:, i, :]),
           reads=[gates.buf], writes=[gf32.buf])
        op("vector", lambda E: E.memset(gf32[:, 8:16], 1.0), writes=[gf32.buf])
        tpg, tpg_bb = ps[3][:, 0:128], [psb[6]]
        op("tensor", lambda E, tpg=tpg: E.transpose(
            tpg[0:16, :], gf32.ap(), env["ident"].ap()),
           reads=[gf32.buf, env["ident"].buf], writes=tpg_bb, war=tpg_bb)
        op("vector", lambda E, tpg=tpg, t0=t0: E.tensor_copy(
            gatesT[0:9, t0:t0 + 128], tpg[0:9, :]),
           reads=tpg_bb, writes=[gatesT.buf])
    if "gates" in env["dbg"]:
        p.op("sync", lambda E: E.dma_start(
            out=env["dbg"]["gates"][:, :],
            in_=gates.ap().rearrange("p a b -> p (a b)")),
            reads=[gates.buf], dsem="b_dbg")

    for th in range(2):
        t0 = th * 1024
        for tb in range(8):
            tt = t0 + tb * 128
            for dh in range(2):
                bap, bbs = bankof(env, 2, dh)
                op("tensor", lambda E, bap=bap, tt=tt, dh=dh: E.matmul(
                    bap, gatesT[0:9, tt:tt + 128], b2e[0:9, dh * 512:(dh + 1) * 512],
                    start=True, stop=True),
                   reads=[gatesT.buf, b2e.buf], writes=bbs, war=bbs)
                op("vector", lambda E, tb=tb, dh=dh, bap=bap: E.tensor_copy(
                    acc[:, tb, dh * 512:(dh + 1) * 512], bap),
                   reads=bbs, writes=[acc.buf])
        for e in range(NE):
            idx = th * NE + e
            for q in range(4):
                wb = w1b[q % 2]
                if not (q < 2 and idx > 0):
                    qeng = "sync" if q % 2 == 0 else "gpsimd"
                    p.raw_wait(qeng, f"cast{e}", 32)
                    op(qeng, lambda E, e=e, q=q, wb=wb: E.dma_start(
                        out=wb.ap(), in_=env["w1s"][e, q * 8:(q + 1) * 8].rearrange(
                            "f (c p) m -> p f c m", p=128)),
                       writes=[wb.buf])
                for fi in range(8):
                    ffb = q * 8 + fi
                    banks = [bankof(env, (ffb % 2), tc) for tc in range(2)]
                    for c in range(NC_):
                        for tc in range(2):
                            bap, bbs = banks[tc]
                            op("tensor", lambda E, wb=wb, fi=fi, c=c, bap=bap,
                               tt=t0 + tc * 512: E.matmul(
                                bap, wb[:, fi, c, :], xnT[:, c, tt:tt + 512],
                                start=(c == 0), stop=(c == NC_ - 1)),
                               reads=([wb.buf, xnT.buf]
                                      if (c == 0 and tc == 0) or (c == NC_ - 1 and tc == 1)
                                      else []),
                               writes=bbs if c == NC_ - 1 else [],
                               war=bbs if c == 0 else [],
                               inc=(c == NC_ - 1))
                    for tc in range(2):
                        bap, bbs = banks[tc]
                        op("scalar", lambda E, e=e, ffb=ffb, tc=tc, bap=bap: E.activation(
                            Hsb[:, ffb, tc * 512:(tc + 1) * 512], bap,
                            AF.Gelu_apprx_tanh,
                            bias=b1t[:, e * 32 + ffb:e * 32 + ffb + 1]),
                           reads=bbs + [b1t.buf], writes=[Hsb.buf])
            for qq in range(2):
                for ffc in range(32):
                    pc = qq * 32 + ffc
                    wb2 = w2b[pc % 6]
                    qeng = "sync" if pc % 2 == 0 else "gpsimd"
                    p.raw_wait(qeng, f"cast{e}", 32)
                    op(qeng, lambda E, e=e, ffc=ffc, wb2=wb2: E.dma_start(
                        out=wb2.ap(), in_=env["w2s"][e, ffc * 128:(ffc + 1) * 128, :]
                        .rearrange("(f p) d -> p f d", p=128)),
                       writes=[wb2.buf])
                    for tb in range(4):
                        tbg = qq * 4 + tb
                        for dh in range(2):
                            first = (tb == 0 and dh == 0)
                            last = (tb == 3 and dh == 1)
                            op("tensor", lambda E, wb2=wb2, tb=tb, dh=dh,
                               tbg=tbg, ffc=ffc: E.matmul(
                                ps[tb][:, dh * 512:(dh + 1) * 512],
                                Hsb[:, ffc, tbg * 128:(tbg + 1) * 128],
                                wb2[:, 0, dh * 512:(dh + 1) * 512],
                                start=(ffc == 0), stop=(ffc == 31)),
                               reads=([wb2.buf, Hsb.buf] if (first or last) else []),
                               writes=[psb[2 * tb + dh]] if ffc == 31 else [],
                               war=[psb[2 * tb + dh]] if ffc == 0 else [],
                               inc=(ffc == 31 or last))
                if qq == 0 and idx + 1 < 2 * NE:
                    th2, e2 = divmod(idx + 1, NE)
                    for q2, peng in ((0, "scalar"), (1, "scalar")):
                        wb = w1b[q2]
                        p.raw_wait(peng, f"cast{e2}", 32)
                        op(peng, lambda E, e2=e2, q2=q2, wb=wb: E.dma_start(
                            out=wb.ap(), in_=env["w1s"][e2, q2 * 8:(q2 + 1) * 8]
                            .rearrange("f (c p) m -> p f c m", p=128)),
                           writes=[wb.buf], dsem="b_" + wb.buf.name + "_pf")
                for tb in range(4):
                    tbg = qq * 4 + tb
                    ti = th * 8 + tbg
                    for dh in range(2):
                        sl = slice(dh * 512, (dh + 1) * 512)
                        if e < 8:
                            op("vector", lambda E, tb=tb, tbg=tbg, ti=ti, e=e, sl=sl: E.scalar_tensor_tensor(
                                acc[:, tbg, sl], ps[tb][:, sl], gates[:, ti, e:e + 1],
                                acc[:, tbg, sl], ALU.mult, ALU.add),
                               reads=[psb[2 * tb + dh], gates.buf, acc.buf],
                               writes=[acc.buf])
                        else:
                            op("vector", lambda E, tb=tb, tbg=tbg, sl=sl: E.tensor_add(
                                acc[:, tbg, sl], acc[:, tbg, sl], ps[tb][:, sl]),
                               reads=[psb[2 * tb + dh], acc.buf],
                               writes=[acc.buf])
        accname = f"acc{th}"
        if accname in env["dbg"]:
            p.op("sync", lambda E, accname=accname: E.dma_start(
                out=env["dbg"][accname][:, :],
                in_=acc.ap().rearrange("p a b -> p (a b)")),
                reads=[acc.buf], dsem="b_dbg")
        for tb in range(8):
            i = th * 8 + tb
            xr, sr = xres[tb % 2], sres[tb % 2]
            op("sync", lambda E, i=i, xr=xr: E.dma_start(
                out=xr.ap(), in_=da["x"][i * 128:(i + 1) * 128, :]), writes=[xr.buf])
            op("sync", lambda E, i=i, sr=sr: E.dma_start(
                out=sr.ap(), in_=env["s_scr"][i * 128:(i + 1) * 128, :]),
               reads=[env["s_scr_buf"]], writes=[sr.buf])
            op("scalar", lambda E, tb=tb: E.activation(
                gfin.ap(), acc[:, tb, :], AF.Tanh, scale=SCALE / 2),
               reads=[acc.buf], writes=[gfin.buf])
            op("vector", lambda E: E.tensor_scalar(
                gfin.ap(), gfin.ap(), 0.5, 0.5, ALU.mult, ALU.add),
               reads=[gfin.buf], writes=[gfin.buf])
            op("vector", lambda E, tb=tb: E.scalar_tensor_tensor(
                gfin.ap(), acc[:, tb, :], SCALE, gfin.ap(), ALU.mult, ALU.mult),
               reads=[acc.buf, gfin.buf], writes=[gfin.buf])
            op("vector", lambda E, sr=sr: E.tensor_mul(sr.ap(), sr.ap(), gfin.ap()),
               reads=[sr.buf, gfin.buf], writes=[sr.buf])
            op("vector", lambda E, xr=xr, sr=sr: E.tensor_add(sr.ap(), sr.ap(), xr.ap()),
               reads=[xr.buf, sr.buf], writes=[sr.buf])
            op("sync", lambda E, i=i, sr=sr: E.dma_start(
                out=env["out"][i * 128:(i + 1) * 128, :], in_=sr.ap()),
               reads=[sr.buf], dsem="b_out")
    p.raw_wait("sync", "b_out", p.cnt.get("b_out", 0))


def kernel(**inputs):
    nc = bass.Bass("TRN2", target_bir_lowering=False)
    build(nc)
    g = host_prep(inputs)
    x = np.ascontiguousarray(inputs["L0_attn_hidden_states"], dtype=np.float32)
    in_maps = []
    for c in range(8):
        m = dict(g)
        m["x"] = np.ascontiguousarray(x[c])
        in_maps.append(m)
    res = run_bass_kernel_spmd(nc, in_maps, core_ids=list(range(8)))
    return np.stack([res.results[c]["out"] for c in range(8)], axis=0)




# version 16
# speedup vs baseline: 1.5391x; 1.0230x over previous
import math
from contextlib import ExitStack

import numpy as np

import concourse.bass as bass
import concourse.mybir as mybir
from concourse.bass_utils import run_bass_kernel_spmd

F32 = mybir.dt.float32
BF16 = mybir.dt.bfloat16
AF = mybir.ActivationFunctionType
ALU = mybir.AluOpType
AX = mybir.AxisListType

T = 2048
D = 1024
HH = 4
KD = 512
VD = 1024
VH = 256
C = 128
CSUB = 32
NT = T // 128
NC_ = D // 128
NCH = T // C
DFF = 4096
NE = 9
EPS = 1e-5
SCALE = 1.0 / math.sqrt(1.25)

A16_N = 75776
A32_N = 14848

ENGS = ("tensor", "vector", "scalar", "sync", "gpsimd")
DMA_ENGS = ("sync", "gpsimd")


class Buf:
    __slots__ = ("name", "last_w", "readers")

    def __init__(self, name):
        self.name = name
        self.last_w = None
        self.readers = []


class View:
    __slots__ = ("arena", "off", "dims", "buf", "nelem")

    def __init__(self, arena, off, dims, name):
        self.arena = arena
        self.off = off
        self.dims = tuple(dims)
        self.nelem = int(np.prod(dims)) if dims else 1
        self.buf = Buf(name)

    def ap(self):
        a = self.arena.h[:, self.off:self.off + self.nelem]
        if len(self.dims) > 1:
            pat = "p (" + " ".join(f"d{i}" for i in range(len(self.dims))) + ") -> p " + \
                  " ".join(f"d{i}" for i in range(len(self.dims)))
            a = a.rearrange(pat, **{f"d{i}": d for i, d in enumerate(self.dims)})
        return a

    def __getitem__(self, idx):
        return self.ap()[idx]


class Arena:

    def __init__(self, handle, n):
        self.h = handle
        self.free_list = [(0, n)]

    def alloc(self, dims, name):
        ne = int(np.prod(dims))
        for i, (off, sz) in enumerate(self.free_list):
            aoff = (off + 15) // 16 * 16
            pad = aoff - off
            if sz >= ne + pad:
                rest = sz - ne - pad
                repl = []
                if pad:
                    repl.append((off, pad))
                if rest:
                    repl.append((aoff + ne, rest))
                self.free_list[i:i + 1] = repl
                return View(self, aoff, dims, name)
        raise RuntimeError(f"arena OOM for {name} ({ne})")

    def free(self, *views):
        for v in views:
            self.free_list.append((v.off, v.nelem))
        self.free_list.sort()
        merged = []
        for off, sz in self.free_list:
            if merged and merged[-1][0] + merged[-1][1] >= off:
                po, ps_ = merged[-1]
                merged[-1] = (po, max(ps_, off + sz - po))
            else:
                merged.append((off, sz))
        self.free_list = merged


class P:

    def __init__(self, nc):
        self.nc = nc
        self.stack = None
        self.prog = {e: [] for e in ENGS}
        self.cnt = {e: 0 for e in ENGS}
        self.waited = {e: {} for e in ENGS}
        self.sems = {}

    def getsem(self, name):
        if name not in self.sems:
            self.sems[name] = self.stack.enter_context(self.nc.semaphore(name))
            self.cnt[name] = 0
        return name

    def op(self, eng, fn, reads=(), writes=(), war=(), inc=True, dsem=None):
        waits = {}
        self_sync = eng != "tensor"

        def add(sem, cntv):
            if (sem != eng or self_sync) and cntv > waits.get(sem, -1):
                waits[sem] = cntv

        for b in reads:
            if b.last_w:
                add(*b.last_w)
        for b in tuple(writes) + tuple(war):
            if b.last_w:
                add(*b.last_w)
            for r in b.readers:
                add(*r)
        wl = []
        wd = self.waited[eng]
        for sem, v in waits.items():
            if wd.get(sem, -1) < v:
                wd[sem] = v
                wl.append((sem, v))
        if eng in DMA_ENGS or dsem is not None:
            if dsem is None:
                assert writes, "DMA op needs writes or dsem"
                dsem = "b_" + writes[0].name
            csem = self.getsem(dsem)
            amt = 16
        else:
            csem = eng
            amt = 1
        if inc:
            self.cnt[csem] += amt
            mark = (csem, self.cnt[csem])
            for b in writes:
                b.last_w = mark
                b.readers = []
            for b in reads:
                b.readers.append(mark)
        else:
            assert not writes
        sems = self.sems

        def run(E, wl=wl, fn=fn, inc=inc, csem=csem, amt=amt):
            for sem, v in wl:
                E.wait_ge(sems[sem], v)
            ins = fn(E)
            if inc:
                ins.then_inc(sems[csem], amt)
        self.prog[eng].append(run)

    def raw_wait(self, eng, sem, val):
        if val <= 0:
            return
        wd = self.waited[eng]
        if sem != eng and wd.get(sem, -1) >= val:
            return
        if sem != eng:
            wd[sem] = val
        sems = self.sems
        self.prog[eng].append(lambda E: E.wait_ge(sems[sem], val))

    def barrier(self):
        snap = dict(self.cnt)
        for e in ENGS:
            for o, v in snap.items():
                if o != e and o not in DMA_ENGS:
                    self.raw_wait(e, o, v)


def host_prep(inp):
    g = {}
    g["w_x1t"] = inp["L0_attn_W_x1"].T
    g["w_x2s"] = inp["L0_attn_W_x2"].reshape(D, 5, 32).transpose(2, 1, 0)
    g["xbias_t"] = inp["L0_attn_x_bias"].reshape(5, NC_, 128).transpose(2, 0, 1)
    g["mu_x_t"] = inp["L0_attn_mu_x"].reshape(NC_, 128).T
    g["w_rt"] = inp["L0_attn_W_r"].T
    g["w_kt"] = inp["L0_attn_W_k"].T
    g["a_wt"] = inp["L0_attn_A_w"].T
    g["b_wt"] = inp["L0_attn_B_w"].T
    g["bw_t"] = inp["L0_attn_b_w"].reshape(4, 128).T
    g["w_vt"] = inp["L0_attn_W_v"].T
    g["w_gt"] = inp["L0_attn_W_g"].T
    g["w_ot"] = inp["L0_attn_W_o"].T
    g["u_t"] = inp["L0_attn_bonus"].T
    g["gnw_rep"] = np.broadcast_to(inp["L0_attn_gn_weight"], (128, D))
    g["gnb_rep"] = np.broadcast_to(inp["L0_attn_gn_bias"], (128, D))
    pm = np.zeros((128, 4, CSUB), np.float32)
    for i in range(4):
        s = np.arange(128)[:, None]
        t = i * CSUB + np.arange(CSUB)[None, :]
        pm[:, i, :] = (s < t) & (s < (i + 1) * CSUB)
    g["pmask"] = pm
    g["ident"] = np.eye(128, dtype=np.float32)
    g["w_routert"] = inp["L0_ffn_w_router"].T
    g["b2ext"] = np.concatenate([inp["L0_ffn_b2"], inp["L0_ffn_bs2"][None]], 0)
    w1 = np.concatenate([inp["L0_ffn_W1"], inp["L0_ffn_Ws1"][None]], 0)
    w2 = np.concatenate([inp["L0_ffn_W2"], inp["L0_ffn_Ws2"][None]], 0)
    b1 = np.concatenate([inp["L0_ffn_b1"], inp["L0_ffn_bs1"][None]], 0)
    g["w1p"] = w1.transpose(0, 2, 1).reshape(NE, D, 32, 128).transpose(0, 2, 1, 3)
    g["w2p"] = w2.transpose(0, 2, 1)
    g["b1t"] = b1.reshape(NE, 32, 128).transpose(2, 0, 1)
    return {k: np.ascontiguousarray(v, dtype=np.float32) for k, v in g.items()}


DRAM_SPECS = [
    ("w_x1t", (D, 160)), ("w_x2s", (32, 5, D)), ("xbias_t", (128, 5, NC_)),
    ("mu_x_t", (128, NC_)), ("w_rt", (D, KD)), ("w_kt", (D, KD)),
    ("a_wt", (D, 64)), ("b_wt", (64, KD)), ("bw_t", (128, 4)),
    ("w_vt", (D, VD)), ("w_gt", (D, VD)), ("w_ot", (D, D)),
    ("u_t", (128, 4)), ("gnw_rep", (128, D)), ("gnb_rep", (128, D)),
    ("pmask", (128, 4, CSUB)), ("ident", (128, 128)),
    ("w_routert", (D, 8)), ("b2ext", (NE, D)),
    ("w1p", (NE, 32, D, 128)), ("w2p", (NE, DFF, D)), ("b1t", (128, NE, 32)),
]


def build(nc, dbg=()):
    p = P(nc)
    da = {}
    da["x"] = nc.dram_tensor("x", [T, D], F32, kind="ExternalInput").ap()
    for name, shape in DRAM_SPECS:
        da[name] = nc.dram_tensor(name, list(shape), F32, kind="ExternalInput").ap()
    out_ap = nc.dram_tensor("out", [T, D], F32, kind="ExternalOutput").ap()
    s_scr = nc.dram_tensor("s_scr", [T, D], F32, kind="Internal").ap()
    w1s = nc.dram_tensor("w1s", [NE, 32, D, 128], BF16, kind="Internal").ap()
    w2s = nc.dram_tensor("w2s", [NE, DFF, D], BF16, kind="Internal").ap()
    dbg_out = {}
    for name, shape in dbg:
        dbg_out[name] = nc.dram_tensor("dbg_" + name, list(shape), F32,
                                       kind="ExternalOutput").ap()

    with ExitStack() as st:
        a16h = st.enter_context(nc.sbuf_tensor("arena16", [128, A16_N], BF16))
        a32h = st.enter_context(nc.sbuf_tensor("arena32", [128, A32_N], F32))
        A16 = Arena(a16h, A16_N)
        A32 = Arena(a32h, A32_N)
        ps = [st.enter_context(nc.psum_tensor(f"ps{i}", [128, 1024], F32))
              for i in range(4)]
        psb = [Buf(f"ps{i}h{j}") for i in range(4) for j in range(2)]
        p.stack = st
        for e in ENGS:
            p.sems[e] = st.enter_context(nc.semaphore("sem_" + e))
        for e in range(NE):
            p.sems[f"cast{e}"] = st.enter_context(nc.semaphore(f"sem_cast{e}"))
            p.cnt[f"cast{e}"] = 0

        env = dict(da=da, out=out_ap, s_scr=s_scr, s_scr_buf=Buf("s_scr"),
                   w1s=w1s, w2s=w2s,
                   dbg=dbg_out, ps=ps, psb=psb, A16=A16, A32=A32)
        ph_prologue(p, env)
        ph_A(p, env)
        ph_B(p, env)
        ph_B2(p, env)
        ph_ffn(p, env)

        with nc.Block() as blk:
            for e in ENGS:
                def runner(E, e=e):
                    for fn in p.prog[e]:
                        fn(E)
                getattr(blk, e)(runner)
    return nc


def bankof(env, i, half):
    if half is None:
        return env["ps"][i][:, :], [env["psb"][2 * i], env["psb"][2 * i + 1]]
    return (env["ps"][i][:, half * 512:(half + 1) * 512],
            [env["psb"][2 * i + half]])


def dump16(p, env, view, name):
    if name not in env["dbg"]:
        return
    p.op("gpsimd", lambda E: E.dma_start(
        out=env["dbg"][name][:, :], in_=view.ap().rearrange(
            "p " + " ".join(f"d{i}" for i in range(len(view.dims))) +
            " -> p (" + " ".join(f"d{i}" for i in range(len(view.dims))) + ")")
        if len(view.dims) > 1 else view.ap()),
        reads=[view.buf], dsem="b_dbg")


def dump32(p, env, view, name):
    if name not in env["dbg"]:
        return
    p.op("sync", lambda E: E.dma_start(
        out=env["dbg"][name][:, :], in_=view.ap().rearrange(
            "p " + " ".join(f"d{i}" for i in range(len(view.dims))) +
            " -> p (" + " ".join(f"d{i}" for i in range(len(view.dims))) + ")")
        if len(view.dims) > 1 else view.ap()),
        reads=[view.buf], dsem="b_dbg")


def ph_prologue(p, env):
    op, da, A16, A32, ps, psb = p.op, env["da"], env["A16"], env["A32"], env["ps"], env["psb"]

    xnT = A16.alloc((NC_, T), "xnT")
    onecol = A16.alloc((8,), "onecol")
    w_x2s = A16.alloc((5, D), "w_x2s")
    zTb = A16.alloc((2, T), "zTb")
    musb = A16.alloc((512,), "musb")
    musb2 = A16.alloc((512,), "musb2")
    ident = A32.alloc((128,), "ident")
    masks = A32.alloc((4, CSUB), "pmask")
    small32 = A32.alloc((64,), "small32")
    xbias = A32.alloc((5 * NC_,), "xbias")
    ones32 = A32.alloc((128,), "ones32")
    epsb = A32.alloc((2,), "epsb")
    env.update(xnT=xnT, onecol=onecol, w_x2s=w_x2s, zTb=zTb, musb=musb,
               musb2=musb2, ident=ident, masks=masks, small32=small32,
               xbias=xbias, ones32=ones32, epsb=epsb)
    env["MU_X"] = lambda c: small32[:, c:c + 1]
    env["BW"] = lambda kb: small32[:, 8 + kb:9 + kb]
    env["UU"] = lambda h: small32[:, 12 + h:13 + h]

    op("sync", lambda E: E.dma_start(out=ident.ap(), in_=da["ident"][:, :]),
       writes=[ident.buf])
    op("sync", lambda E: E.dma_start(out=masks.ap().rearrange("p a b -> p (a b)"),
                                     in_=da["pmask"][:, :, :].rearrange("p a b -> p (a b)")),
       writes=[masks.buf])
    op("sync", lambda E: E.dma_start(out=small32[:, 0:8], in_=da["mu_x_t"][:, :]),
       writes=[small32.buf])
    op("sync", lambda E: E.dma_start(out=small32[:, 8:12], in_=da["bw_t"][:, :]),
       writes=[small32.buf])
    op("sync", lambda E: E.dma_start(out=small32[:, 12:16], in_=da["u_t"][:, :]),
       writes=[small32.buf])
    op("sync", lambda E: E.dma_start(out=xbias.ap().rearrange("p (a b) -> p a b", a=5),
                                     in_=da["xbias_t"][:, :, :]),
       writes=[xbias.buf])
    op("vector", lambda E: E.memset(ones32.ap(), 1.0), writes=[ones32.buf])
    op("vector", lambda E: E.memset(epsb.ap(), EPS), writes=[epsb.buf])
    op("vector", lambda E: E.memset(onecol.ap(), 1.0), writes=[onecol.buf])
    op("gpsimd", lambda E: E.dma_start(
        out=w_x2s[0:32, :, :], in_=da["w_x2s"][:, :, :]), writes=[w_x2s.buf])

    xtile = [A32.alloc((D,), f"xtile{j}") for j in range(2)]
    xcn = A32.alloc((D,), "xcn")
    stat = A32.alloc((8,), "lnstat")
    sqv = A32.alloc((D,), "lnsq")
    for i in range(NT):
        xt = xtile[i % 2]
        op("sync", lambda E, i=i, xt=xt: E.dma_start(
            out=xt.ap(), in_=da["x"][i * 128:(i + 1) * 128, :]), writes=[xt.buf])
        op("vector", lambda E, xt=xt: E.reduce_sum(stat[:, 0:1], xt.ap(), axis=AX.X),
           reads=[xt.buf], writes=[stat.buf])
        op("vector", lambda E: E.tensor_scalar_mul(stat[:, 1:2], stat[:, 0:1], 1.0 / D),
           reads=[stat.buf], writes=[stat.buf])
        op("vector", lambda E, xt=xt: E.tensor_scalar(
            xcn.ap(), xt.ap(), stat[:, 1:2], None, ALU.subtract),
           reads=[xt.buf, stat.buf], writes=[xcn.buf])
        op("scalar", lambda E: E.activation(sqv.ap(), xcn.ap(), AF.Square),
           reads=[xcn.buf], writes=[sqv.buf])
        op("vector", lambda E: E.reduce_sum(stat[:, 2:3], sqv.ap(), axis=AX.X),
           reads=[sqv.buf], writes=[stat.buf])
        op("vector", lambda E: E.tensor_scalar_mul(stat[:, 3:4], stat[:, 2:3], 1.0 / D),
           reads=[stat.buf], writes=[stat.buf])
        op("scalar", lambda E: E.activation(stat[:, 4:5], stat[:, 3:4], AF.Ln, bias=epsb[:, 0:1]),
           reads=[stat.buf, epsb.buf], writes=[stat.buf])
        op("scalar", lambda E: E.activation(stat[:, 5:6], stat[:, 4:5], AF.Exp, scale=-0.5),
           reads=[stat.buf], writes=[stat.buf])
        op("vector", lambda E: E.tensor_scalar(xcn.ap(), xcn.ap(), stat[:, 5:6], None, ALU.mult),
           reads=[stat.buf, xcn.buf], writes=[xcn.buf])
        tp = ps[3][:, 0:128]
        for c in range(NC_):
            op("tensor", lambda E, c=c, tp=tp: E.transpose(
                tp, xcn[:, c * 128:(c + 1) * 128], ident.ap()),
               reads=[xcn.buf, ident.buf], writes=[psb[6]], war=[psb[6]])
            op("vector", lambda E, c=c, i=i, tp=tp: E.tensor_copy(
                xnT[:, c, i * 128:(i + 1) * 128], tp),
               reads=[psb[6]], writes=[xnT.buf])
    p.barrier()
    A32.free(xtile[0], xtile[1], xcn, stat, sqv)
    dump16(p, env, xnT, "xnT")


def emit_casts(p, env, experts):
    sems = p.sems
    for e in experts:
        def c1(E, e=e):
            E.dma_start(out=env["w1s"][e], in_=env["da"]["w1p"][e]).then_inc(
                sems[f"cast{e}"], 16)
        def c2(E, e=e):
            E.dma_start(out=env["w2s"][e], in_=env["da"]["w2p"][e]).then_inc(
                sems[f"cast{e}"], 16)
        p.prog["gpsimd"].append(c1)
        p.prog["gpsimd"].append(c2)


def emit_delta(p, env, dbuf, t0, width):
    op, xnT = p.op, env["xnT"]
    for c in range(NC_):
        if t0 == 0:
            op("vector", lambda E, c=c: E.tensor_scalar_mul(
                dbuf[:, c, 0:1], xnT[:, c, 0:1], -1.0),
               reads=[xnT.buf], writes=[dbuf.buf])
            op("vector", lambda E, c=c, w=width: E.tensor_sub(
                dbuf[:, c, 1:w], xnT[:, c, 0:w - 1], xnT[:, c, 1:w]),
               reads=[xnT.buf], writes=[dbuf.buf])
        else:
            op("vector", lambda E, c=c, t0=t0, w=width: E.tensor_sub(
                dbuf[:, c, 0:w], xnT[:, c, t0 - 1:t0 + w - 1], xnT[:, c, t0:t0 + w]),
               reads=[xnT.buf], writes=[dbuf.buf])


def emit_xmix(p, env, zview, zrow, slot, dbuf, xmix, t0, width, pbanks):
    op, xnT, xbias = p.op, env["xnT"], env["xbias"]
    musb2 = (env["musb"], env["musb2"])
    for c in range(NC_):
        bap, bbs = pbanks[c % 2]
        mb = musb2[c % 2]
        op("tensor", lambda E, c=c, bap=bap, zrow=zrow, slot=slot, t0=t0, w=width: E.matmul(
            bap[:, 0:w], env["w_x2s"][0:32, slot, c * 128:(c + 1) * 128],
            zview[0:32, zrow, t0:t0 + w], start=True, stop=True),
           reads=[zview.buf, env["w_x2s"].buf], writes=bbs, war=bbs)
        op("vector", lambda E, c=c, bap=bap, slot=slot, w=width, mb=mb: E.scalar_tensor_tensor(
            mb[:, 0:w], bap[:, 0:w], xbias[:, slot * NC_ + c:slot * NC_ + c + 1],
            dbuf[:, c, 0:w], ALU.add, ALU.mult),
           reads=bbs + [xbias.buf, dbuf.buf], writes=[mb.buf])
        op("vector", lambda E, c=c, t0=t0, w=width, mb=mb: E.tensor_add(
            xmix[:, c, 0:w], mb[:, 0:w], xnT[:, c, t0:t0 + w]),
           reads=[mb.buf, xnT.buf], writes=[xmix.buf])


def emit_proj(p, env, xmix, wview, width, nkb, pbank_fn, evac_fn):
    op = p.op
    for kb in range(nkb):
        bap, bbs = pbank_fn(kb)
        for c in range(NC_):
            op("tensor", lambda E, kb=kb, c=c, bap=bap, w=width: E.matmul(
                bap, wview[:, c, kb * 128:(kb + 1) * 128], xmix[:, c, 0:w],
                start=(c == 0), stop=(c == NC_ - 1)),
               reads=([wview.buf, xmix.buf] if c in (0, NC_ - 1) else []),
               writes=bbs if c == NC_ - 1 else [],
               war=bbs if c == 0 else [],
               inc=(c == NC_ - 1))
        evac_fn(kb, bap, bbs)


def ph_A(p, env):
    op, da, A16, A32 = p.op, env["da"], env["A16"], env["A32"]
    ps, psb, xnT = env["ps"], env["psb"], env["xnT"]

    rT = A16.alloc((4, T), "rT")
    kT = A16.alloc((4, T), "kT")
    zTa = A16.alloc((3, T), "zTa")
    dbufA = A16.alloc((NC_, 512), "dbufA")
    xmixA = A16.alloc((NC_, 512), "xmixA")
    t1w = A16.alloc((512,), "t1w")
    w_x1t = A16.alloc((NC_, 160), "w_x1t")
    w_rt = A16.alloc((NC_, KD), "w_rt")
    w_kt = A16.alloc((NC_, KD), "w_kt")
    a_wt = A16.alloc((NC_, 64), "a_wt")
    b_wt = A16.alloc((KD,), "b_wt")
    ew = A32.alloc((4, T), "ew")
    env.update(rT=rT, kT=kT, ew=ew)
    zTb, musb, xbias = env["zTb"], env["musb"], env["xbias"]

    for name, view in [("w_x1t", w_x1t), ("w_rt", w_rt), ("w_kt", w_kt),
                       ("a_wt", a_wt)]:
        op("gpsimd", lambda E, name=name, view=view: E.dma_start(
            out=view.ap(), in_=da[name].rearrange("(c p) m -> p c m", p=128)),
           writes=[view.buf])
    op("gpsimd", lambda E: E.dma_start(out=b_wt[0:64, :], in_=da["b_wt"][:, :]),
       writes=[b_wt.buf])

    for ts in range(4):
        t0 = ts * 512
        emit_delta(p, env, dbufA, t0, 512)
        for c in range(NC_):
            op("vector", lambda E, c=c, t0=t0: E.scalar_tensor_tensor(
                xmixA[:, c, :], dbufA[:, c, :], env["MU_X"](c),
                xnT[:, c, t0:t0 + 512], ALU.mult, ALU.add),
               reads=[dbufA.buf, xnT.buf], writes=[xmixA.buf])
        zbank = [bankof(env, 0, 0), bankof(env, 0, 1), bankof(env, 1, 0),
                 bankof(env, 1, 1), bankof(env, 2, 0)]
        for n in range(5):
            bap, bbs = zbank[n]
            for c in range(NC_):
                op("tensor", lambda E, n=n, c=c, bap=bap: E.matmul(
                    bap[0:32, :], w_x1t[:, c, n * 32:(n + 1) * 32], xmixA[:, c, :],
                    start=(c == 0), stop=(c == NC_ - 1)),
                   reads=([w_x1t.buf, xmixA.buf] if c in (0, NC_ - 1) else []),
                   writes=bbs if c == NC_ - 1 else [],
                   war=bbs if c == 0 else [], inc=(c == NC_ - 1))
        for n in range(5):
            bap, bbs = zbank[n]
            tgt = (zTa, n) if n < 3 else (zTb, n - 3)
            op("scalar", lambda E, bap=bap, tgt=tgt, t0=t0: E.activation(
                tgt[0][0:32, tgt[1], t0:t0 + 512], bap[0:32, :], AF.Tanh),
               reads=bbs, writes=[tgt[0].buf])

        emit_xmix(p, env, zTa, 0, 0, dbufA, xmixA, t0, 512,
                  (bankof(env, 2, 1), bankof(env, 2, 0)))
        def evac_r(kb, bap, bbs, t0=t0):
            op("vector", lambda E, kb=kb, bap=bap: E.tensor_copy(
                rT[:, kb, t0:t0 + 512], bap),
               reads=bbs, writes=[rT.buf])
        emit_proj(p, env, xmixA, w_rt, 512, 4, lambda kb: bankof(env, 3, kb % 2), evac_r)

        emit_xmix(p, env, zTa, 1, 1, dbufA, xmixA, t0, 512,
                  (bankof(env, 2, 1), bankof(env, 2, 0)))
        bap, bbs = bankof(env, 0, 0)
        for c in range(NC_):
            op("tensor", lambda E, c=c, bap=bap: E.matmul(
                bap[0:64, :], a_wt[:, c, :], xmixA[:, c, :],
                start=(c == 0), stop=(c == NC_ - 1)),
               reads=([a_wt.buf, xmixA.buf] if c in (0, NC_ - 1) else []),
               writes=bbs if c == NC_ - 1 else [],
               war=bbs if c == 0 else [], inc=(c == NC_ - 1))
        op("scalar", lambda E, bap=bap: E.activation(t1w[0:64, :], bap[0:64, :], AF.Tanh),
           reads=bbs, writes=[t1w.buf])
        for kb in range(4):
            bap2, bbs2 = bankof(env, 1, kb % 2)
            op("tensor", lambda E, kb=kb, bap2=bap2: E.matmul(
                bap2, b_wt[0:64, kb * 128:(kb + 1) * 128], t1w[0:64, :],
                start=True, stop=True),
               reads=[t1w.buf, b_wt.buf], writes=bbs2, war=bbs2)
            op("scalar", lambda E, kb=kb, bap2=bap2, t0=t0: E.activation(
                ew[:, kb, t0:t0 + 512], bap2, AF.Exp, bias=env["BW"](kb)),
               reads=bbs2, writes=[ew.buf])

        emit_xmix(p, env, zTa, 2, 2, dbufA, xmixA, t0, 512,
                  (bankof(env, 2, 1), bankof(env, 2, 0)))
        def evac_k(kb, bap, bbs, t0=t0):
            op("vector", lambda E, kb=kb, bap=bap: E.tensor_copy(
                kT[:, kb, t0:t0 + 512], bap),
               reads=bbs, writes=[kT.buf])
        emit_proj(p, env, xmixA, w_kt, 512, 4, lambda kb: bankof(env, 3, kb % 2), evac_k)

    p.barrier()
    A16.free(zTa, dbufA, xmixA, t1w, w_x1t, w_rt, w_kt, a_wt, b_wt)
    dump16(p, env, rT, "rT")
    dump16(p, env, kT, "kT")
    dump32(p, env, ew, "ew")
    dump16(p, env, zTb, "zTb")


def ph_B(p, env):
    op, da, A16, A32 = p.op, env["da"], env["A16"], env["A32"]
    ps, psb, xnT = env["ps"], env["psb"], env["xnT"]
    rT, kT, ew, zTb, masks, ident = env["rT"], env["kT"], env["ew"], env["zTb"], env["masks"], env["ident"]

    w_vt = A16.alloc((NC_, VD), "w_vt")
    o_bf = A16.alloc((NT, VD), "o_bf")
    Vc = A16.alloc((VD,), "Vc")
    S_bf = [A16.alloc((VH,), f"S_bf{h}") for h in range(HH)]
    dbufB = A16.alloc((NC_, 128), "dbufB")
    xmixB = A16.alloc((NC_, 128), "xmixB")
    eqv = [A16.alloc((C,), f"eqv{j}") for j in range(2)]
    qbf = [A16.alloc((C,), f"qbf{j}") for j in range(2)]
    kbf = [A16.alloc((C,), f"kbf{j}") for j in range(2)]
    Pfull = [A16.alloc((C,), f"Pfull{j}") for j in range(2)]
    khT = [A16.alloc((C,), f"khT{j}") for j in range(2)]
    S_fp = [A32.alloc((VH,), f"S_fp{h}") for h in range(HH)]
    negbuf = [A32.alloc((130,), f"negbuf{j}") for j in range(2)]
    negneg = [A32.alloc((C,), f"negneg{j}") for j in range(2)]
    rku = [A16.alloc((C,), f"rku{j}") for j in range(2)]
    dcol = [A32.alloc((2,), f"dcol{j}") for j in range(2)]
    khat = [A32.alloc((C,), f"khat{j}") for j in range(2)]
    env.update(o_bf=o_bf, dbufB=dbufB, xmixB=xmixB)

    op("gpsimd", lambda E: E.dma_start(
        out=w_vt.ap(), in_=da["w_vt"].rearrange("(c p) m -> p c m", p=128)),
       writes=[w_vt.buf])
    emit_casts(p, env, range(0, 5))
    for h in range(HH):
        op("vector", lambda E, h=h: E.memset(S_fp[h].ap(), 0.0), writes=[S_fp[h].buf])
        op("vector", lambda E, h=h: E.memset(S_bf[h].ap(), 0.0), writes=[S_bf[h].buf])
    for j in range(2):
        op("vector", lambda E, j=j: E.memset(negbuf[j][:, 0:2], 0.0),
           writes=[negbuf[j].buf])
        op("vector", lambda E, j=j: E.memset(Pfull[j].ap(), 0.0),
           writes=[Pfull[j].buf])

    for ci in range(NCH):
        t0 = ci * C
        emit_delta(p, env, dbufB, t0, 128)
        emit_xmix(p, env, zTb, 0, 3, dbufB, xmixB, t0, 128,
                  (bankof(env, 2, 1), bankof(env, 2, 0)))
        for vh in range(2):
            bap, bbs = bankof(env, 0, vh)
            for c in range(NC_):
                op("tensor", lambda E, vh=vh, c=c, bap=bap: E.matmul(
                    bap, xmixB[:, c, :], w_vt[:, c, vh * 512:(vh + 1) * 512],
                    start=(c == 0), stop=(c == NC_ - 1)),
                   reads=([xmixB.buf, w_vt.buf] if c in (0, NC_ - 1) else []),
                   writes=bbs if c == NC_ - 1 else [],
                   war=bbs if c == 0 else [], inc=(c == NC_ - 1))
            op("vector", lambda E, vh=vh, bap=bap: E.tensor_copy(
                Vc[:, vh * 512:(vh + 1) * 512], bap),
               reads=bbs, writes=[Vc.buf])

        for h in range(HH):
            j = h % 2
            if j == 0:
                o_ap, o_bb = ps[1][:, 0:256], [psb[2]]
                sd_ap, d_ap, sd_bb = ps[3][:, 0:256], ps[3][:, 256:257], [psb[6]]
                tp_ap, tp_bb = ps[3][:, 640:768], [psb[7]]
            else:
                o_ap, o_bb = ps[1][:, 512:768], [psb[3]]
                sd_ap, d_ap, sd_bb = ps[0][:, 0:256], ps[0][:, 256:257], [psb[0]]
                tp_ap, tp_bb = ps[0][:, 640:768], [psb[1]]
            nbb = negbuf[j]
            nng = negneg[j]
            NB = lambda a, b, nbb=nbb: nbb[:, 2 + a:2 + b]
            NBP = lambda a, b, nbb=nbb: nbb[:, 1 + a:1 + b]
            eq, qb, kb, Pf, kT_, rk, dc, kh = (eqv[j], qbf[j], kbf[j], Pfull[j],
                                               khT[j], rku[j], dcol[j], khat[j])
            vslice = Vc[:, h * VH:(h + 1) * VH]
            op("vector", lambda E, h=h, t0=t0, NB=NB: E.tensor_tensor_scan(
                NB(0, 128), env["ones32"][:, 0:128], ew[:, h, t0:t0 + 128], 0.0,
                ALU.mult, ALU.add),
               reads=[ew.buf, env["ones32"].buf], writes=[nbb.buf])
            op("vector", lambda E, nng=nng, NB=NB: E.tensor_scalar_mul(nng.ap(), NB(0, 128), -1.0),
               reads=[nbb.buf], writes=[nng.buf])
            op("scalar", lambda E, eq=eq, NBP=NBP: E.activation(eq[:, 0:128], NBP(0, 128), AF.Exp, scale=-1.0),
               reads=[nbb.buf], writes=[eq.buf])
            op("vector", lambda E, h=h, t0=t0, qb=qb, eq=eq: E.tensor_mul(
                qb[:, 0:128], rT[:, h, t0:t0 + 128], eq[:, 0:128]),
               reads=[rT.buf, eq.buf], writes=[qb.buf])
            op("tensor", lambda E, h=h, o_ap=o_ap, qb=qb: E.matmul(
                o_ap, qb[:, 0:128], S_bf[h].ap(), start=True, stop=False),
               reads=[qb.buf, S_bf[h].buf], war=o_bb, inc=False)
            op("vector", lambda E, h=h, t0=t0, rk=rk: E.tensor_mul(
                rk[:, 0:128], rT[:, h, t0:t0 + 128], kT[:, h, t0:t0 + 128]),
               reads=[rT.buf, kT.buf], writes=[rk.buf])
            op("vector", lambda E, h=h, rk=rk: E.tensor_scalar(
                rk[:, 0:128], rk[:, 0:128], env["UU"](h), None, ALU.mult),
               reads=[rk.buf], writes=[rk.buf])
            op("tensor", lambda E, d_ap=d_ap, rk=rk: E.matmul(
                d_ap, rk[:, 0:128], env["onecol"][:, 0:1],
                start=True, stop=True),
               reads=[rk.buf, env["onecol"].buf], writes=sd_bb, war=sd_bb)
            op("vector", lambda E, dc=dc, d_ap=d_ap: E.tensor_copy(dc[:, 0:1], d_ap),
               reads=sd_bb, writes=[dc.buf])
            for i in range(4):
                sw = (i + 1) * CSUB
                pm_ap, pm_bb = ((ps[2][:, 0:CSUB], [psb[4]]) if i % 2 == 0
                                else (ps[2][:, 512:512 + CSUB], [psb[5]]))
                mid = i * CSUB + CSUB // 2 - 1
                op("scalar", lambda E, i=i, mid=mid, eq=eq, NBP=NBP, NB=NB: E.activation(
                    eq[:, 0:CSUB], NBP(i * CSUB, (i + 1) * CSUB), AF.Exp,
                    scale=-1.0, bias=NB(mid, mid + 1)),
                   reads=[nbb.buf], writes=[eq.buf])
                op("vector", lambda E, i=i, h=h, t0=t0, qb=qb, eq=eq: E.tensor_mul(
                    qb[:, 0:CSUB], rT[:, h, t0 + i * CSUB:t0 + (i + 1) * CSUB],
                    eq[:, 0:CSUB]),
                   reads=[rT.buf, eq.buf], writes=[qb.buf])
                op("scalar", lambda E, sw=sw, mid=mid, eq=eq, NB=NB, nng=nng: E.activation(
                    eq[:, 0:sw], NB(0, sw), AF.Exp, bias=nng[:, mid:mid + 1]),
                   reads=[nbb.buf, nng.buf], writes=[eq.buf])
                op("vector", lambda E, h=h, t0=t0, sw=sw, kb=kb, eq=eq: E.tensor_mul(
                    kb[:, 0:sw], kT[:, h, t0:t0 + sw], eq[:, 0:sw]),
                   reads=[kT.buf, eq.buf], writes=[kb.buf])
                op("tensor", lambda E, sw=sw, pm_ap=pm_ap, kb=kb, qb=qb: E.matmul(
                    pm_ap[0:sw, :], kb[:, 0:sw], qb[:, 0:CSUB],
                    start=True, stop=True),
                   reads=[kb.buf, qb.buf], writes=pm_bb, war=pm_bb)
                op("vector", lambda E, i=i, sw=sw, pm_ap=pm_ap: E.tensor_mul(
                    pm_ap[0:sw, :], pm_ap[0:sw, :], masks[0:sw, i, :]),
                   reads=pm_bb + [masks.buf], writes=pm_bb)
                op("vector", lambda E, i=i, sw=sw, pm_ap=pm_ap, Pf=Pf: E.tensor_copy(
                    Pf[0:sw, i * CSUB:(i + 1) * CSUB], pm_ap[0:sw, :]),
                   reads=pm_bb, writes=[Pf.buf])
            op("tensor", lambda E, o_ap=o_ap, vslice=vslice, Pf=Pf: E.matmul(
                o_ap, Pf.ap(), vslice, start=False, stop=True),
               reads=[Pf.buf, Vc.buf], writes=o_bb, inc=True)
            op("vector", lambda E, h=h, ci=ci, vslice=vslice, o_ap=o_ap, dc=dc: E.scalar_tensor_tensor(
                o_bf[:, ci, h * VH:(h + 1) * VH], vslice, dc[:, 0:1], o_ap,
                ALU.mult, ALU.add),
               reads=[Vc.buf, dc.buf] + o_bb, writes=[o_bf.buf])
            op("scalar", lambda E, eq=eq, NB=NB, nng=nng: E.activation(
                eq[:, 0:128], NB(0, 128), AF.Exp, bias=nng[:, 127:128]),
               reads=[nbb.buf, nng.buf], writes=[eq.buf])
            op("vector", lambda E, h=h, t0=t0, kh=kh, eq=eq: E.tensor_mul(
                kh.ap(), kT[:, h, t0:t0 + 128], eq[:, 0:128]),
               reads=[kT.buf, eq.buf], writes=[kh.buf])
            op("tensor", lambda E, tp_ap=tp_ap, kh=kh: E.transpose(tp_ap, kh.ap(), ident.ap()),
               reads=[kh.buf, ident.buf], writes=tp_bb, war=tp_bb)
            op("vector", lambda E, tp_ap=tp_ap, kT_=kT_: E.tensor_copy(kT_.ap(), tp_ap),
               reads=tp_bb, writes=[kT_.buf])
            op("tensor", lambda E, vslice=vslice, sd_ap=sd_ap, kT_=kT_: E.matmul(
                sd_ap, kT_.ap(), vslice, start=True, stop=True),
               reads=[kT_.buf, Vc.buf], writes=sd_bb, war=sd_bb)
            op("scalar", lambda E, dc=dc, NB=NB: E.activation(
                dc[:, 1:2], NB(127, 128), AF.Exp, scale=-1.0),
               reads=[nbb.buf], writes=[dc.buf])
            op("vector", lambda E, h=h, dc=dc, sd_ap=sd_ap: E.scalar_tensor_tensor(
                S_fp[h].ap(), S_fp[h].ap(), dc[:, 1:2], sd_ap,
                ALU.mult, ALU.add),
               reads=[S_fp[h].buf, dc.buf] + sd_bb, writes=[S_fp[h].buf])
            op("vector", lambda E, h=h: E.tensor_copy(S_bf[h].ap(), S_fp[h].ap()),
               reads=[S_fp[h].buf], writes=[S_bf[h].buf])
    p.barrier()
    A16.free(w_vt, Vc, *S_bf, *eqv, *qbf, *kbf, *Pfull, *khT, *rku,
             env["rT"], env["kT"])
    A32.free(*S_fp, *negbuf, *negneg, *dcol, *khat, env["ew"])
    dump16(p, env, o_bf, "o_bf")


def ph_B2(p, env):
    op, da, A16, A32 = p.op, env["da"], env["A16"], env["A32"]
    ps, psb, xnT = env["ps"], env["psb"], env["xnT"]
    o_bf, zTb, dbufB, xmixB = env["o_bf"], env["zTb"], env["dbufB"], env["xmixB"]

    w_gt = A16.alloc((NC_, VD), "w_gt")
    w_ot = A16.alloc((NC_, D), "w_ot")
    gnw = A16.alloc((D,), "gnw16")
    gnb = A16.alloc((D,), "gnb16")
    gsil = [A16.alloc((VD,), f"gsil{j}") for j in range(2)]
    o_n = [A16.alloc((VD,), f"o_n{j}") for j in range(2)]
    ogT = [A16.alloc((NC_, 128), f"ogT{j}") for j in range(2)]
    stats = A32.alloc((NT, HH, 2), "gnstats")
    sq = A32.alloc((VH,), "gnsq")
    st8 = A32.alloc((8,), "gnst8")
    og = [A32.alloc((VD,), f"og{j}") for j in range(2)]
    s_sb = [A32.alloc((D,), f"s_sb{j}") for j in range(2)]

    op("gpsimd", lambda E: E.dma_start(
        out=w_gt.ap(), in_=da["w_gt"].rearrange("(c p) m -> p c m", p=128)),
       writes=[w_gt.buf])
    op("gpsimd", lambda E: E.dma_start(
        out=w_ot.ap(), in_=da["w_ot"].rearrange("(c p) m -> p c m", p=128)),
       writes=[w_ot.buf])
    op("gpsimd", lambda E: E.dma_start(out=gnw.ap(), in_=da["gnw_rep"][:, :]),
       writes=[gnw.buf])
    op("gpsimd", lambda E: E.dma_start(out=gnb.ap(), in_=da["gnb_rep"][:, :]),
       writes=[gnb.buf])
    emit_casts(p, env, range(5, NE))

    for i in range(NT):
        for h in range(HH):
            osl = o_bf[:, i, h * VH:(h + 1) * VH]
            op("scalar", lambda E, osl=osl: E.activation(
                sq.ap(), osl, AF.Identity, accum_out=st8[:, 0:1]),
               reads=[o_bf.buf], writes=[sq.buf, st8.buf])
            op("scalar", lambda E, osl=osl: E.activation(
                sq.ap(), osl, AF.Square, accum_out=st8[:, 1:2]),
               reads=[o_bf.buf], writes=[sq.buf, st8.buf])
            op("vector", lambda E, i=i, h=h: E.tensor_scalar_mul(
                stats[:, i, h, 0:1], st8[:, 0:1], 1.0 / VH),
               reads=[st8.buf], writes=[stats.buf])
            op("vector", lambda E: E.tensor_scalar_mul(st8[:, 2:3], st8[:, 1:2], 1.0 / VH),
               reads=[st8.buf], writes=[st8.buf])
            op("vector", lambda E, i=i, h=h: E.tensor_mul(
                st8[:, 3:4], stats[:, i, h, 0:1], stats[:, i, h, 0:1]),
               reads=[stats.buf], writes=[st8.buf])
            op("vector", lambda E: E.tensor_sub(st8[:, 4:5], st8[:, 2:3], st8[:, 3:4]),
               reads=[st8.buf], writes=[st8.buf])
            op("scalar", lambda E: E.activation(st8[:, 5:6], st8[:, 4:5], AF.Ln, bias=env["epsb"][:, 0:1]),
               reads=[st8.buf, env["epsb"].buf], writes=[st8.buf])
            op("scalar", lambda E, i=i, h=h: E.activation(
                stats[:, i, h, 1:2], st8[:, 5:6], AF.Exp, scale=-0.5),
               reads=[st8.buf], writes=[stats.buf])

    for i in range(NT):
        t0 = i * 128
        pj = i % 2
        gs, on_, ogj, ogTj, ssb = gsil[pj], o_n[pj], og[pj], ogT[pj], s_sb[pj]
        emit_delta(p, env, dbufB, t0, 128)
        emit_xmix(p, env, zTb, 1, 4, dbufB, xmixB, t0, 128,
                  (bankof(env, 2, 1), bankof(env, 2, 0)))
        for vh in range(2):
            bap, bbs = bankof(env, 0, vh)
            for c in range(NC_):
                op("tensor", lambda E, vh=vh, c=c, bap=bap: E.matmul(
                    bap, xmixB[:, c, :], w_gt[:, c, vh * 512:(vh + 1) * 512],
                    start=(c == 0), stop=(c == NC_ - 1)),
                   reads=([xmixB.buf, w_gt.buf] if c in (0, NC_ - 1) else []),
                   writes=bbs if c == NC_ - 1 else [],
                   war=bbs if c == 0 else [], inc=(c == NC_ - 1))
            op("scalar", lambda E, vh=vh, bap=bap, gs=gs: E.activation(
                gs[:, vh * 512:(vh + 1) * 512], bap, AF.Silu),
               reads=bbs, writes=[gs.buf])
        for h in range(HH):
            op("vector", lambda E, i=i, h=h, on_=on_: E.tensor_scalar(
                on_[:, h * VH:(h + 1) * VH], o_bf[:, i, h * VH:(h + 1) * VH],
                stats[:, i, h, 0:1], stats[:, i, h, 1:2], ALU.subtract, ALU.mult),
               reads=[o_bf.buf, stats.buf], writes=[on_.buf])
        op("vector", lambda E, on_=on_: E.tensor_mul(on_.ap(), on_.ap(), gnw.ap()),
           reads=[gnw.buf, on_.buf], writes=[on_.buf])
        op("vector", lambda E, on_=on_: E.tensor_add(on_.ap(), on_.ap(), gnb.ap()),
           reads=[gnb.buf, on_.buf], writes=[on_.buf])
        op("vector", lambda E, on_=on_, gs=gs, ogj=ogj: E.tensor_mul(ogj.ap(), on_.ap(), gs.ap()),
           reads=[gs.buf, on_.buf], writes=[ogj.buf])
        tp_ap, tp_bb = (ps[3][:, 640:768], [psb[7]]) if pj == 0 else (ps[0][:, 640:768], [psb[1]])
        for c in range(NC_):
            op("tensor", lambda E, c=c, tp_ap=tp_ap, ogj=ogj: E.transpose(
                tp_ap, ogj[:, c * 128:(c + 1) * 128], env["ident"].ap()),
               reads=[ogj.buf, env["ident"].buf], writes=tp_bb, war=tp_bb)
            op("vector", lambda E, c=c, tp_ap=tp_ap, ogTj=ogTj: E.tensor_copy(ogTj[:, c, :], tp_ap),
               reads=tp_bb, writes=[ogTj.buf])
        for oh in range(2):
            bap, bbs = bankof(env, 1, oh) if pj == 0 else bankof(env, 2, oh)
            for c in range(NC_):
                op("tensor", lambda E, oh=oh, c=c, bap=bap, ogTj=ogTj: E.matmul(
                    bap, ogTj[:, c, :], w_ot[:, c, oh * 512:(oh + 1) * 512],
                    start=(c == 0), stop=(c == NC_ - 1)),
                   reads=([ogTj.buf, w_ot.buf] if c in (0, NC_ - 1) else []),
                   writes=bbs if c == NC_ - 1 else [],
                   war=bbs if c == 0 else [], inc=(c == NC_ - 1))
            op("vector", lambda E, oh=oh, bap=bap, ssb=ssb: E.tensor_copy(
                ssb[:, oh * 512:(oh + 1) * 512], bap),
               reads=bbs, writes=[ssb.buf])
        op("sync", lambda E, i=i, ssb=ssb: E.dma_start(
            out=env["s_scr"][i * 128:(i + 1) * 128, :], in_=ssb.ap()),
           reads=[ssb.buf], writes=[env["s_scr_buf"]])
        if "s" in env["dbg"]:
            op("sync", lambda E, i=i, ssb=ssb: E.dma_start(
                out=env["dbg"]["s"][i * 128:(i + 1) * 128, :], in_=ssb.ap()),
               reads=[ssb.buf], dsem="b_dbg")
    p.barrier()
    A16.free(w_gt, w_ot, gnw, gnb, *gsil, *o_n, *ogT, o_bf, env["zTb"],
             env["w_x2s"], env["musb"], env["musb2"], dbufB, xmixB)
    A32.free(stats, sq, st8, *og, *s_sb, env["xbias"])


def ph_ffn(p, env):
    op, da, A16, A32 = p.op, env["da"], env["A16"], env["A32"]
    ps, psb, xnT = env["ps"], env["psb"], env["xnT"]

    Hsb = A16.alloc((32, 1024), "Hsb")
    w1b = [A16.alloc((8, NC_, 128), f"w1b{j}") for j in range(2)]
    w2b = [A16.alloc((1, 1024), f"w2b{j}") for j in range(6)]
    wrt = A16.alloc((NC_, 8), "wrt")
    b2e = A16.alloc((D,), "b2e")
    gatesT = A16.alloc((T,), "gatesT")
    b1t = A32.alloc((NE * 32,), "b1t")
    gates = A32.alloc((NT, 8), "gates")
    gf32 = A32.alloc((16,), "gf32")
    gtmp = A32.alloc((16,), "gtmp")
    acc = A32.alloc((8, D), "acc")
    xres = [A32.alloc((D,), f"xres{j}") for j in range(2)]
    sres = [A32.alloc((D,), f"sres{j}") for j in range(2)]
    gfin = A32.alloc((D,), "gfin")

    op("gpsimd", lambda E: E.dma_start(
        out=wrt.ap(), in_=da["w_routert"].rearrange("(c p) m -> p c m", p=128)),
       writes=[wrt.buf])
    op("gpsimd", lambda E: E.dma_start(out=b2e[0:NE, :], in_=da["b2ext"][:, :]),
       writes=[b2e.buf])
    op("sync", lambda E: E.dma_start(
        out=b1t.ap().rearrange("p (a b) -> p a b", a=NE), in_=da["b1t"][:, :, :]),
       writes=[b1t.buf])

    for i in range(NT):
        t0 = i * 128
        bap, bbs = bankof(env, 2, 0)
        for c in range(NC_):
            op("tensor", lambda E, c=c, bap=bap, t0=t0: E.matmul(
                bap[:, 0:8], xnT[:, c, t0:t0 + 128], wrt[:, c, :],
                start=(c == 0), stop=(c == NC_ - 1)),
               reads=([xnT.buf, wrt.buf] if c in (0, NC_ - 1) else []),
               writes=bbs if c == NC_ - 1 else [],
               war=bbs if c == 0 else [], inc=(c == NC_ - 1))
        op("vector", lambda E, bap=bap: E.reduce_max(gtmp[:, 0:1], bap[:, 0:8], axis=AX.X),
           reads=bbs, writes=[gtmp.buf])
        op("vector", lambda E: E.tensor_scalar_mul(gtmp[:, 1:2], gtmp[:, 0:1], -1.0),
           reads=[gtmp.buf], writes=[gtmp.buf])
        op("scalar", lambda E, bap=bap, i=i: E.activation(
            gates[:, i, :], bap[:, 0:8], AF.Exp, bias=gtmp[:, 1:2]),
           reads=bbs + [gtmp.buf], writes=[gates.buf])
        op("vector", lambda E, i=i: E.reduce_sum(gtmp[:, 2:3], gates[:, i, :], axis=AX.X),
           reads=[gates.buf], writes=[gtmp.buf])
        op("vector", lambda E: E.reciprocal(gtmp[:, 3:4], gtmp[:, 2:3]),
           reads=[gtmp.buf], writes=[gtmp.buf])
        op("vector", lambda E, i=i: E.tensor_scalar(
            gates[:, i, :], gates[:, i, :], gtmp[:, 3:4], None, ALU.mult),
           reads=[gtmp.buf, gates.buf], writes=[gates.buf])
        op("vector", lambda E, i=i: E.reduce_max(gtmp[:, 4:5], gates[:, i, :], axis=AX.X),
           reads=[gates.buf], writes=[gtmp.buf])
        op("vector", lambda E, i=i: E.tensor_scalar(
            gtmp[:, 8:16], gates[:, i, :], gtmp[:, 4:5], None, ALU.is_ge),
           reads=[gates.buf, gtmp.buf], writes=[gtmp.buf])
        op("vector", lambda E, i=i: E.scalar_tensor_tensor(
            gtmp[:, 8:16], gtmp[:, 8:16], -1e30, gates[:, i, :], ALU.mult, ALU.add),
           reads=[gtmp.buf, gates.buf], writes=[gtmp.buf])
        op("vector", lambda E: E.reduce_max(gtmp[:, 5:6], gtmp[:, 8:16], axis=AX.X),
           reads=[gtmp.buf], writes=[gtmp.buf])
        op("vector", lambda E, i=i: E.tensor_scalar(
            gtmp[:, 8:16], gates[:, i, :], gtmp[:, 5:6], None, ALU.is_ge),
           reads=[gates.buf, gtmp.buf], writes=[gtmp.buf])
        op("vector", lambda E, i=i: E.tensor_mul(
            gates[:, i, :], gates[:, i, :], gtmp[:, 8:16]),
           reads=[gtmp.buf, gates.buf], writes=[gates.buf])
        op("vector", lambda E, i=i: E.tensor_copy(gf32[:, 0:8], gates[:, i, :]),
           reads=[gates.buf], writes=[gf32.buf])
        op("vector", lambda E: E.memset(gf32[:, 8:16], 1.0), writes=[gf32.buf])
        tpg, tpg_bb = ps[3][:, 0:128], [psb[6]]
        op("tensor", lambda E, tpg=tpg: E.transpose(
            tpg[0:16, :], gf32.ap(), env["ident"].ap()),
           reads=[gf32.buf, env["ident"].buf], writes=tpg_bb, war=tpg_bb)
        op("vector", lambda E, tpg=tpg, t0=t0: E.tensor_copy(
            gatesT[0:9, t0:t0 + 128], tpg[0:9, :]),
           reads=tpg_bb, writes=[gatesT.buf])
    if "gates" in env["dbg"]:
        p.op("sync", lambda E: E.dma_start(
            out=env["dbg"]["gates"][:, :],
            in_=gates.ap().rearrange("p a b -> p (a b)")),
            reads=[gates.buf], dsem="b_dbg")

    for th in range(2):
        t0 = th * 1024
        for tb in range(8):
            tt = t0 + tb * 128
            for dh in range(2):
                bap, bbs = bankof(env, 2, dh)
                op("tensor", lambda E, bap=bap, tt=tt, dh=dh: E.matmul(
                    bap, gatesT[0:9, tt:tt + 128], b2e[0:9, dh * 512:(dh + 1) * 512],
                    start=True, stop=True),
                   reads=[gatesT.buf, b2e.buf], writes=bbs, war=bbs)
                op("vector", lambda E, tb=tb, dh=dh, bap=bap: E.tensor_copy(
                    acc[:, tb, dh * 512:(dh + 1) * 512], bap),
                   reads=bbs, writes=[acc.buf])
        for e in range(NE):
            idx = th * NE + e
            for q in range(4):
                wb = w1b[q % 2]
                if not (q < 2 and idx > 0):
                    qeng = "sync" if q % 2 == 0 else "gpsimd"
                    p.raw_wait(qeng, f"cast{e}", 32)
                    op(qeng, lambda E, e=e, q=q, wb=wb: E.dma_start(
                        out=wb.ap(), in_=env["w1s"][e, q * 8:(q + 1) * 8].rearrange(
                            "f (c p) m -> p f c m", p=128)),
                       writes=[wb.buf])
                for fi in range(8):
                    ffb = q * 8 + fi
                    banks = [bankof(env, (ffb % 2), tc) for tc in range(2)]
                    for c in range(NC_):
                        for tc in range(2):
                            bap, bbs = banks[tc]
                            op("tensor", lambda E, wb=wb, fi=fi, c=c, bap=bap,
                               tt=t0 + tc * 512: E.matmul(
                                bap, wb[:, fi, c, :], xnT[:, c, tt:tt + 512],
                                start=(c == 0), stop=(c == NC_ - 1)),
                               reads=([wb.buf, xnT.buf]
                                      if (c == 0 and tc == 0) or (c == NC_ - 1 and tc == 1)
                                      else []),
                               writes=bbs if c == NC_ - 1 else [],
                               war=bbs if c == 0 else [],
                               inc=(c == NC_ - 1))
                    for tc in range(2):
                        bap, bbs = banks[tc]
                        op("scalar", lambda E, e=e, ffb=ffb, tc=tc, bap=bap: E.activation(
                            Hsb[:, ffb, tc * 512:(tc + 1) * 512], bap,
                            AF.Gelu_apprx_tanh,
                            bias=b1t[:, e * 32 + ffb:e * 32 + ffb + 1]),
                           reads=bbs + [b1t.buf], writes=[Hsb.buf])
            for qq in range(2):
                for ffc in range(32):
                    pc = qq * 32 + ffc
                    wb2 = w2b[pc % 6]
                    qeng = "sync" if pc % 2 == 0 else "gpsimd"
                    p.raw_wait(qeng, f"cast{e}", 32)
                    op(qeng, lambda E, e=e, ffc=ffc, wb2=wb2: E.dma_start(
                        out=wb2.ap(), in_=env["w2s"][e, ffc * 128:(ffc + 1) * 128, :]
                        .rearrange("(f p) d -> p f d", p=128)),
                       writes=[wb2.buf])
                    for tb in range(4):
                        tbg = qq * 4 + tb
                        for dh in range(2):
                            first = (tb == 0 and dh == 0)
                            last = (tb == 3 and dh == 1)
                            op("tensor", lambda E, wb2=wb2, tb=tb, dh=dh,
                               tbg=tbg, ffc=ffc: E.matmul(
                                ps[tb][:, dh * 512:(dh + 1) * 512],
                                Hsb[:, ffc, tbg * 128:(tbg + 1) * 128],
                                wb2[:, 0, dh * 512:(dh + 1) * 512],
                                start=(ffc == 0), stop=(ffc == 31)),
                               reads=([wb2.buf, Hsb.buf] if (first or last) else []),
                               writes=[psb[2 * tb + dh]] if ffc == 31 else [],
                               war=[psb[2 * tb + dh]] if ffc == 0 else [],
                               inc=(ffc == 31 or last))
                if qq == 0 and idx + 1 < 2 * NE:
                    th2, e2 = divmod(idx + 1, NE)
                    for q2, peng in ((0, "scalar"), (1, "scalar")):
                        wb = w1b[q2]
                        p.raw_wait(peng, f"cast{e2}", 32)
                        op(peng, lambda E, e2=e2, q2=q2, wb=wb: E.dma_start(
                            out=wb.ap(), in_=env["w1s"][e2, q2 * 8:(q2 + 1) * 8]
                            .rearrange("f (c p) m -> p f c m", p=128)),
                           writes=[wb.buf], dsem="b_" + wb.buf.name + "_pf")
                for tb in range(4):
                    tbg = qq * 4 + tb
                    ti = th * 8 + tbg
                    for dh in range(2):
                        sl = slice(dh * 512, (dh + 1) * 512)
                        if e < 8:
                            op("vector", lambda E, tb=tb, tbg=tbg, ti=ti, e=e, sl=sl: E.scalar_tensor_tensor(
                                acc[:, tbg, sl], ps[tb][:, sl], gates[:, ti, e:e + 1],
                                acc[:, tbg, sl], ALU.mult, ALU.add),
                               reads=[psb[2 * tb + dh], gates.buf, acc.buf],
                               writes=[acc.buf])
                        else:
                            op("vector", lambda E, tb=tb, tbg=tbg, sl=sl: E.tensor_add(
                                acc[:, tbg, sl], acc[:, tbg, sl], ps[tb][:, sl]),
                               reads=[psb[2 * tb + dh], acc.buf],
                               writes=[acc.buf])
        accname = f"acc{th}"
        if accname in env["dbg"]:
            p.op("sync", lambda E, accname=accname: E.dma_start(
                out=env["dbg"][accname][:, :],
                in_=acc.ap().rearrange("p a b -> p (a b)")),
                reads=[acc.buf], dsem="b_dbg")
        for tb in range(8):
            i = th * 8 + tb
            xr, sr = xres[tb % 2], sres[tb % 2]
            op("sync", lambda E, i=i, xr=xr: E.dma_start(
                out=xr.ap(), in_=da["x"][i * 128:(i + 1) * 128, :]), writes=[xr.buf])
            op("sync", lambda E, i=i, sr=sr: E.dma_start(
                out=sr.ap(), in_=env["s_scr"][i * 128:(i + 1) * 128, :]),
               reads=[env["s_scr_buf"]], writes=[sr.buf])
            op("scalar", lambda E, tb=tb: E.activation(
                gfin.ap(), acc[:, tb, :], AF.Tanh, scale=SCALE / 2),
               reads=[acc.buf], writes=[gfin.buf])
            op("vector", lambda E: E.tensor_scalar(
                gfin.ap(), gfin.ap(), 0.5, 0.5, ALU.mult, ALU.add),
               reads=[gfin.buf], writes=[gfin.buf])
            op("vector", lambda E, tb=tb: E.scalar_tensor_tensor(
                gfin.ap(), acc[:, tb, :], SCALE, gfin.ap(), ALU.mult, ALU.mult),
               reads=[acc.buf, gfin.buf], writes=[gfin.buf])
            op("vector", lambda E, sr=sr: E.tensor_mul(sr.ap(), sr.ap(), gfin.ap()),
               reads=[sr.buf, gfin.buf], writes=[sr.buf])
            op("vector", lambda E, xr=xr, sr=sr: E.tensor_add(sr.ap(), sr.ap(), xr.ap()),
               reads=[xr.buf, sr.buf], writes=[sr.buf])
            op("sync", lambda E, i=i, sr=sr: E.dma_start(
                out=env["out"][i * 128:(i + 1) * 128, :], in_=sr.ap()),
               reads=[sr.buf], dsem=f"b_out{tb % 2}")
    p.raw_wait("sync", "b_out0", p.cnt.get("b_out0", 0))
    p.raw_wait("sync", "b_out1", p.cnt.get("b_out1", 0))


def kernel(**inputs):
    nc = bass.Bass("TRN2", target_bir_lowering=False)
    build(nc)
    g = host_prep(inputs)
    x = np.ascontiguousarray(inputs["L0_attn_hidden_states"], dtype=np.float32)
    in_maps = []
    for c in range(8):
        m = dict(g)
        m["x"] = np.ascontiguousarray(x[c])
        in_maps.append(m)
    res = run_bass_kernel_spmd(nc, in_maps, core_ids=list(range(8)))
    return np.stack([res.results[c]["out"] for c in range(8)], axis=0)




# version 17 (final)
# speedup vs baseline: 1.5409x; 1.0012x over previous
import math
from contextlib import ExitStack

import numpy as np

import concourse.bass as bass
import concourse.mybir as mybir
from concourse.bass_utils import run_bass_kernel_spmd

F32 = mybir.dt.float32
BF16 = mybir.dt.bfloat16
AF = mybir.ActivationFunctionType
ALU = mybir.AluOpType
AX = mybir.AxisListType

T = 2048
D = 1024
HH = 4
KD = 512
VD = 1024
VH = 256
C = 128
CSUB = 32
NT = T // 128
NC_ = D // 128
NCH = T // C
DFF = 4096
NE = 9
EPS = 1e-5
SCALE = 1.0 / math.sqrt(1.25)

A16_N = 75776
A32_N = 14848

ENGS = ("tensor", "vector", "scalar", "sync", "gpsimd")
DMA_ENGS = ("sync", "gpsimd")


class Buf:
    __slots__ = ("name", "last_w", "readers")

    def __init__(self, name):
        self.name = name
        self.last_w = None
        self.readers = []


class View:
    __slots__ = ("arena", "off", "dims", "buf", "nelem")

    def __init__(self, arena, off, dims, name):
        self.arena = arena
        self.off = off
        self.dims = tuple(dims)
        self.nelem = int(np.prod(dims)) if dims else 1
        self.buf = Buf(name)

    def ap(self):
        a = self.arena.h[:, self.off:self.off + self.nelem]
        if len(self.dims) > 1:
            pat = "p (" + " ".join(f"d{i}" for i in range(len(self.dims))) + ") -> p " + \
                  " ".join(f"d{i}" for i in range(len(self.dims)))
            a = a.rearrange(pat, **{f"d{i}": d for i, d in enumerate(self.dims)})
        return a

    def __getitem__(self, idx):
        return self.ap()[idx]


class Arena:

    def __init__(self, handle, n):
        self.h = handle
        self.free_list = [(0, n)]

    def alloc(self, dims, name):
        ne = int(np.prod(dims))
        for i, (off, sz) in enumerate(self.free_list):
            aoff = (off + 15) // 16 * 16
            pad = aoff - off
            if sz >= ne + pad:
                rest = sz - ne - pad
                repl = []
                if pad:
                    repl.append((off, pad))
                if rest:
                    repl.append((aoff + ne, rest))
                self.free_list[i:i + 1] = repl
                return View(self, aoff, dims, name)
        raise RuntimeError(f"arena OOM for {name} ({ne})")

    def free(self, *views):
        for v in views:
            self.free_list.append((v.off, v.nelem))
        self.free_list.sort()
        merged = []
        for off, sz in self.free_list:
            if merged and merged[-1][0] + merged[-1][1] >= off:
                po, ps_ = merged[-1]
                merged[-1] = (po, max(ps_, off + sz - po))
            else:
                merged.append((off, sz))
        self.free_list = merged


class P:

    def __init__(self, nc):
        self.nc = nc
        self.stack = None
        self.prog = {e: [] for e in ENGS}
        self.cnt = {e: 0 for e in ENGS}
        self.waited = {e: {} for e in ENGS}
        self.sems = {}

    def getsem(self, name):
        if name not in self.sems:
            self.sems[name] = self.stack.enter_context(self.nc.semaphore(name))
            self.cnt[name] = 0
        return name

    def op(self, eng, fn, reads=(), writes=(), war=(), inc=True, dsem=None):
        waits = {}
        self_sync = eng != "tensor"

        def add(sem, cntv):
            if (sem != eng or self_sync) and cntv > waits.get(sem, -1):
                waits[sem] = cntv

        for b in reads:
            if b.last_w:
                add(*b.last_w)
        for b in tuple(writes) + tuple(war):
            if b.last_w:
                add(*b.last_w)
            for r in b.readers:
                add(*r)
        wl = []
        wd = self.waited[eng]
        for sem, v in waits.items():
            if wd.get(sem, -1) < v:
                wd[sem] = v
                wl.append((sem, v))
        if eng in DMA_ENGS or dsem is not None:
            if dsem is None:
                assert writes, "DMA op needs writes or dsem"
                dsem = "b_" + writes[0].name
            csem = self.getsem(dsem)
            amt = 16
        else:
            csem = eng
            amt = 1
        if inc:
            self.cnt[csem] += amt
            mark = (csem, self.cnt[csem])
            for b in writes:
                b.last_w = mark
                b.readers = []
            for b in reads:
                b.readers.append(mark)
        else:
            assert not writes
        sems = self.sems

        def run(E, wl=wl, fn=fn, inc=inc, csem=csem, amt=amt):
            for sem, v in wl:
                E.wait_ge(sems[sem], v)
            ins = fn(E)
            if inc:
                ins.then_inc(sems[csem], amt)
        self.prog[eng].append(run)

    def raw_wait(self, eng, sem, val):
        if val <= 0:
            return
        wd = self.waited[eng]
        if sem != eng and wd.get(sem, -1) >= val:
            return
        if sem != eng:
            wd[sem] = val
        sems = self.sems
        self.prog[eng].append(lambda E: E.wait_ge(sems[sem], val))

    def barrier(self):
        snap = dict(self.cnt)
        for e in ENGS:
            for o, v in snap.items():
                if o != e and o not in DMA_ENGS:
                    self.raw_wait(e, o, v)


def host_prep(inp):
    g = {}
    g["w_x1t"] = inp["L0_attn_W_x1"].T
    g["w_x2s"] = inp["L0_attn_W_x2"].reshape(D, 5, 32).transpose(2, 1, 0)
    g["xbias_t"] = inp["L0_attn_x_bias"].reshape(5, NC_, 128).transpose(2, 0, 1)
    g["mu_x_t"] = inp["L0_attn_mu_x"].reshape(NC_, 128).T
    g["w_rt"] = inp["L0_attn_W_r"].T
    g["w_kt"] = inp["L0_attn_W_k"].T
    g["a_wt"] = inp["L0_attn_A_w"].T
    g["b_wt"] = inp["L0_attn_B_w"].T
    g["bw_t"] = inp["L0_attn_b_w"].reshape(4, 128).T
    g["w_vt"] = inp["L0_attn_W_v"].T
    g["w_gt"] = inp["L0_attn_W_g"].T
    g["w_ot"] = inp["L0_attn_W_o"].T
    g["u_t"] = inp["L0_attn_bonus"].T
    g["gnw_rep"] = np.broadcast_to(inp["L0_attn_gn_weight"], (128, D))
    g["gnb_rep"] = np.broadcast_to(inp["L0_attn_gn_bias"], (128, D))
    pm = np.zeros((128, 4, CSUB), np.float32)
    for i in range(4):
        s = np.arange(128)[:, None]
        t = i * CSUB + np.arange(CSUB)[None, :]
        pm[:, i, :] = (s < t) & (s < (i + 1) * CSUB)
    g["pmask"] = pm
    g["ident"] = np.eye(128, dtype=np.float32)
    g["w_routert"] = inp["L0_ffn_w_router"].T
    g["b2ext"] = np.concatenate([inp["L0_ffn_b2"], inp["L0_ffn_bs2"][None]], 0)
    w1 = np.concatenate([inp["L0_ffn_W1"], inp["L0_ffn_Ws1"][None]], 0)
    w2 = np.concatenate([inp["L0_ffn_W2"], inp["L0_ffn_Ws2"][None]], 0)
    b1 = np.concatenate([inp["L0_ffn_b1"], inp["L0_ffn_bs1"][None]], 0)
    g["w1p"] = w1.transpose(0, 2, 1).reshape(NE, D, 32, 128).transpose(0, 2, 1, 3)
    g["w2p"] = w2.transpose(0, 2, 1)
    g["b1t"] = b1.reshape(NE, 32, 128).transpose(2, 0, 1)
    return {k: np.ascontiguousarray(v, dtype=np.float32) for k, v in g.items()}


DRAM_SPECS = [
    ("w_x1t", (D, 160)), ("w_x2s", (32, 5, D)), ("xbias_t", (128, 5, NC_)),
    ("mu_x_t", (128, NC_)), ("w_rt", (D, KD)), ("w_kt", (D, KD)),
    ("a_wt", (D, 64)), ("b_wt", (64, KD)), ("bw_t", (128, 4)),
    ("w_vt", (D, VD)), ("w_gt", (D, VD)), ("w_ot", (D, D)),
    ("u_t", (128, 4)), ("gnw_rep", (128, D)), ("gnb_rep", (128, D)),
    ("pmask", (128, 4, CSUB)), ("ident", (128, 128)),
    ("w_routert", (D, 8)), ("b2ext", (NE, D)),
    ("w1p", (NE, 32, D, 128)), ("w2p", (NE, DFF, D)), ("b1t", (128, NE, 32)),
]


def build(nc, dbg=()):
    p = P(nc)
    da = {}
    da["x"] = nc.dram_tensor("x", [T, D], F32, kind="ExternalInput").ap()
    for name, shape in DRAM_SPECS:
        da[name] = nc.dram_tensor(name, list(shape), F32, kind="ExternalInput").ap()
    out_ap = nc.dram_tensor("out", [T, D], F32, kind="ExternalOutput").ap()
    s_scr = nc.dram_tensor("s_scr", [T, D], F32, kind="Internal").ap()
    w1s = nc.dram_tensor("w1s", [NE, 32, D, 128], BF16, kind="Internal").ap()
    w2s = nc.dram_tensor("w2s", [NE, DFF, D], BF16, kind="Internal").ap()
    dbg_out = {}
    for name, shape in dbg:
        dbg_out[name] = nc.dram_tensor("dbg_" + name, list(shape), F32,
                                       kind="ExternalOutput").ap()

    with ExitStack() as st:
        a16h = st.enter_context(nc.sbuf_tensor("arena16", [128, A16_N], BF16))
        a32h = st.enter_context(nc.sbuf_tensor("arena32", [128, A32_N], F32))
        A16 = Arena(a16h, A16_N)
        A32 = Arena(a32h, A32_N)
        ps = [st.enter_context(nc.psum_tensor(f"ps{i}", [128, 1024], F32))
              for i in range(4)]
        psb = [Buf(f"ps{i}h{j}") for i in range(4) for j in range(2)]
        p.stack = st
        for e in ENGS:
            p.sems[e] = st.enter_context(nc.semaphore("sem_" + e))
        for e in range(NE):
            p.sems[f"cast{e}"] = st.enter_context(nc.semaphore(f"sem_cast{e}"))
            p.cnt[f"cast{e}"] = 0

        env = dict(da=da, out=out_ap, s_scr=s_scr, s_scr_buf=Buf("s_scr"),
                   w1s=w1s, w2s=w2s,
                   dbg=dbg_out, ps=ps, psb=psb, A16=A16, A32=A32)
        ph_prologue(p, env)
        ph_A(p, env)
        ph_B(p, env)
        ph_B2(p, env)
        ph_ffn(p, env)

        with nc.Block() as blk:
            for e in ENGS:
                def runner(E, e=e):
                    for fn in p.prog[e]:
                        fn(E)
                getattr(blk, e)(runner)
    return nc


def bankof(env, i, half):
    if half is None:
        return env["ps"][i][:, :], [env["psb"][2 * i], env["psb"][2 * i + 1]]
    return (env["ps"][i][:, half * 512:(half + 1) * 512],
            [env["psb"][2 * i + half]])


def dump16(p, env, view, name):
    if name not in env["dbg"]:
        return
    p.op("gpsimd", lambda E: E.dma_start(
        out=env["dbg"][name][:, :], in_=view.ap().rearrange(
            "p " + " ".join(f"d{i}" for i in range(len(view.dims))) +
            " -> p (" + " ".join(f"d{i}" for i in range(len(view.dims))) + ")")
        if len(view.dims) > 1 else view.ap()),
        reads=[view.buf], dsem="b_dbg")


def dump32(p, env, view, name):
    if name not in env["dbg"]:
        return
    p.op("sync", lambda E: E.dma_start(
        out=env["dbg"][name][:, :], in_=view.ap().rearrange(
            "p " + " ".join(f"d{i}" for i in range(len(view.dims))) +
            " -> p (" + " ".join(f"d{i}" for i in range(len(view.dims))) + ")")
        if len(view.dims) > 1 else view.ap()),
        reads=[view.buf], dsem="b_dbg")


def ph_prologue(p, env):
    op, da, A16, A32, ps, psb = p.op, env["da"], env["A16"], env["A32"], env["ps"], env["psb"]

    xnT = A16.alloc((NC_, T), "xnT")
    onecol = A16.alloc((8,), "onecol")
    w_x2s = A16.alloc((5, D), "w_x2s")
    zTb = A16.alloc((2, T), "zTb")
    musb = A16.alloc((512,), "musb")
    musb2 = A16.alloc((512,), "musb2")
    ident = A32.alloc((128,), "ident")
    masks = A32.alloc((4, CSUB), "pmask")
    small32 = A32.alloc((64,), "small32")
    xbias = A32.alloc((5 * NC_,), "xbias")
    ones32 = A32.alloc((128,), "ones32")
    epsb = A32.alloc((2,), "epsb")
    env.update(xnT=xnT, onecol=onecol, w_x2s=w_x2s, zTb=zTb, musb=musb,
               musb2=musb2, ident=ident, masks=masks, small32=small32,
               xbias=xbias, ones32=ones32, epsb=epsb)
    env["MU_X"] = lambda c: small32[:, c:c + 1]
    env["BW"] = lambda kb: small32[:, 8 + kb:9 + kb]
    env["UU"] = lambda h: small32[:, 12 + h:13 + h]

    op("sync", lambda E: E.dma_start(out=ident.ap(), in_=da["ident"][:, :]),
       writes=[ident.buf])
    op("sync", lambda E: E.dma_start(out=masks.ap().rearrange("p a b -> p (a b)"),
                                     in_=da["pmask"][:, :, :].rearrange("p a b -> p (a b)")),
       writes=[masks.buf])
    op("sync", lambda E: E.dma_start(out=small32[:, 0:8], in_=da["mu_x_t"][:, :]),
       writes=[small32.buf])
    op("sync", lambda E: E.dma_start(out=small32[:, 8:12], in_=da["bw_t"][:, :]),
       writes=[small32.buf])
    op("sync", lambda E: E.dma_start(out=small32[:, 12:16], in_=da["u_t"][:, :]),
       writes=[small32.buf])
    op("sync", lambda E: E.dma_start(out=xbias.ap().rearrange("p (a b) -> p a b", a=5),
                                     in_=da["xbias_t"][:, :, :]),
       writes=[xbias.buf])
    op("vector", lambda E: E.memset(ones32.ap(), 1.0), writes=[ones32.buf])
    op("vector", lambda E: E.memset(epsb.ap(), EPS), writes=[epsb.buf])
    op("vector", lambda E: E.memset(onecol.ap(), 1.0), writes=[onecol.buf])
    op("gpsimd", lambda E: E.dma_start(
        out=w_x2s[0:32, :, :], in_=da["w_x2s"][:, :, :]), writes=[w_x2s.buf])

    xtile = [A32.alloc((D,), f"xtile{j}") for j in range(2)]
    xcn = A32.alloc((D,), "xcn")
    stat = A32.alloc((8,), "lnstat")
    sqv = A32.alloc((D,), "lnsq")
    for i in range(NT):
        xt = xtile[i % 2]
        op("sync", lambda E, i=i, xt=xt: E.dma_start(
            out=xt.ap(), in_=da["x"][i * 128:(i + 1) * 128, :]), writes=[xt.buf])
        op("vector", lambda E, xt=xt: E.reduce_sum(stat[:, 0:1], xt.ap(), axis=AX.X),
           reads=[xt.buf], writes=[stat.buf])
        op("vector", lambda E: E.tensor_scalar_mul(stat[:, 1:2], stat[:, 0:1], 1.0 / D),
           reads=[stat.buf], writes=[stat.buf])
        op("vector", lambda E, xt=xt: E.tensor_scalar(
            xcn.ap(), xt.ap(), stat[:, 1:2], None, ALU.subtract),
           reads=[xt.buf, stat.buf], writes=[xcn.buf])
        op("scalar", lambda E: E.activation(sqv.ap(), xcn.ap(), AF.Square),
           reads=[xcn.buf], writes=[sqv.buf])
        op("vector", lambda E: E.reduce_sum(stat[:, 2:3], sqv.ap(), axis=AX.X),
           reads=[sqv.buf], writes=[stat.buf])
        op("vector", lambda E: E.tensor_scalar_mul(stat[:, 3:4], stat[:, 2:3], 1.0 / D),
           reads=[stat.buf], writes=[stat.buf])
        op("scalar", lambda E: E.activation(stat[:, 4:5], stat[:, 3:4], AF.Ln, bias=epsb[:, 0:1]),
           reads=[stat.buf, epsb.buf], writes=[stat.buf])
        op("scalar", lambda E: E.activation(stat[:, 5:6], stat[:, 4:5], AF.Exp, scale=-0.5),
           reads=[stat.buf], writes=[stat.buf])
        op("vector", lambda E: E.tensor_scalar(xcn.ap(), xcn.ap(), stat[:, 5:6], None, ALU.mult),
           reads=[stat.buf, xcn.buf], writes=[xcn.buf])
        tp = ps[3][:, 0:128]
        for c in range(NC_):
            op("tensor", lambda E, c=c, tp=tp: E.transpose(
                tp, xcn[:, c * 128:(c + 1) * 128], ident.ap()),
               reads=[xcn.buf, ident.buf], writes=[psb[6]], war=[psb[6]])
            op("vector", lambda E, c=c, i=i, tp=tp: E.tensor_copy(
                xnT[:, c, i * 128:(i + 1) * 128], tp),
               reads=[psb[6]], writes=[xnT.buf])
    p.barrier()
    A32.free(xtile[0], xtile[1], xcn, stat, sqv)
    dump16(p, env, xnT, "xnT")


def emit_casts(p, env, experts):
    sems = p.sems
    for e in experts:
        def c1(E, e=e):
            E.dma_start(out=env["w1s"][e], in_=env["da"]["w1p"][e]).then_inc(
                sems[f"cast{e}"], 16)
        def c2(E, e=e):
            E.dma_start(out=env["w2s"][e], in_=env["da"]["w2p"][e]).then_inc(
                sems[f"cast{e}"], 16)
        p.prog["gpsimd"].append(c1)
        p.prog["gpsimd"].append(c2)


def emit_delta(p, env, dbuf, t0, width):
    op, xnT = p.op, env["xnT"]
    for c in range(NC_):
        if t0 == 0:
            op("vector", lambda E, c=c: E.tensor_scalar_mul(
                dbuf[:, c, 0:1], xnT[:, c, 0:1], -1.0),
               reads=[xnT.buf], writes=[dbuf.buf])
            op("vector", lambda E, c=c, w=width: E.tensor_sub(
                dbuf[:, c, 1:w], xnT[:, c, 0:w - 1], xnT[:, c, 1:w]),
               reads=[xnT.buf], writes=[dbuf.buf])
        else:
            op("vector", lambda E, c=c, t0=t0, w=width: E.tensor_sub(
                dbuf[:, c, 0:w], xnT[:, c, t0 - 1:t0 + w - 1], xnT[:, c, t0:t0 + w]),
               reads=[xnT.buf], writes=[dbuf.buf])


def emit_xmix(p, env, zview, zrow, slot, dbuf, xmix, t0, width, pbanks):
    op, xnT, xbias = p.op, env["xnT"], env["xbias"]
    musb2 = (env["musb"], env["musb2"])
    for c in range(NC_):
        bap, bbs = pbanks[c % 2]
        mb = musb2[c % 2]
        op("tensor", lambda E, c=c, bap=bap, zrow=zrow, slot=slot, t0=t0, w=width: E.matmul(
            bap[:, 0:w], env["w_x2s"][0:32, slot, c * 128:(c + 1) * 128],
            zview[0:32, zrow, t0:t0 + w], start=True, stop=True),
           reads=[zview.buf, env["w_x2s"].buf], writes=bbs, war=bbs)
        op("vector", lambda E, c=c, bap=bap, slot=slot, w=width, mb=mb: E.scalar_tensor_tensor(
            mb[:, 0:w], bap[:, 0:w], xbias[:, slot * NC_ + c:slot * NC_ + c + 1],
            dbuf[:, c, 0:w], ALU.add, ALU.mult),
           reads=bbs + [xbias.buf, dbuf.buf], writes=[mb.buf])
        op("vector", lambda E, c=c, t0=t0, w=width, mb=mb: E.tensor_add(
            xmix[:, c, 0:w], mb[:, 0:w], xnT[:, c, t0:t0 + w]),
           reads=[mb.buf, xnT.buf], writes=[xmix.buf])


def emit_proj(p, env, xmix, wview, width, nkb, pbank_fn, evac_fn):
    op = p.op
    for kb in range(nkb):
        bap, bbs = pbank_fn(kb)
        for c in range(NC_):
            op("tensor", lambda E, kb=kb, c=c, bap=bap, w=width: E.matmul(
                bap, wview[:, c, kb * 128:(kb + 1) * 128], xmix[:, c, 0:w],
                start=(c == 0), stop=(c == NC_ - 1)),
               reads=([wview.buf, xmix.buf] if c in (0, NC_ - 1) else []),
               writes=bbs if c == NC_ - 1 else [],
               war=bbs if c == 0 else [],
               inc=(c == NC_ - 1))
        evac_fn(kb, bap, bbs)


def ph_A(p, env):
    op, da, A16, A32 = p.op, env["da"], env["A16"], env["A32"]
    ps, psb, xnT = env["ps"], env["psb"], env["xnT"]

    rT = A16.alloc((4, T), "rT")
    kT = A16.alloc((4, T), "kT")
    zTa = A16.alloc((3, T), "zTa")
    dbufA = A16.alloc((NC_, 512), "dbufA")
    xmixA = A16.alloc((NC_, 512), "xmixA")
    t1w = A16.alloc((512,), "t1w")
    w_x1t = A16.alloc((NC_, 160), "w_x1t")
    w_rt = A16.alloc((NC_, KD), "w_rt")
    w_kt = A16.alloc((NC_, KD), "w_kt")
    a_wt = A16.alloc((NC_, 64), "a_wt")
    b_wt = A16.alloc((KD,), "b_wt")
    ew = A32.alloc((4, T), "ew")
    env.update(rT=rT, kT=kT, ew=ew)
    zTb, musb, xbias = env["zTb"], env["musb"], env["xbias"]

    for name, view in [("w_x1t", w_x1t), ("w_rt", w_rt), ("w_kt", w_kt),
                       ("a_wt", a_wt)]:
        op("gpsimd", lambda E, name=name, view=view: E.dma_start(
            out=view.ap(), in_=da[name].rearrange("(c p) m -> p c m", p=128)),
           writes=[view.buf])
    op("gpsimd", lambda E: E.dma_start(out=b_wt[0:64, :], in_=da["b_wt"][:, :]),
       writes=[b_wt.buf])

    for ts in range(4):
        t0 = ts * 512
        emit_delta(p, env, dbufA, t0, 512)
        for c in range(NC_):
            op("vector", lambda E, c=c, t0=t0: E.scalar_tensor_tensor(
                xmixA[:, c, :], dbufA[:, c, :], env["MU_X"](c),
                xnT[:, c, t0:t0 + 512], ALU.mult, ALU.add),
               reads=[dbufA.buf, xnT.buf], writes=[xmixA.buf])
        zbank = [bankof(env, 0, 0), bankof(env, 0, 1), bankof(env, 1, 0),
                 bankof(env, 1, 1), bankof(env, 2, 0)]
        for n in range(5):
            bap, bbs = zbank[n]
            for c in range(NC_):
                op("tensor", lambda E, n=n, c=c, bap=bap: E.matmul(
                    bap[0:32, :], w_x1t[:, c, n * 32:(n + 1) * 32], xmixA[:, c, :],
                    start=(c == 0), stop=(c == NC_ - 1)),
                   reads=([w_x1t.buf, xmixA.buf] if c in (0, NC_ - 1) else []),
                   writes=bbs if c == NC_ - 1 else [],
                   war=bbs if c == 0 else [], inc=(c == NC_ - 1))
        for n in range(5):
            bap, bbs = zbank[n]
            tgt = (zTa, n) if n < 3 else (zTb, n - 3)
            op("scalar", lambda E, bap=bap, tgt=tgt, t0=t0: E.activation(
                tgt[0][0:32, tgt[1], t0:t0 + 512], bap[0:32, :], AF.Tanh),
               reads=bbs, writes=[tgt[0].buf])

        emit_xmix(p, env, zTa, 0, 0, dbufA, xmixA, t0, 512,
                  (bankof(env, 2, 1), bankof(env, 2, 0)))
        def evac_r(kb, bap, bbs, t0=t0):
            op("vector", lambda E, kb=kb, bap=bap: E.tensor_copy(
                rT[:, kb, t0:t0 + 512], bap),
               reads=bbs, writes=[rT.buf])
        emit_proj(p, env, xmixA, w_rt, 512, 4, lambda kb: bankof(env, 3, kb % 2), evac_r)

        emit_xmix(p, env, zTa, 1, 1, dbufA, xmixA, t0, 512,
                  (bankof(env, 2, 1), bankof(env, 2, 0)))
        bap, bbs = bankof(env, 0, 0)
        for c in range(NC_):
            op("tensor", lambda E, c=c, bap=bap: E.matmul(
                bap[0:64, :], a_wt[:, c, :], xmixA[:, c, :],
                start=(c == 0), stop=(c == NC_ - 1)),
               reads=([a_wt.buf, xmixA.buf] if c in (0, NC_ - 1) else []),
               writes=bbs if c == NC_ - 1 else [],
               war=bbs if c == 0 else [], inc=(c == NC_ - 1))
        op("scalar", lambda E, bap=bap: E.activation(t1w[0:64, :], bap[0:64, :], AF.Tanh),
           reads=bbs, writes=[t1w.buf])
        for kb in range(4):
            bap2, bbs2 = bankof(env, 1, kb % 2)
            op("tensor", lambda E, kb=kb, bap2=bap2: E.matmul(
                bap2, b_wt[0:64, kb * 128:(kb + 1) * 128], t1w[0:64, :],
                start=True, stop=True),
               reads=[t1w.buf, b_wt.buf], writes=bbs2, war=bbs2)
            op("scalar", lambda E, kb=kb, bap2=bap2, t0=t0: E.activation(
                ew[:, kb, t0:t0 + 512], bap2, AF.Exp, bias=env["BW"](kb)),
               reads=bbs2, writes=[ew.buf])

        emit_xmix(p, env, zTa, 2, 2, dbufA, xmixA, t0, 512,
                  (bankof(env, 2, 1), bankof(env, 2, 0)))
        def evac_k(kb, bap, bbs, t0=t0):
            op("vector", lambda E, kb=kb, bap=bap: E.tensor_copy(
                kT[:, kb, t0:t0 + 512], bap),
               reads=bbs, writes=[kT.buf])
        emit_proj(p, env, xmixA, w_kt, 512, 4, lambda kb: bankof(env, 3, kb % 2), evac_k)

    p.barrier()
    A16.free(zTa, dbufA, xmixA, t1w, w_x1t, w_rt, w_kt, a_wt, b_wt)
    dump16(p, env, rT, "rT")
    dump16(p, env, kT, "kT")
    dump32(p, env, ew, "ew")
    dump16(p, env, zTb, "zTb")


def ph_B(p, env):
    op, da, A16, A32 = p.op, env["da"], env["A16"], env["A32"]
    ps, psb, xnT = env["ps"], env["psb"], env["xnT"]
    rT, kT, ew, zTb, masks, ident = env["rT"], env["kT"], env["ew"], env["zTb"], env["masks"], env["ident"]

    w_vt = A16.alloc((NC_, VD), "w_vt")
    o_bf = A16.alloc((NT, VD), "o_bf")
    Vc = A16.alloc((VD,), "Vc")
    S_bf = [A16.alloc((VH,), f"S_bf{h}") for h in range(HH)]
    dbufB = A16.alloc((NC_, 128), "dbufB")
    xmixB = A16.alloc((NC_, 128), "xmixB")
    eqv = [A16.alloc((C,), f"eqv{j}") for j in range(2)]
    qbf = [A16.alloc((C,), f"qbf{j}") for j in range(2)]
    kbf = [A16.alloc((C,), f"kbf{j}") for j in range(2)]
    Pfull = [A16.alloc((C,), f"Pfull{j}") for j in range(2)]
    khT = [A16.alloc((C,), f"khT{j}") for j in range(2)]
    S_fp = [A32.alloc((VH,), f"S_fp{h}") for h in range(HH)]
    negbuf = [A32.alloc((130,), f"negbuf{j}") for j in range(2)]
    negneg = [A32.alloc((C,), f"negneg{j}") for j in range(2)]
    rku = [A16.alloc((C,), f"rku{j}") for j in range(2)]
    dcol = [A32.alloc((2,), f"dcol{j}") for j in range(2)]
    khat = [A32.alloc((C,), f"khat{j}") for j in range(2)]
    env.update(o_bf=o_bf, dbufB=dbufB, xmixB=xmixB)

    op("gpsimd", lambda E: E.dma_start(
        out=w_vt.ap(), in_=da["w_vt"].rearrange("(c p) m -> p c m", p=128)),
       writes=[w_vt.buf])
    emit_casts(p, env, range(0, 5))
    for h in range(HH):
        op("vector", lambda E, h=h: E.memset(S_fp[h].ap(), 0.0), writes=[S_fp[h].buf])
        op("vector", lambda E, h=h: E.memset(S_bf[h].ap(), 0.0), writes=[S_bf[h].buf])
    for j in range(2):
        op("vector", lambda E, j=j: E.memset(negbuf[j][:, 0:2], 0.0),
           writes=[negbuf[j].buf])
        op("vector", lambda E, j=j: E.memset(Pfull[j].ap(), 0.0),
           writes=[Pfull[j].buf])

    for ci in range(NCH):
        t0 = ci * C
        emit_delta(p, env, dbufB, t0, 128)
        emit_xmix(p, env, zTb, 0, 3, dbufB, xmixB, t0, 128,
                  (bankof(env, 2, 1), bankof(env, 2, 0)))
        for vh in range(2):
            bap, bbs = bankof(env, 0, vh)
            for c in range(NC_):
                op("tensor", lambda E, vh=vh, c=c, bap=bap: E.matmul(
                    bap, xmixB[:, c, :], w_vt[:, c, vh * 512:(vh + 1) * 512],
                    start=(c == 0), stop=(c == NC_ - 1)),
                   reads=([xmixB.buf, w_vt.buf] if c in (0, NC_ - 1) else []),
                   writes=bbs if c == NC_ - 1 else [],
                   war=bbs if c == 0 else [], inc=(c == NC_ - 1))
            op("vector", lambda E, vh=vh, bap=bap: E.tensor_copy(
                Vc[:, vh * 512:(vh + 1) * 512], bap),
               reads=bbs, writes=[Vc.buf])

        for h in range(HH):
            j = h % 2
            if j == 0:
                o_ap, o_bb = ps[1][:, 0:256], [psb[2]]
                sd_ap, d_ap, sd_bb = ps[3][:, 0:256], ps[3][:, 256:257], [psb[6]]
                tp_ap, tp_bb = ps[3][:, 640:768], [psb[7]]
            else:
                o_ap, o_bb = ps[1][:, 512:768], [psb[3]]
                sd_ap, d_ap, sd_bb = ps[0][:, 0:256], ps[0][:, 256:257], [psb[0]]
                tp_ap, tp_bb = ps[0][:, 640:768], [psb[1]]
            nbb = negbuf[j]
            nng = negneg[j]
            NB = lambda a, b, nbb=nbb: nbb[:, 2 + a:2 + b]
            NBP = lambda a, b, nbb=nbb: nbb[:, 1 + a:1 + b]
            eq, qb, kb, Pf, kT_, rk, dc, kh = (eqv[j], qbf[j], kbf[j], Pfull[j],
                                               khT[j], rku[j], dcol[j], khat[j])
            vslice = Vc[:, h * VH:(h + 1) * VH]
            op("vector", lambda E, h=h, t0=t0, NB=NB: E.tensor_tensor_scan(
                NB(0, 128), env["ones32"][:, 0:128], ew[:, h, t0:t0 + 128], 0.0,
                ALU.mult, ALU.add),
               reads=[ew.buf, env["ones32"].buf], writes=[nbb.buf])
            op("vector", lambda E, nng=nng, NB=NB: E.tensor_scalar_mul(nng.ap(), NB(0, 128), -1.0),
               reads=[nbb.buf], writes=[nng.buf])
            op("scalar", lambda E, eq=eq, NBP=NBP: E.activation(eq[:, 0:128], NBP(0, 128), AF.Exp, scale=-1.0),
               reads=[nbb.buf], writes=[eq.buf])
            op("vector", lambda E, h=h, t0=t0, qb=qb, eq=eq: E.tensor_mul(
                qb[:, 0:128], rT[:, h, t0:t0 + 128], eq[:, 0:128]),
               reads=[rT.buf, eq.buf], writes=[qb.buf])
            op("tensor", lambda E, h=h, o_ap=o_ap, qb=qb: E.matmul(
                o_ap, qb[:, 0:128], S_bf[h].ap(), start=True, stop=False),
               reads=[qb.buf, S_bf[h].buf], war=o_bb, inc=False)
            op("vector", lambda E, h=h, t0=t0, rk=rk: E.tensor_mul(
                rk[:, 0:128], rT[:, h, t0:t0 + 128], kT[:, h, t0:t0 + 128]),
               reads=[rT.buf, kT.buf], writes=[rk.buf])
            op("vector", lambda E, h=h, rk=rk: E.tensor_scalar(
                rk[:, 0:128], rk[:, 0:128], env["UU"](h), None, ALU.mult),
               reads=[rk.buf], writes=[rk.buf])
            op("tensor", lambda E, d_ap=d_ap, rk=rk: E.matmul(
                d_ap, rk[:, 0:128], env["onecol"][:, 0:1],
                start=True, stop=True),
               reads=[rk.buf, env["onecol"].buf], writes=sd_bb, war=sd_bb)
            op("vector", lambda E, dc=dc, d_ap=d_ap: E.tensor_copy(dc[:, 0:1], d_ap),
               reads=sd_bb, writes=[dc.buf])
            for i in range(4):
                sw = (i + 1) * CSUB
                pm_ap, pm_bb = ((ps[2][:, 0:CSUB], [psb[4]]) if i % 2 == 0
                                else (ps[2][:, 512:512 + CSUB], [psb[5]]))
                mid = i * CSUB + CSUB // 2 - 1
                op("scalar", lambda E, i=i, mid=mid, eq=eq, NBP=NBP, NB=NB: E.activation(
                    eq[:, 0:CSUB], NBP(i * CSUB, (i + 1) * CSUB), AF.Exp,
                    scale=-1.0, bias=NB(mid, mid + 1)),
                   reads=[nbb.buf], writes=[eq.buf])
                op("vector", lambda E, i=i, h=h, t0=t0, qb=qb, eq=eq: E.tensor_mul(
                    qb[:, 0:CSUB], rT[:, h, t0 + i * CSUB:t0 + (i + 1) * CSUB],
                    eq[:, 0:CSUB]),
                   reads=[rT.buf, eq.buf], writes=[qb.buf])
                op("scalar", lambda E, sw=sw, mid=mid, eq=eq, NB=NB, nng=nng: E.activation(
                    eq[:, 0:sw], NB(0, sw), AF.Exp, bias=nng[:, mid:mid + 1]),
                   reads=[nbb.buf, nng.buf], writes=[eq.buf])
                op("vector", lambda E, h=h, t0=t0, sw=sw, kb=kb, eq=eq: E.tensor_mul(
                    kb[:, 0:sw], kT[:, h, t0:t0 + sw], eq[:, 0:sw]),
                   reads=[kT.buf, eq.buf], writes=[kb.buf])
                op("tensor", lambda E, sw=sw, pm_ap=pm_ap, kb=kb, qb=qb: E.matmul(
                    pm_ap[0:sw, :], kb[:, 0:sw], qb[:, 0:CSUB],
                    start=True, stop=True),
                   reads=[kb.buf, qb.buf], writes=pm_bb, war=pm_bb)
                op("vector", lambda E, i=i, sw=sw, pm_ap=pm_ap: E.tensor_mul(
                    pm_ap[0:sw, :], pm_ap[0:sw, :], masks[0:sw, i, :]),
                   reads=pm_bb + [masks.buf], writes=pm_bb)
                op("vector", lambda E, i=i, sw=sw, pm_ap=pm_ap, Pf=Pf: E.tensor_copy(
                    Pf[0:sw, i * CSUB:(i + 1) * CSUB], pm_ap[0:sw, :]),
                   reads=pm_bb, writes=[Pf.buf])
            op("tensor", lambda E, o_ap=o_ap, vslice=vslice, Pf=Pf: E.matmul(
                o_ap, Pf.ap(), vslice, start=False, stop=True),
               reads=[Pf.buf, Vc.buf], writes=o_bb, inc=True)
            op("vector", lambda E, h=h, ci=ci, vslice=vslice, o_ap=o_ap, dc=dc: E.scalar_tensor_tensor(
                o_bf[:, ci, h * VH:(h + 1) * VH], vslice, dc[:, 0:1], o_ap,
                ALU.mult, ALU.add),
               reads=[Vc.buf, dc.buf] + o_bb, writes=[o_bf.buf])
            op("scalar", lambda E, eq=eq, NB=NB, nng=nng: E.activation(
                eq[:, 0:128], NB(0, 128), AF.Exp, bias=nng[:, 127:128]),
               reads=[nbb.buf, nng.buf], writes=[eq.buf])
            op("vector", lambda E, h=h, t0=t0, kh=kh, eq=eq: E.tensor_mul(
                kh.ap(), kT[:, h, t0:t0 + 128], eq[:, 0:128]),
               reads=[kT.buf, eq.buf], writes=[kh.buf])
            op("tensor", lambda E, tp_ap=tp_ap, kh=kh: E.transpose(tp_ap, kh.ap(), ident.ap()),
               reads=[kh.buf, ident.buf], writes=tp_bb, war=tp_bb)
            op("vector", lambda E, tp_ap=tp_ap, kT_=kT_: E.tensor_copy(kT_.ap(), tp_ap),
               reads=tp_bb, writes=[kT_.buf])
            op("tensor", lambda E, vslice=vslice, sd_ap=sd_ap, kT_=kT_: E.matmul(
                sd_ap, kT_.ap(), vslice, start=True, stop=True),
               reads=[kT_.buf, Vc.buf], writes=sd_bb, war=sd_bb)
            op("scalar", lambda E, dc=dc, NB=NB: E.activation(
                dc[:, 1:2], NB(127, 128), AF.Exp, scale=-1.0),
               reads=[nbb.buf], writes=[dc.buf])
            op("vector", lambda E, h=h, dc=dc, sd_ap=sd_ap: E.scalar_tensor_tensor(
                S_fp[h].ap(), S_fp[h].ap(), dc[:, 1:2], sd_ap,
                ALU.mult, ALU.add),
               reads=[S_fp[h].buf, dc.buf] + sd_bb, writes=[S_fp[h].buf])
            op("vector", lambda E, h=h: E.tensor_copy(S_bf[h].ap(), S_fp[h].ap()),
               reads=[S_fp[h].buf], writes=[S_bf[h].buf])
    p.barrier()
    A16.free(w_vt, Vc, *S_bf, *eqv, *qbf, *kbf, *Pfull, *khT, *rku,
             env["rT"], env["kT"])
    A32.free(*S_fp, *negbuf, *negneg, *dcol, *khat, env["ew"])
    dump16(p, env, o_bf, "o_bf")


def ph_B2(p, env):
    op, da, A16, A32 = p.op, env["da"], env["A16"], env["A32"]
    ps, psb, xnT = env["ps"], env["psb"], env["xnT"]
    o_bf, zTb, dbufB, xmixB = env["o_bf"], env["zTb"], env["dbufB"], env["xmixB"]

    w_gt = A16.alloc((NC_, VD), "w_gt")
    w_ot = A16.alloc((NC_, D), "w_ot")
    gnw = A16.alloc((D,), "gnw16")
    gnb = A16.alloc((D,), "gnb16")
    gsil = [A16.alloc((VD,), f"gsil{j}") for j in range(2)]
    o_n = [A16.alloc((VD,), f"o_n{j}") for j in range(2)]
    ogT = [A16.alloc((NC_, 128), f"ogT{j}") for j in range(2)]
    stats = A32.alloc((NT, HH, 2), "gnstats")
    sq2 = [A32.alloc((VH,), f"gnsq{j}") for j in range(2)]
    st82 = [A32.alloc((8,), f"gnst8{j}") for j in range(2)]
    og = [A32.alloc((VD,), f"og{j}") for j in range(2)]
    s_sb = [A32.alloc((D,), f"s_sb{j}") for j in range(2)]

    op("gpsimd", lambda E: E.dma_start(
        out=w_gt.ap(), in_=da["w_gt"].rearrange("(c p) m -> p c m", p=128)),
       writes=[w_gt.buf])
    op("gpsimd", lambda E: E.dma_start(
        out=w_ot.ap(), in_=da["w_ot"].rearrange("(c p) m -> p c m", p=128)),
       writes=[w_ot.buf])
    op("gpsimd", lambda E: E.dma_start(out=gnw.ap(), in_=da["gnw_rep"][:, :]),
       writes=[gnw.buf])
    op("gpsimd", lambda E: E.dma_start(out=gnb.ap(), in_=da["gnb_rep"][:, :]),
       writes=[gnb.buf])
    emit_casts(p, env, range(5, NE))

    for i in range(NT):
        for h in range(HH):
            osl = o_bf[:, i, h * VH:(h + 1) * VH]
            sq = sq2[(i * HH + h) % 2]
            st8 = st82[(i * HH + h) % 2]
            op("scalar", lambda E, osl=osl, sq=sq, st8=st8: E.activation(
                sq.ap(), osl, AF.Identity, accum_out=st8[:, 0:1]),
               reads=[o_bf.buf], writes=[sq.buf, st8.buf])
            op("scalar", lambda E, osl=osl, sq=sq, st8=st8: E.activation(
                sq.ap(), osl, AF.Square, accum_out=st8[:, 1:2]),
               reads=[o_bf.buf], writes=[sq.buf, st8.buf])
            op("vector", lambda E, i=i, h=h, st8=st8: E.tensor_scalar_mul(
                stats[:, i, h, 0:1], st8[:, 0:1], 1.0 / VH),
               reads=[st8.buf], writes=[stats.buf])
            op("vector", lambda E, st8=st8: E.tensor_scalar_mul(st8[:, 2:3], st8[:, 1:2], 1.0 / VH),
               reads=[st8.buf], writes=[st8.buf])
            op("vector", lambda E, i=i, h=h, st8=st8: E.tensor_mul(
                st8[:, 3:4], stats[:, i, h, 0:1], stats[:, i, h, 0:1]),
               reads=[stats.buf], writes=[st8.buf])
            op("vector", lambda E, st8=st8: E.tensor_sub(st8[:, 4:5], st8[:, 2:3], st8[:, 3:4]),
               reads=[st8.buf], writes=[st8.buf])
            op("scalar", lambda E, st8=st8: E.activation(st8[:, 5:6], st8[:, 4:5], AF.Ln, bias=env["epsb"][:, 0:1]),
               reads=[st8.buf, env["epsb"].buf], writes=[st8.buf])
            op("scalar", lambda E, i=i, h=h, st8=st8: E.activation(
                stats[:, i, h, 1:2], st8[:, 5:6], AF.Exp, scale=-0.5),
               reads=[st8.buf], writes=[stats.buf])

    for i in range(NT):
        t0 = i * 128
        pj = i % 2
        gs, on_, ogj, ogTj, ssb = gsil[pj], o_n[pj], og[pj], ogT[pj], s_sb[pj]
        emit_delta(p, env, dbufB, t0, 128)
        emit_xmix(p, env, zTb, 1, 4, dbufB, xmixB, t0, 128,
                  (bankof(env, 2, 1), bankof(env, 2, 0)))
        for vh in range(2):
            bap, bbs = bankof(env, 0, vh)
            for c in range(NC_):
                op("tensor", lambda E, vh=vh, c=c, bap=bap: E.matmul(
                    bap, xmixB[:, c, :], w_gt[:, c, vh * 512:(vh + 1) * 512],
                    start=(c == 0), stop=(c == NC_ - 1)),
                   reads=([xmixB.buf, w_gt.buf] if c in (0, NC_ - 1) else []),
                   writes=bbs if c == NC_ - 1 else [],
                   war=bbs if c == 0 else [], inc=(c == NC_ - 1))
            op("scalar", lambda E, vh=vh, bap=bap, gs=gs: E.activation(
                gs[:, vh * 512:(vh + 1) * 512], bap, AF.Silu),
               reads=bbs, writes=[gs.buf])
        for h in range(HH):
            op("vector", lambda E, i=i, h=h, on_=on_: E.tensor_scalar(
                on_[:, h * VH:(h + 1) * VH], o_bf[:, i, h * VH:(h + 1) * VH],
                stats[:, i, h, 0:1], stats[:, i, h, 1:2], ALU.subtract, ALU.mult),
               reads=[o_bf.buf, stats.buf], writes=[on_.buf])
        op("vector", lambda E, on_=on_: E.tensor_mul(on_.ap(), on_.ap(), gnw.ap()),
           reads=[gnw.buf, on_.buf], writes=[on_.buf])
        op("vector", lambda E, on_=on_: E.tensor_add(on_.ap(), on_.ap(), gnb.ap()),
           reads=[gnb.buf, on_.buf], writes=[on_.buf])
        op("vector", lambda E, on_=on_, gs=gs, ogj=ogj: E.tensor_mul(ogj.ap(), on_.ap(), gs.ap()),
           reads=[gs.buf, on_.buf], writes=[ogj.buf])
        tp_ap, tp_bb = (ps[3][:, 640:768], [psb[7]]) if pj == 0 else (ps[0][:, 640:768], [psb[1]])
        for c in range(NC_):
            op("tensor", lambda E, c=c, tp_ap=tp_ap, ogj=ogj: E.transpose(
                tp_ap, ogj[:, c * 128:(c + 1) * 128], env["ident"].ap()),
               reads=[ogj.buf, env["ident"].buf], writes=tp_bb, war=tp_bb)
            op("vector", lambda E, c=c, tp_ap=tp_ap, ogTj=ogTj: E.tensor_copy(ogTj[:, c, :], tp_ap),
               reads=tp_bb, writes=[ogTj.buf])
        for oh in range(2):
            bap, bbs = bankof(env, 1, oh) if pj == 0 else bankof(env, 2, oh)
            for c in range(NC_):
                op("tensor", lambda E, oh=oh, c=c, bap=bap, ogTj=ogTj: E.matmul(
                    bap, ogTj[:, c, :], w_ot[:, c, oh * 512:(oh + 1) * 512],
                    start=(c == 0), stop=(c == NC_ - 1)),
                   reads=([ogTj.buf, w_ot.buf] if c in (0, NC_ - 1) else []),
                   writes=bbs if c == NC_ - 1 else [],
                   war=bbs if c == 0 else [], inc=(c == NC_ - 1))
            op("vector", lambda E, oh=oh, bap=bap, ssb=ssb: E.tensor_copy(
                ssb[:, oh * 512:(oh + 1) * 512], bap),
               reads=bbs, writes=[ssb.buf])
        op("sync", lambda E, i=i, ssb=ssb: E.dma_start(
            out=env["s_scr"][i * 128:(i + 1) * 128, :], in_=ssb.ap()),
           reads=[ssb.buf], writes=[env["s_scr_buf"]])
        if "s" in env["dbg"]:
            op("sync", lambda E, i=i, ssb=ssb: E.dma_start(
                out=env["dbg"]["s"][i * 128:(i + 1) * 128, :], in_=ssb.ap()),
               reads=[ssb.buf], dsem="b_dbg")
    p.barrier()
    A16.free(w_gt, w_ot, gnw, gnb, *gsil, *o_n, *ogT, o_bf, env["zTb"],
             env["w_x2s"], env["musb"], env["musb2"], dbufB, xmixB)
    A32.free(stats, *sq2, *st82, *og, *s_sb, env["xbias"])


def ph_ffn(p, env):
    op, da, A16, A32 = p.op, env["da"], env["A16"], env["A32"]
    ps, psb, xnT = env["ps"], env["psb"], env["xnT"]

    Hsb = A16.alloc((32, 1024), "Hsb")
    w1b = [A16.alloc((8, NC_, 128), f"w1b{j}") for j in range(2)]
    w2b = [A16.alloc((1, 1024), f"w2b{j}") for j in range(6)]
    wrt = A16.alloc((NC_, 8), "wrt")
    b2e = A16.alloc((D,), "b2e")
    gatesT = A16.alloc((T,), "gatesT")
    b1t = A32.alloc((NE * 32,), "b1t")
    gates = A32.alloc((NT, 8), "gates")
    gf32 = A32.alloc((16,), "gf32")
    gtmp = A32.alloc((16,), "gtmp")
    acc = A32.alloc((8, D), "acc")
    xres = [A32.alloc((D,), f"xres{j}") for j in range(2)]
    sres = [A32.alloc((D,), f"sres{j}") for j in range(2)]
    gfin = A32.alloc((D,), "gfin")

    op("gpsimd", lambda E: E.dma_start(
        out=wrt.ap(), in_=da["w_routert"].rearrange("(c p) m -> p c m", p=128)),
       writes=[wrt.buf])
    op("gpsimd", lambda E: E.dma_start(out=b2e[0:NE, :], in_=da["b2ext"][:, :]),
       writes=[b2e.buf])
    op("sync", lambda E: E.dma_start(
        out=b1t.ap().rearrange("p (a b) -> p a b", a=NE), in_=da["b1t"][:, :, :]),
       writes=[b1t.buf])

    for i in range(NT):
        t0 = i * 128
        bap, bbs = bankof(env, 2, 0)
        for c in range(NC_):
            op("tensor", lambda E, c=c, bap=bap, t0=t0: E.matmul(
                bap[:, 0:8], xnT[:, c, t0:t0 + 128], wrt[:, c, :],
                start=(c == 0), stop=(c == NC_ - 1)),
               reads=([xnT.buf, wrt.buf] if c in (0, NC_ - 1) else []),
               writes=bbs if c == NC_ - 1 else [],
               war=bbs if c == 0 else [], inc=(c == NC_ - 1))
        op("vector", lambda E, bap=bap: E.reduce_max(gtmp[:, 0:1], bap[:, 0:8], axis=AX.X),
           reads=bbs, writes=[gtmp.buf])
        op("vector", lambda E: E.tensor_scalar_mul(gtmp[:, 1:2], gtmp[:, 0:1], -1.0),
           reads=[gtmp.buf], writes=[gtmp.buf])
        op("scalar", lambda E, bap=bap, i=i: E.activation(
            gates[:, i, :], bap[:, 0:8], AF.Exp, bias=gtmp[:, 1:2]),
           reads=bbs + [gtmp.buf], writes=[gates.buf])
        op("vector", lambda E, i=i: E.reduce_sum(gtmp[:, 2:3], gates[:, i, :], axis=AX.X),
           reads=[gates.buf], writes=[gtmp.buf])
        op("vector", lambda E: E.reciprocal(gtmp[:, 3:4], gtmp[:, 2:3]),
           reads=[gtmp.buf], writes=[gtmp.buf])
        op("vector", lambda E, i=i: E.tensor_scalar(
            gates[:, i, :], gates[:, i, :], gtmp[:, 3:4], None, ALU.mult),
           reads=[gtmp.buf, gates.buf], writes=[gates.buf])
        op("vector", lambda E, i=i: E.reduce_max(gtmp[:, 4:5], gates[:, i, :], axis=AX.X),
           reads=[gates.buf], writes=[gtmp.buf])
        op("vector", lambda E, i=i: E.tensor_scalar(
            gtmp[:, 8:16], gates[:, i, :], gtmp[:, 4:5], None, ALU.is_ge),
           reads=[gates.buf, gtmp.buf], writes=[gtmp.buf])
        op("vector", lambda E, i=i: E.scalar_tensor_tensor(
            gtmp[:, 8:16], gtmp[:, 8:16], -1e30, gates[:, i, :], ALU.mult, ALU.add),
           reads=[gtmp.buf, gates.buf], writes=[gtmp.buf])
        op("vector", lambda E: E.reduce_max(gtmp[:, 5:6], gtmp[:, 8:16], axis=AX.X),
           reads=[gtmp.buf], writes=[gtmp.buf])
        op("vector", lambda E, i=i: E.tensor_scalar(
            gtmp[:, 8:16], gates[:, i, :], gtmp[:, 5:6], None, ALU.is_ge),
           reads=[gates.buf, gtmp.buf], writes=[gtmp.buf])
        op("vector", lambda E, i=i: E.tensor_mul(
            gates[:, i, :], gates[:, i, :], gtmp[:, 8:16]),
           reads=[gtmp.buf, gates.buf], writes=[gates.buf])
        op("vector", lambda E, i=i: E.tensor_copy(gf32[:, 0:8], gates[:, i, :]),
           reads=[gates.buf], writes=[gf32.buf])
        op("vector", lambda E: E.memset(gf32[:, 8:16], 1.0), writes=[gf32.buf])
        tpg, tpg_bb = ps[3][:, 0:128], [psb[6]]
        op("tensor", lambda E, tpg=tpg: E.transpose(
            tpg[0:16, :], gf32.ap(), env["ident"].ap()),
           reads=[gf32.buf, env["ident"].buf], writes=tpg_bb, war=tpg_bb)
        op("vector", lambda E, tpg=tpg, t0=t0: E.tensor_copy(
            gatesT[0:9, t0:t0 + 128], tpg[0:9, :]),
           reads=tpg_bb, writes=[gatesT.buf])
    if "gates" in env["dbg"]:
        p.op("sync", lambda E: E.dma_start(
            out=env["dbg"]["gates"][:, :],
            in_=gates.ap().rearrange("p a b -> p (a b)")),
            reads=[gates.buf], dsem="b_dbg")

    for th in range(2):
        t0 = th * 1024
        for tb in range(8):
            tt = t0 + tb * 128
            for dh in range(2):
                bap, bbs = bankof(env, 2, dh)
                op("tensor", lambda E, bap=bap, tt=tt, dh=dh: E.matmul(
                    bap, gatesT[0:9, tt:tt + 128], b2e[0:9, dh * 512:(dh + 1) * 512],
                    start=True, stop=True),
                   reads=[gatesT.buf, b2e.buf], writes=bbs, war=bbs)
                op("vector", lambda E, tb=tb, dh=dh, bap=bap: E.tensor_copy(
                    acc[:, tb, dh * 512:(dh + 1) * 512], bap),
                   reads=bbs, writes=[acc.buf])
        for e in range(NE):
            idx = th * NE + e
            for q in range(4):
                wb = w1b[q % 2]
                if not (q < 2 and idx > 0):
                    qeng = "sync" if q % 2 == 0 else "gpsimd"
                    p.raw_wait(qeng, f"cast{e}", 32)
                    op(qeng, lambda E, e=e, q=q, wb=wb: E.dma_start(
                        out=wb.ap(), in_=env["w1s"][e, q * 8:(q + 1) * 8].rearrange(
                            "f (c p) m -> p f c m", p=128)),
                       writes=[wb.buf])
                for fi in range(8):
                    ffb = q * 8 + fi
                    banks = [bankof(env, (ffb % 2), tc) for tc in range(2)]
                    for c in range(NC_):
                        for tc in range(2):
                            bap, bbs = banks[tc]
                            op("tensor", lambda E, wb=wb, fi=fi, c=c, bap=bap,
                               tt=t0 + tc * 512: E.matmul(
                                bap, wb[:, fi, c, :], xnT[:, c, tt:tt + 512],
                                start=(c == 0), stop=(c == NC_ - 1)),
                               reads=([wb.buf, xnT.buf]
                                      if (c == 0 and tc == 0) or (c == NC_ - 1 and tc == 1)
                                      else []),
                               writes=bbs if c == NC_ - 1 else [],
                               war=bbs if c == 0 else [],
                               inc=(c == NC_ - 1))
                    for tc in range(2):
                        bap, bbs = banks[tc]
                        op("scalar", lambda E, e=e, ffb=ffb, tc=tc, bap=bap: E.activation(
                            Hsb[:, ffb, tc * 512:(tc + 1) * 512], bap,
                            AF.Gelu_apprx_tanh,
                            bias=b1t[:, e * 32 + ffb:e * 32 + ffb + 1]),
                           reads=bbs + [b1t.buf], writes=[Hsb.buf])
            for qq in range(2):
                for ffc in range(32):
                    pc = qq * 32 + ffc
                    wb2 = w2b[pc % 6]
                    qeng = "sync" if pc % 2 == 0 else "gpsimd"
                    p.raw_wait(qeng, f"cast{e}", 32)
                    op(qeng, lambda E, e=e, ffc=ffc, wb2=wb2: E.dma_start(
                        out=wb2.ap(), in_=env["w2s"][e, ffc * 128:(ffc + 1) * 128, :]
                        .rearrange("(f p) d -> p f d", p=128)),
                       writes=[wb2.buf])
                    for tb in range(4):
                        tbg = qq * 4 + tb
                        for dh in range(2):
                            first = (tb == 0 and dh == 0)
                            last = (tb == 3 and dh == 1)
                            op("tensor", lambda E, wb2=wb2, tb=tb, dh=dh,
                               tbg=tbg, ffc=ffc: E.matmul(
                                ps[tb][:, dh * 512:(dh + 1) * 512],
                                Hsb[:, ffc, tbg * 128:(tbg + 1) * 128],
                                wb2[:, 0, dh * 512:(dh + 1) * 512],
                                start=(ffc == 0), stop=(ffc == 31)),
                               reads=([wb2.buf, Hsb.buf] if (first or last) else []),
                               writes=[psb[2 * tb + dh]] if ffc == 31 else [],
                               war=[psb[2 * tb + dh]] if ffc == 0 else [],
                               inc=(ffc == 31 or last))
                if qq == 0 and idx + 1 < 2 * NE:
                    th2, e2 = divmod(idx + 1, NE)
                    for q2, peng in ((0, "scalar"), (1, "scalar")):
                        wb = w1b[q2]
                        p.raw_wait(peng, f"cast{e2}", 32)
                        op(peng, lambda E, e2=e2, q2=q2, wb=wb: E.dma_start(
                            out=wb.ap(), in_=env["w1s"][e2, q2 * 8:(q2 + 1) * 8]
                            .rearrange("f (c p) m -> p f c m", p=128)),
                           writes=[wb.buf], dsem="b_" + wb.buf.name + "_pf")
                for tb in range(4):
                    tbg = qq * 4 + tb
                    ti = th * 8 + tbg
                    for dh in range(2):
                        sl = slice(dh * 512, (dh + 1) * 512)
                        if e < 8:
                            op("vector", lambda E, tb=tb, tbg=tbg, ti=ti, e=e, sl=sl: E.scalar_tensor_tensor(
                                acc[:, tbg, sl], ps[tb][:, sl], gates[:, ti, e:e + 1],
                                acc[:, tbg, sl], ALU.mult, ALU.add),
                               reads=[psb[2 * tb + dh], gates.buf, acc.buf],
                               writes=[acc.buf])
                        else:
                            op("vector", lambda E, tb=tb, tbg=tbg, sl=sl: E.tensor_add(
                                acc[:, tbg, sl], acc[:, tbg, sl], ps[tb][:, sl]),
                               reads=[psb[2 * tb + dh], acc.buf],
                               writes=[acc.buf])
        accname = f"acc{th}"
        if accname in env["dbg"]:
            p.op("sync", lambda E, accname=accname: E.dma_start(
                out=env["dbg"][accname][:, :],
                in_=acc.ap().rearrange("p a b -> p (a b)")),
                reads=[acc.buf], dsem="b_dbg")
        for tb in range(8):
            i = th * 8 + tb
            xr, sr = xres[tb % 2], sres[tb % 2]
            op("sync", lambda E, i=i, xr=xr: E.dma_start(
                out=xr.ap(), in_=da["x"][i * 128:(i + 1) * 128, :]), writes=[xr.buf])
            op("sync", lambda E, i=i, sr=sr: E.dma_start(
                out=sr.ap(), in_=env["s_scr"][i * 128:(i + 1) * 128, :]),
               reads=[env["s_scr_buf"]], writes=[sr.buf])
            op("scalar", lambda E, tb=tb: E.activation(
                gfin.ap(), acc[:, tb, :], AF.Tanh, scale=SCALE / 2),
               reads=[acc.buf], writes=[gfin.buf])
            op("vector", lambda E: E.tensor_scalar(
                gfin.ap(), gfin.ap(), 0.5, 0.5, ALU.mult, ALU.add),
               reads=[gfin.buf], writes=[gfin.buf])
            op("vector", lambda E, tb=tb: E.scalar_tensor_tensor(
                gfin.ap(), acc[:, tb, :], SCALE, gfin.ap(), ALU.mult, ALU.mult),
               reads=[acc.buf, gfin.buf], writes=[gfin.buf])
            op("vector", lambda E, sr=sr: E.tensor_mul(sr.ap(), sr.ap(), gfin.ap()),
               reads=[sr.buf, gfin.buf], writes=[sr.buf])
            op("vector", lambda E, xr=xr, sr=sr: E.tensor_add(sr.ap(), sr.ap(), xr.ap()),
               reads=[xr.buf, sr.buf], writes=[sr.buf])
            op("sync", lambda E, i=i, sr=sr: E.dma_start(
                out=env["out"][i * 128:(i + 1) * 128, :], in_=sr.ap()),
               reads=[sr.buf], dsem=f"b_out{tb % 2}")
    p.raw_wait("sync", "b_out0", p.cnt.get("b_out0", 0))
    p.raw_wait("sync", "b_out1", p.cnt.get("b_out1", 0))


def kernel(**inputs):
    nc = bass.Bass("TRN2", target_bir_lowering=False)
    build(nc)
    g = host_prep(inputs)
    x = np.ascontiguousarray(inputs["L0_attn_hidden_states"], dtype=np.float32)
    in_maps = []
    for c in range(8):
        m = dict(g)
        m["x"] = np.ascontiguousarray(x[c])
        in_maps.append(m)
    res = run_bass_kernel_spmd(nc, in_maps, core_ids=list(range(8)))
    return np.stack([res.results[c]["out"] for c in range(8)], axis=0)




# version 18
# speedup vs baseline: 1.5497x; 1.0057x over previous
import math
from contextlib import ExitStack

import numpy as np

import concourse.bass as bass
import concourse.mybir as mybir
from concourse.bass_utils import run_bass_kernel_spmd

F32 = mybir.dt.float32
BF16 = mybir.dt.bfloat16
AF = mybir.ActivationFunctionType
ALU = mybir.AluOpType
AX = mybir.AxisListType

T = 2048
D = 1024
HH = 4
KD = 512
VD = 1024
VH = 256
C = 128
CSUB = 32
NT = T // 128
NC_ = D // 128
NCH = T // C
DFF = 4096
NE = 9
EPS = 1e-5
SCALE = 1.0 / math.sqrt(1.25)

A16_N = 75776
A32_N = 14848

ENGS = ("tensor", "vector", "scalar", "sync", "gpsimd")
DMA_ENGS = ("sync", "gpsimd")


class Buf:
    __slots__ = ("name", "last_w", "readers")

    def __init__(self, name):
        self.name = name
        self.last_w = None
        self.readers = []


class View:
    __slots__ = ("arena", "off", "dims", "buf", "nelem")

    def __init__(self, arena, off, dims, name):
        self.arena = arena
        self.off = off
        self.dims = tuple(dims)
        self.nelem = int(np.prod(dims)) if dims else 1
        self.buf = Buf(name)

    def ap(self):
        a = self.arena.h[:, self.off:self.off + self.nelem]
        if len(self.dims) > 1:
            pat = "p (" + " ".join(f"d{i}" for i in range(len(self.dims))) + ") -> p " + \
                  " ".join(f"d{i}" for i in range(len(self.dims)))
            a = a.rearrange(pat, **{f"d{i}": d for i, d in enumerate(self.dims)})
        return a

    def __getitem__(self, idx):
        return self.ap()[idx]


class Arena:

    def __init__(self, handle, n):
        self.h = handle
        self.free_list = [(0, n)]

    def alloc(self, dims, name):
        ne = int(np.prod(dims))
        for i, (off, sz) in enumerate(self.free_list):
            aoff = (off + 15) // 16 * 16
            pad = aoff - off
            if sz >= ne + pad:
                rest = sz - ne - pad
                repl = []
                if pad:
                    repl.append((off, pad))
                if rest:
                    repl.append((aoff + ne, rest))
                self.free_list[i:i + 1] = repl
                return View(self, aoff, dims, name)
        raise RuntimeError(f"arena OOM for {name} ({ne})")

    def free(self, *views):
        for v in views:
            self.free_list.append((v.off, v.nelem))
        self.free_list.sort()
        merged = []
        for off, sz in self.free_list:
            if merged and merged[-1][0] + merged[-1][1] >= off:
                po, ps_ = merged[-1]
                merged[-1] = (po, max(ps_, off + sz - po))
            else:
                merged.append((off, sz))
        self.free_list = merged


class P:

    def __init__(self, nc):
        self.nc = nc
        self.stack = None
        self.prog = {e: [] for e in ENGS}
        self.cnt = {e: 0 for e in ENGS}
        self.waited = {e: {} for e in ENGS}
        self.sems = {}

    def getsem(self, name):
        if name not in self.sems:
            self.sems[name] = self.stack.enter_context(self.nc.semaphore(name))
            self.cnt[name] = 0
        return name

    def op(self, eng, fn, reads=(), writes=(), war=(), inc=True, dsem=None):
        waits = {}
        self_sync = eng != "tensor"

        def add(sem, cntv):
            if (sem != eng or self_sync) and cntv > waits.get(sem, -1):
                waits[sem] = cntv

        for b in reads:
            if b.last_w:
                add(*b.last_w)
        for b in tuple(writes) + tuple(war):
            if b.last_w:
                add(*b.last_w)
            for r in b.readers:
                add(*r)
        wl = []
        wd = self.waited[eng]
        for sem, v in waits.items():
            if wd.get(sem, -1) < v:
                wd[sem] = v
                wl.append((sem, v))
        if eng in DMA_ENGS or dsem is not None:
            if dsem is None:
                assert writes, "DMA op needs writes or dsem"
                dsem = "b_" + writes[0].name
            csem = self.getsem(dsem)
            amt = 16
        else:
            csem = eng
            amt = 1
        if inc:
            self.cnt[csem] += amt
            mark = (csem, self.cnt[csem])
            for b in writes:
                b.last_w = mark
                b.readers = []
            for b in reads:
                b.readers.append(mark)
        else:
            assert not writes
        sems = self.sems

        def run(E, wl=wl, fn=fn, inc=inc, csem=csem, amt=amt):
            for sem, v in wl:
                E.wait_ge(sems[sem], v)
            ins = fn(E)
            if inc:
                ins.then_inc(sems[csem], amt)
        self.prog[eng].append(run)

    def raw_wait(self, eng, sem, val):
        if val <= 0:
            return
        wd = self.waited[eng]
        if sem != eng and wd.get(sem, -1) >= val:
            return
        if sem != eng:
            wd[sem] = val
        sems = self.sems
        self.prog[eng].append(lambda E: E.wait_ge(sems[sem], val))

    def barrier(self):
        snap = dict(self.cnt)
        for e in ENGS:
            for o, v in snap.items():
                if o != e and o not in DMA_ENGS:
                    self.raw_wait(e, o, v)


def host_prep(inp):
    g = {}
    g["w_x1t"] = inp["L0_attn_W_x1"].T
    g["w_x2s"] = inp["L0_attn_W_x2"].reshape(D, 5, 32).transpose(2, 1, 0)
    g["xbias_t"] = inp["L0_attn_x_bias"].reshape(5, NC_, 128).transpose(2, 0, 1)
    g["mu_x_t"] = inp["L0_attn_mu_x"].reshape(NC_, 128).T
    g["w_rt"] = inp["L0_attn_W_r"].T
    g["w_kt"] = inp["L0_attn_W_k"].T
    g["a_wt"] = inp["L0_attn_A_w"].T
    g["b_wt"] = inp["L0_attn_B_w"].T
    g["bw_t"] = inp["L0_attn_b_w"].reshape(4, 128).T
    g["w_vt"] = inp["L0_attn_W_v"].T
    g["w_gt"] = inp["L0_attn_W_g"].T
    g["w_ot"] = inp["L0_attn_W_o"].T
    g["u_t"] = inp["L0_attn_bonus"].T
    g["gnw_rep"] = np.broadcast_to(inp["L0_attn_gn_weight"], (128, D))
    g["gnb_rep"] = np.broadcast_to(inp["L0_attn_gn_bias"], (128, D))
    pm = np.zeros((128, 4, CSUB), np.float32)
    for i in range(4):
        s = np.arange(128)[:, None]
        t = i * CSUB + np.arange(CSUB)[None, :]
        pm[:, i, :] = (s < t) & (s < (i + 1) * CSUB)
    g["pmask"] = pm
    g["ident"] = np.eye(128, dtype=np.float32)
    g["w_routert"] = inp["L0_ffn_w_router"].T
    g["b2ext"] = np.concatenate([inp["L0_ffn_b2"], inp["L0_ffn_bs2"][None]], 0)
    w1 = np.concatenate([inp["L0_ffn_W1"], inp["L0_ffn_Ws1"][None]], 0)
    w2 = np.concatenate([inp["L0_ffn_W2"], inp["L0_ffn_Ws2"][None]], 0)
    b1 = np.concatenate([inp["L0_ffn_b1"], inp["L0_ffn_bs1"][None]], 0)
    g["w1p"] = w1.transpose(0, 2, 1).reshape(NE, D, 32, 128).transpose(0, 2, 1, 3)
    g["w2p"] = w2.transpose(0, 2, 1)
    g["b1t"] = b1.reshape(NE, 32, 128).transpose(2, 0, 1)
    return {k: np.ascontiguousarray(v, dtype=np.float32) for k, v in g.items()}


DRAM_SPECS = [
    ("w_x1t", (D, 160)), ("w_x2s", (32, 5, D)), ("xbias_t", (128, 5, NC_)),
    ("mu_x_t", (128, NC_)), ("w_rt", (D, KD)), ("w_kt", (D, KD)),
    ("a_wt", (D, 64)), ("b_wt", (64, KD)), ("bw_t", (128, 4)),
    ("w_vt", (D, VD)), ("w_gt", (D, VD)), ("w_ot", (D, D)),
    ("u_t", (128, 4)), ("gnw_rep", (128, D)), ("gnb_rep", (128, D)),
    ("pmask", (128, 4, CSUB)), ("ident", (128, 128)),
    ("w_routert", (D, 8)), ("b2ext", (NE, D)),
    ("w1p", (NE, 32, D, 128)), ("w2p", (NE, DFF, D)), ("b1t", (128, NE, 32)),
]


def build(nc, dbg=()):
    p = P(nc)
    da = {}
    da["x"] = nc.dram_tensor("x", [T, D], F32, kind="ExternalInput").ap()
    for name, shape in DRAM_SPECS:
        da[name] = nc.dram_tensor(name, list(shape), F32, kind="ExternalInput").ap()
    out_ap = nc.dram_tensor("out", [T, D], F32, kind="ExternalOutput").ap()
    s_scr = nc.dram_tensor("s_scr", [T, D], F32, kind="Internal").ap()
    w1s = nc.dram_tensor("w1s", [NE, 32, D, 128], BF16, kind="Internal").ap()
    w2s = nc.dram_tensor("w2s", [NE, DFF, D], BF16, kind="Internal").ap()
    dbg_out = {}
    for name, shape in dbg:
        dbg_out[name] = nc.dram_tensor("dbg_" + name, list(shape), F32,
                                       kind="ExternalOutput").ap()

    with ExitStack() as st:
        a16h = st.enter_context(nc.sbuf_tensor("arena16", [128, A16_N], BF16))
        a32h = st.enter_context(nc.sbuf_tensor("arena32", [128, A32_N], F32))
        A16 = Arena(a16h, A16_N)
        A32 = Arena(a32h, A32_N)
        ps = [st.enter_context(nc.psum_tensor(f"ps{i}", [128, 1024], F32))
              for i in range(4)]
        psb = [Buf(f"ps{i}h{j}") for i in range(4) for j in range(2)]
        p.stack = st
        for e in ENGS:
            p.sems[e] = st.enter_context(nc.semaphore("sem_" + e))
        for e in range(NE):
            p.sems[f"cast{e}"] = st.enter_context(nc.semaphore(f"sem_cast{e}"))
            p.cnt[f"cast{e}"] = 0

        env = dict(da=da, out=out_ap, s_scr=s_scr, s_scr_buf=Buf("s_scr"),
                   w1s=w1s, w2s=w2s,
                   dbg=dbg_out, ps=ps, psb=psb, A16=A16, A32=A32)
        ph_prologue(p, env)
        ph_A(p, env)
        ph_B(p, env)
        ph_B2(p, env)
        ph_ffn(p, env)

        with nc.Block() as blk:
            for e in ENGS:
                def runner(E, e=e):
                    for fn in p.prog[e]:
                        fn(E)
                getattr(blk, e)(runner)
    return nc


def bankof(env, i, half):
    if half is None:
        return env["ps"][i][:, :], [env["psb"][2 * i], env["psb"][2 * i + 1]]
    return (env["ps"][i][:, half * 512:(half + 1) * 512],
            [env["psb"][2 * i + half]])


def dump16(p, env, view, name):
    if name not in env["dbg"]:
        return
    p.op("gpsimd", lambda E: E.dma_start(
        out=env["dbg"][name][:, :], in_=view.ap().rearrange(
            "p " + " ".join(f"d{i}" for i in range(len(view.dims))) +
            " -> p (" + " ".join(f"d{i}" for i in range(len(view.dims))) + ")")
        if len(view.dims) > 1 else view.ap()),
        reads=[view.buf], dsem="b_dbg")


def dump32(p, env, view, name):
    if name not in env["dbg"]:
        return
    p.op("sync", lambda E: E.dma_start(
        out=env["dbg"][name][:, :], in_=view.ap().rearrange(
            "p " + " ".join(f"d{i}" for i in range(len(view.dims))) +
            " -> p (" + " ".join(f"d{i}" for i in range(len(view.dims))) + ")")
        if len(view.dims) > 1 else view.ap()),
        reads=[view.buf], dsem="b_dbg")


def ph_prologue(p, env):
    op, da, A16, A32, ps, psb = p.op, env["da"], env["A16"], env["A32"], env["ps"], env["psb"]

    xnT = A16.alloc((NC_, T), "xnT")
    onecol = A16.alloc((8,), "onecol")
    w_x2s = A16.alloc((5, D), "w_x2s")
    zTb = A16.alloc((2, T), "zTb")
    musb = A16.alloc((512,), "musb")
    musb2 = A16.alloc((512,), "musb2")
    ident = A32.alloc((128,), "ident")
    masks = A32.alloc((4, CSUB), "pmask")
    small32 = A32.alloc((64,), "small32")
    xbias = A32.alloc((5 * NC_,), "xbias")
    ones32 = A32.alloc((128,), "ones32")
    epsb = A32.alloc((2,), "epsb")
    env.update(xnT=xnT, onecol=onecol, w_x2s=w_x2s, zTb=zTb, musb=musb,
               musb2=musb2, ident=ident, masks=masks, small32=small32,
               xbias=xbias, ones32=ones32, epsb=epsb)
    env["MU_X"] = lambda c: small32[:, c:c + 1]
    env["BW"] = lambda kb: small32[:, 8 + kb:9 + kb]
    env["UU"] = lambda h: small32[:, 12 + h:13 + h]

    op("sync", lambda E: E.dma_start(out=ident.ap(), in_=da["ident"][:, :]),
       writes=[ident.buf])
    op("sync", lambda E: E.dma_start(out=masks.ap().rearrange("p a b -> p (a b)"),
                                     in_=da["pmask"][:, :, :].rearrange("p a b -> p (a b)")),
       writes=[masks.buf])
    op("sync", lambda E: E.dma_start(out=small32[:, 0:8], in_=da["mu_x_t"][:, :]),
       writes=[small32.buf])
    op("sync", lambda E: E.dma_start(out=small32[:, 8:12], in_=da["bw_t"][:, :]),
       writes=[small32.buf])
    op("sync", lambda E: E.dma_start(out=small32[:, 12:16], in_=da["u_t"][:, :]),
       writes=[small32.buf])
    op("sync", lambda E: E.dma_start(out=xbias.ap().rearrange("p (a b) -> p a b", a=5),
                                     in_=da["xbias_t"][:, :, :]),
       writes=[xbias.buf])
    op("vector", lambda E: E.memset(ones32.ap(), 1.0), writes=[ones32.buf])
    op("vector", lambda E: E.memset(epsb.ap(), EPS), writes=[epsb.buf])
    op("vector", lambda E: E.memset(onecol.ap(), 1.0), writes=[onecol.buf])
    op("gpsimd", lambda E: E.dma_start(
        out=w_x2s[0:32, :, :], in_=da["w_x2s"][:, :, :]), writes=[w_x2s.buf])

    xtile = [A32.alloc((D,), f"xtile{j}") for j in range(2)]
    xcn = A32.alloc((D,), "xcn")
    stat = A32.alloc((8,), "lnstat")
    sqv = A32.alloc((D,), "lnsq")
    for i in range(NT):
        xt = xtile[i % 2]
        op("sync", lambda E, i=i, xt=xt: E.dma_start(
            out=xt.ap(), in_=da["x"][i * 128:(i + 1) * 128, :]), writes=[xt.buf])
        op("vector", lambda E, xt=xt: E.reduce_sum(stat[:, 0:1], xt.ap(), axis=AX.X),
           reads=[xt.buf], writes=[stat.buf])
        op("vector", lambda E: E.tensor_scalar_mul(stat[:, 1:2], stat[:, 0:1], 1.0 / D),
           reads=[stat.buf], writes=[stat.buf])
        op("vector", lambda E, xt=xt: E.tensor_scalar(
            xcn.ap(), xt.ap(), stat[:, 1:2], None, ALU.subtract),
           reads=[xt.buf, stat.buf], writes=[xcn.buf])
        op("scalar", lambda E: E.activation(sqv.ap(), xcn.ap(), AF.Square),
           reads=[xcn.buf], writes=[sqv.buf])
        op("vector", lambda E: E.reduce_sum(stat[:, 2:3], sqv.ap(), axis=AX.X),
           reads=[sqv.buf], writes=[stat.buf])
        op("vector", lambda E: E.tensor_scalar_mul(stat[:, 3:4], stat[:, 2:3], 1.0 / D),
           reads=[stat.buf], writes=[stat.buf])
        op("scalar", lambda E: E.activation(stat[:, 4:5], stat[:, 3:4], AF.Ln, bias=epsb[:, 0:1]),
           reads=[stat.buf, epsb.buf], writes=[stat.buf])
        op("scalar", lambda E: E.activation(stat[:, 5:6], stat[:, 4:5], AF.Exp, scale=-0.5),
           reads=[stat.buf], writes=[stat.buf])
        op("vector", lambda E: E.tensor_scalar(xcn.ap(), xcn.ap(), stat[:, 5:6], None, ALU.mult),
           reads=[stat.buf, xcn.buf], writes=[xcn.buf])
        tp = ps[3][:, 0:128]
        for c in range(NC_):
            op("tensor", lambda E, c=c, tp=tp: E.transpose(
                tp, xcn[:, c * 128:(c + 1) * 128], ident.ap()),
               reads=[xcn.buf, ident.buf], writes=[psb[6]], war=[psb[6]])
            op("vector", lambda E, c=c, i=i, tp=tp: E.tensor_copy(
                xnT[:, c, i * 128:(i + 1) * 128], tp),
               reads=[psb[6]], writes=[xnT.buf])
    p.barrier()
    A32.free(xtile[0], xtile[1], xcn, stat, sqv)
    dump16(p, env, xnT, "xnT")


def emit_casts(p, env, experts):
    sems = p.sems
    for e in experts:
        def c1(E, e=e):
            E.dma_start(out=env["w1s"][e], in_=env["da"]["w1p"][e]).then_inc(
                sems[f"cast{e}"], 16)
        def c2(E, e=e):
            E.dma_start(out=env["w2s"][e], in_=env["da"]["w2p"][e]).then_inc(
                sems[f"cast{e}"], 16)
        p.prog["gpsimd"].append(c1)
        p.prog["gpsimd"].append(c2)


def emit_delta(p, env, dbuf, t0, width):
    op, xnT = p.op, env["xnT"]
    for c in range(NC_):
        if t0 == 0:
            op("vector", lambda E, c=c: E.tensor_scalar_mul(
                dbuf[:, c, 0:1], xnT[:, c, 0:1], -1.0),
               reads=[xnT.buf], writes=[dbuf.buf])
            op("vector", lambda E, c=c, w=width: E.tensor_sub(
                dbuf[:, c, 1:w], xnT[:, c, 0:w - 1], xnT[:, c, 1:w]),
               reads=[xnT.buf], writes=[dbuf.buf])
        else:
            op("vector", lambda E, c=c, t0=t0, w=width: E.tensor_sub(
                dbuf[:, c, 0:w], xnT[:, c, t0 - 1:t0 + w - 1], xnT[:, c, t0:t0 + w]),
               reads=[xnT.buf], writes=[dbuf.buf])


def emit_xmix(p, env, zview, zrow, slot, dbuf, xmix, t0, width, pbanks):
    op, xnT, xbias = p.op, env["xnT"], env["xbias"]
    musb2 = (env["musb"], env["musb2"])
    for c in range(NC_):
        bap, bbs = pbanks[c % 2]
        mb = musb2[c % 2]
        op("tensor", lambda E, c=c, bap=bap, zrow=zrow, slot=slot, t0=t0, w=width: E.matmul(
            bap[:, 0:w], env["w_x2s"][0:32, slot, c * 128:(c + 1) * 128],
            zview[0:32, zrow, t0:t0 + w], start=True, stop=True),
           reads=[zview.buf, env["w_x2s"].buf], writes=bbs, war=bbs)
        op("vector", lambda E, c=c, bap=bap, slot=slot, w=width, mb=mb: E.scalar_tensor_tensor(
            mb[:, 0:w], bap[:, 0:w], xbias[:, slot * NC_ + c:slot * NC_ + c + 1],
            dbuf[:, c, 0:w], ALU.add, ALU.mult),
           reads=bbs + [xbias.buf, dbuf.buf], writes=[mb.buf])
        op("vector", lambda E, c=c, t0=t0, w=width, mb=mb: E.tensor_add(
            xmix[:, c, 0:w], mb[:, 0:w], xnT[:, c, t0:t0 + w]),
           reads=[mb.buf, xnT.buf], writes=[xmix.buf])


def emit_proj(p, env, xmix, wview, width, nkb, pbank_fn, evac_fn):
    op = p.op
    for kb in range(nkb):
        bap, bbs = pbank_fn(kb)
        for c in range(NC_):
            op("tensor", lambda E, kb=kb, c=c, bap=bap, w=width: E.matmul(
                bap, wview[:, c, kb * 128:(kb + 1) * 128], xmix[:, c, 0:w],
                start=(c == 0), stop=(c == NC_ - 1)),
               reads=([wview.buf, xmix.buf] if c in (0, NC_ - 1) else []),
               writes=bbs if c == NC_ - 1 else [],
               war=bbs if c == 0 else [],
               inc=(c == NC_ - 1))
        evac_fn(kb, bap, bbs)


def ph_A(p, env):
    op, da, A16, A32 = p.op, env["da"], env["A16"], env["A32"]
    ps, psb, xnT = env["ps"], env["psb"], env["xnT"]

    rT = A16.alloc((4, T), "rT")
    kT = A16.alloc((4, T), "kT")
    zTa = A16.alloc((3, T), "zTa")
    dbufA = A16.alloc((NC_, 512), "dbufA")
    xmixA = A16.alloc((NC_, 512), "xmixA")
    t1w = A16.alloc((512,), "t1w")
    w_x1t = A16.alloc((NC_, 160), "w_x1t")
    w_rt = A16.alloc((NC_, KD), "w_rt")
    w_kt = A16.alloc((NC_, KD), "w_kt")
    a_wt = A16.alloc((NC_, 64), "a_wt")
    b_wt = A16.alloc((KD,), "b_wt")
    ew = A32.alloc((4, T), "ew")
    env.update(rT=rT, kT=kT, ew=ew)
    zTb, musb, xbias = env["zTb"], env["musb"], env["xbias"]

    for name, view in [("w_x1t", w_x1t), ("w_rt", w_rt), ("w_kt", w_kt),
                       ("a_wt", a_wt)]:
        op("gpsimd", lambda E, name=name, view=view: E.dma_start(
            out=view.ap(), in_=da[name].rearrange("(c p) m -> p c m", p=128)),
           writes=[view.buf])
    op("gpsimd", lambda E: E.dma_start(out=b_wt[0:64, :], in_=da["b_wt"][:, :]),
       writes=[b_wt.buf])

    for ts in range(4):
        t0 = ts * 512
        emit_delta(p, env, dbufA, t0, 512)
        for c in range(NC_):
            op("vector", lambda E, c=c, t0=t0: E.scalar_tensor_tensor(
                xmixA[:, c, :], dbufA[:, c, :], env["MU_X"](c),
                xnT[:, c, t0:t0 + 512], ALU.mult, ALU.add),
               reads=[dbufA.buf, xnT.buf], writes=[xmixA.buf])
        zbank = [bankof(env, 0, 0), bankof(env, 0, 1), bankof(env, 1, 0),
                 bankof(env, 1, 1), bankof(env, 2, 0)]
        for n in range(5):
            bap, bbs = zbank[n]
            for c in range(NC_):
                op("tensor", lambda E, n=n, c=c, bap=bap: E.matmul(
                    bap[0:32, :], w_x1t[:, c, n * 32:(n + 1) * 32], xmixA[:, c, :],
                    start=(c == 0), stop=(c == NC_ - 1)),
                   reads=([w_x1t.buf, xmixA.buf] if c in (0, NC_ - 1) else []),
                   writes=bbs if c == NC_ - 1 else [],
                   war=bbs if c == 0 else [], inc=(c == NC_ - 1))
        for n in range(5):
            bap, bbs = zbank[n]
            tgt = (zTa, n) if n < 3 else (zTb, n - 3)
            op("scalar", lambda E, bap=bap, tgt=tgt, t0=t0: E.activation(
                tgt[0][0:32, tgt[1], t0:t0 + 512], bap[0:32, :], AF.Tanh),
               reads=bbs, writes=[tgt[0].buf])

        emit_xmix(p, env, zTa, 0, 0, dbufA, xmixA, t0, 512,
                  (bankof(env, 2, 1), bankof(env, 2, 0)))
        def evac_r(kb, bap, bbs, t0=t0):
            op("vector", lambda E, kb=kb, bap=bap: E.tensor_copy(
                rT[:, kb, t0:t0 + 512], bap),
               reads=bbs, writes=[rT.buf])
        emit_proj(p, env, xmixA, w_rt, 512, 4, lambda kb: bankof(env, 3, kb % 2), evac_r)

        emit_xmix(p, env, zTa, 1, 1, dbufA, xmixA, t0, 512,
                  (bankof(env, 2, 1), bankof(env, 2, 0)))
        bap, bbs = bankof(env, 0, 0)
        for c in range(NC_):
            op("tensor", lambda E, c=c, bap=bap: E.matmul(
                bap[0:64, :], a_wt[:, c, :], xmixA[:, c, :],
                start=(c == 0), stop=(c == NC_ - 1)),
               reads=([a_wt.buf, xmixA.buf] if c in (0, NC_ - 1) else []),
               writes=bbs if c == NC_ - 1 else [],
               war=bbs if c == 0 else [], inc=(c == NC_ - 1))
        op("scalar", lambda E, bap=bap: E.activation(t1w[0:64, :], bap[0:64, :], AF.Tanh),
           reads=bbs, writes=[t1w.buf])
        for kb in range(4):
            bap2, bbs2 = bankof(env, 1, kb % 2)
            op("tensor", lambda E, kb=kb, bap2=bap2: E.matmul(
                bap2, b_wt[0:64, kb * 128:(kb + 1) * 128], t1w[0:64, :],
                start=True, stop=True),
               reads=[t1w.buf, b_wt.buf], writes=bbs2, war=bbs2)
            op("scalar", lambda E, kb=kb, bap2=bap2, t0=t0: E.activation(
                ew[:, kb, t0:t0 + 512], bap2, AF.Exp, bias=env["BW"](kb)),
               reads=bbs2, writes=[ew.buf])

        emit_xmix(p, env, zTa, 2, 2, dbufA, xmixA, t0, 512,
                  (bankof(env, 2, 1), bankof(env, 2, 0)))
        def evac_k(kb, bap, bbs, t0=t0):
            op("vector", lambda E, kb=kb, bap=bap: E.tensor_copy(
                kT[:, kb, t0:t0 + 512], bap),
               reads=bbs, writes=[kT.buf])
        emit_proj(p, env, xmixA, w_kt, 512, 4, lambda kb: bankof(env, 3, kb % 2), evac_k)

    p.barrier()
    A16.free(zTa, dbufA, xmixA, t1w, w_x1t, w_rt, w_kt, a_wt, b_wt)
    dump16(p, env, rT, "rT")
    dump16(p, env, kT, "kT")
    dump32(p, env, ew, "ew")
    dump16(p, env, zTb, "zTb")


def ph_B(p, env):
    op, da, A16, A32 = p.op, env["da"], env["A16"], env["A32"]
    ps, psb, xnT = env["ps"], env["psb"], env["xnT"]
    rT, kT, ew, zTb, masks, ident = env["rT"], env["kT"], env["ew"], env["zTb"], env["masks"], env["ident"]

    w_vt = A16.alloc((NC_, VD), "w_vt")
    o_bf = A16.alloc((NT, VD), "o_bf")
    Vc = A16.alloc((VD,), "Vc")
    S_bf = [A16.alloc((VH,), f"S_bf{h}") for h in range(HH)]
    dbufB = A16.alloc((NC_, 128), "dbufB")
    xmixB = A16.alloc((NC_, 128), "xmixB")
    eqv = [A16.alloc((C,), f"eqv{j}") for j in range(2)]
    qbf = [A16.alloc((C,), f"qbf{j}") for j in range(2)]
    kbf = [A16.alloc((C,), f"kbf{j}") for j in range(2)]
    Pfull = [A16.alloc((C,), f"Pfull{j}") for j in range(2)]
    khT = [A16.alloc((C,), f"khT{j}") for j in range(2)]
    S_fp = [A32.alloc((VH,), f"S_fp{h}") for h in range(HH)]
    negbuf = [A32.alloc((130,), f"negbuf{j}") for j in range(2)]
    negneg = [A32.alloc((C,), f"negneg{j}") for j in range(2)]
    rku = [A16.alloc((C,), f"rku{j}") for j in range(2)]
    dcol = [A32.alloc((2,), f"dcol{j}") for j in range(2)]
    khat = [A32.alloc((C,), f"khat{j}") for j in range(2)]
    env.update(o_bf=o_bf, dbufB=dbufB, xmixB=xmixB)

    op("gpsimd", lambda E: E.dma_start(
        out=w_vt.ap(), in_=da["w_vt"].rearrange("(c p) m -> p c m", p=128)),
       writes=[w_vt.buf])
    emit_casts(p, env, range(0, 5))
    for h in range(HH):
        op("vector", lambda E, h=h: E.memset(S_fp[h].ap(), 0.0), writes=[S_fp[h].buf])
        op("vector", lambda E, h=h: E.memset(S_bf[h].ap(), 0.0), writes=[S_bf[h].buf])
    for j in range(2):
        op("vector", lambda E, j=j: E.memset(negbuf[j][:, 0:2], 0.0),
           writes=[negbuf[j].buf])
        op("vector", lambda E, j=j: E.memset(Pfull[j].ap(), 0.0),
           writes=[Pfull[j].buf])

    for ci in range(NCH):
        t0 = ci * C
        emit_delta(p, env, dbufB, t0, 128)
        emit_xmix(p, env, zTb, 0, 3, dbufB, xmixB, t0, 128,
                  (bankof(env, 2, 1), bankof(env, 2, 0)))
        for vh in range(2):
            bap, bbs = bankof(env, 0, vh)
            for c in range(NC_):
                op("tensor", lambda E, vh=vh, c=c, bap=bap: E.matmul(
                    bap, xmixB[:, c, :], w_vt[:, c, vh * 512:(vh + 1) * 512],
                    start=(c == 0), stop=(c == NC_ - 1)),
                   reads=([xmixB.buf, w_vt.buf] if c in (0, NC_ - 1) else []),
                   writes=bbs if c == NC_ - 1 else [],
                   war=bbs if c == 0 else [], inc=(c == NC_ - 1))
            op("vector", lambda E, vh=vh, bap=bap: E.tensor_copy(
                Vc[:, vh * 512:(vh + 1) * 512], bap),
               reads=bbs, writes=[Vc.buf])

        for h in range(HH):
            j = h % 2
            if j == 0:
                o_ap, o_bb = ps[1][:, 0:256], [psb[2]]
                sd_ap, d_ap, sd_bb = ps[3][:, 0:256], ps[3][:, 256:257], [psb[6]]
                tp_ap, tp_bb = ps[3][:, 640:768], [psb[7]]
            else:
                o_ap, o_bb = ps[1][:, 512:768], [psb[3]]
                sd_ap, d_ap, sd_bb = ps[0][:, 0:256], ps[0][:, 256:257], [psb[0]]
                tp_ap, tp_bb = ps[0][:, 640:768], [psb[1]]
            nbb = negbuf[j]
            nng = negneg[j]
            NB = lambda a, b, nbb=nbb: nbb[:, 2 + a:2 + b]
            NBP = lambda a, b, nbb=nbb: nbb[:, 1 + a:1 + b]
            eq, qb, kb, Pf, kT_, rk, dc, kh = (eqv[j], qbf[j], kbf[j], Pfull[j],
                                               khT[j], rku[j], dcol[j], khat[j])
            vslice = Vc[:, h * VH:(h + 1) * VH]
            op("vector", lambda E, h=h, t0=t0, NB=NB: E.tensor_tensor_scan(
                NB(0, 128), env["ones32"][:, 0:128], ew[:, h, t0:t0 + 128], 0.0,
                ALU.mult, ALU.add),
               reads=[ew.buf, env["ones32"].buf], writes=[nbb.buf])
            op("vector", lambda E, nng=nng, NB=NB: E.tensor_scalar_mul(nng.ap(), NB(0, 128), -1.0),
               reads=[nbb.buf], writes=[nng.buf])
            op("scalar", lambda E, eq=eq, NBP=NBP: E.activation(eq[:, 0:128], NBP(0, 128), AF.Exp, scale=-1.0),
               reads=[nbb.buf], writes=[eq.buf])
            op("vector", lambda E, h=h, t0=t0, qb=qb, eq=eq: E.tensor_mul(
                qb[:, 0:128], rT[:, h, t0:t0 + 128], eq[:, 0:128]),
               reads=[rT.buf, eq.buf], writes=[qb.buf])
            op("tensor", lambda E, h=h, o_ap=o_ap, qb=qb: E.matmul(
                o_ap, qb[:, 0:128], S_bf[h].ap(), start=True, stop=False),
               reads=[qb.buf, S_bf[h].buf], war=o_bb, inc=False)
            op("vector", lambda E, h=h, t0=t0, rk=rk: E.tensor_mul(
                rk[:, 0:128], rT[:, h, t0:t0 + 128], kT[:, h, t0:t0 + 128]),
               reads=[rT.buf, kT.buf], writes=[rk.buf])
            op("vector", lambda E, h=h, rk=rk: E.tensor_scalar(
                rk[:, 0:128], rk[:, 0:128], env["UU"](h), None, ALU.mult),
               reads=[rk.buf], writes=[rk.buf])
            op("tensor", lambda E, d_ap=d_ap, rk=rk: E.matmul(
                d_ap, rk[:, 0:128], env["onecol"][:, 0:1],
                start=True, stop=True),
               reads=[rk.buf, env["onecol"].buf], writes=sd_bb, war=sd_bb)
            op("vector", lambda E, dc=dc, d_ap=d_ap: E.tensor_copy(dc[:, 0:1], d_ap),
               reads=sd_bb, writes=[dc.buf])
            for i in range(4):
                sw = (i + 1) * CSUB
                pm_ap, pm_bb = ((ps[2][:, 0:CSUB], [psb[4]]) if i % 2 == 0
                                else (ps[2][:, 512:512 + CSUB], [psb[5]]))
                mid = i * CSUB + CSUB // 2 - 1
                op("scalar", lambda E, i=i, mid=mid, eq=eq, NBP=NBP, NB=NB: E.activation(
                    eq[:, 0:CSUB], NBP(i * CSUB, (i + 1) * CSUB), AF.Exp,
                    scale=-1.0, bias=NB(mid, mid + 1)),
                   reads=[nbb.buf], writes=[eq.buf])
                op("vector", lambda E, i=i, h=h, t0=t0, qb=qb, eq=eq: E.tensor_mul(
                    qb[:, 0:CSUB], rT[:, h, t0 + i * CSUB:t0 + (i + 1) * CSUB],
                    eq[:, 0:CSUB]),
                   reads=[rT.buf, eq.buf], writes=[qb.buf])
                op("scalar", lambda E, sw=sw, mid=mid, eq=eq, NB=NB, nng=nng: E.activation(
                    eq[:, 0:sw], NB(0, sw), AF.Exp, bias=nng[:, mid:mid + 1]),
                   reads=[nbb.buf, nng.buf], writes=[eq.buf])
                op("vector", lambda E, h=h, t0=t0, sw=sw, kb=kb, eq=eq: E.tensor_mul(
                    kb[:, 0:sw], kT[:, h, t0:t0 + sw], eq[:, 0:sw]),
                   reads=[kT.buf, eq.buf], writes=[kb.buf])
                op("tensor", lambda E, sw=sw, pm_ap=pm_ap, kb=kb, qb=qb: E.matmul(
                    pm_ap[0:sw, :], kb[:, 0:sw], qb[:, 0:CSUB],
                    start=True, stop=True),
                   reads=[kb.buf, qb.buf], writes=pm_bb, war=pm_bb)
                op("vector", lambda E, i=i, sw=sw, pm_ap=pm_ap: E.tensor_mul(
                    pm_ap[0:sw, :], pm_ap[0:sw, :], masks[0:sw, i, :]),
                   reads=pm_bb + [masks.buf], writes=pm_bb)
                op("vector", lambda E, i=i, sw=sw, pm_ap=pm_ap, Pf=Pf: E.tensor_copy(
                    Pf[0:sw, i * CSUB:(i + 1) * CSUB], pm_ap[0:sw, :]),
                   reads=pm_bb, writes=[Pf.buf])
            op("tensor", lambda E, o_ap=o_ap, vslice=vslice, Pf=Pf: E.matmul(
                o_ap, Pf.ap(), vslice, start=False, stop=True),
               reads=[Pf.buf, Vc.buf], writes=o_bb, inc=True)
            op("vector", lambda E, h=h, ci=ci, vslice=vslice, o_ap=o_ap, dc=dc: E.scalar_tensor_tensor(
                o_bf[:, ci, h * VH:(h + 1) * VH], vslice, dc[:, 0:1], o_ap,
                ALU.mult, ALU.add),
               reads=[Vc.buf, dc.buf] + o_bb, writes=[o_bf.buf])
            op("scalar", lambda E, eq=eq, NB=NB, nng=nng: E.activation(
                eq[:, 0:128], NB(0, 128), AF.Exp, bias=nng[:, 127:128]),
               reads=[nbb.buf, nng.buf], writes=[eq.buf])
            op("vector", lambda E, h=h, t0=t0, kh=kh, eq=eq: E.tensor_mul(
                kh.ap(), kT[:, h, t0:t0 + 128], eq[:, 0:128]),
               reads=[kT.buf, eq.buf], writes=[kh.buf])
            op("tensor", lambda E, tp_ap=tp_ap, kh=kh: E.transpose(tp_ap, kh.ap(), ident.ap()),
               reads=[kh.buf, ident.buf], writes=tp_bb, war=tp_bb)
            op("vector", lambda E, tp_ap=tp_ap, kT_=kT_: E.tensor_copy(kT_.ap(), tp_ap),
               reads=tp_bb, writes=[kT_.buf])
            op("tensor", lambda E, vslice=vslice, sd_ap=sd_ap, kT_=kT_: E.matmul(
                sd_ap, kT_.ap(), vslice, start=True, stop=True),
               reads=[kT_.buf, Vc.buf], writes=sd_bb, war=sd_bb)
            op("scalar", lambda E, dc=dc, NB=NB: E.activation(
                dc[:, 1:2], NB(127, 128), AF.Exp, scale=-1.0),
               reads=[nbb.buf], writes=[dc.buf])
            op("vector", lambda E, h=h, dc=dc, sd_ap=sd_ap: E.scalar_tensor_tensor(
                S_fp[h].ap(), S_fp[h].ap(), dc[:, 1:2], sd_ap,
                ALU.mult, ALU.add),
               reads=[S_fp[h].buf, dc.buf] + sd_bb, writes=[S_fp[h].buf])
            op("vector", lambda E, h=h: E.tensor_copy(S_bf[h].ap(), S_fp[h].ap()),
               reads=[S_fp[h].buf], writes=[S_bf[h].buf])
    p.barrier()
    A16.free(w_vt, Vc, *S_bf, *eqv, *qbf, *kbf, *Pfull, *khT, *rku,
             env["rT"], env["kT"])
    A32.free(*S_fp, *negbuf, *negneg, *dcol, *khat, env["ew"])
    dump16(p, env, o_bf, "o_bf")


def ph_B2(p, env):
    op, da, A16, A32 = p.op, env["da"], env["A16"], env["A32"]
    ps, psb, xnT = env["ps"], env["psb"], env["xnT"]
    o_bf, zTb, dbufB, xmixB = env["o_bf"], env["zTb"], env["dbufB"], env["xmixB"]

    w_gt = A16.alloc((NC_, VD), "w_gt")
    w_ot = A16.alloc((NC_, D), "w_ot")
    gnw = A16.alloc((D,), "gnw16")
    gnb = A16.alloc((D,), "gnb16")
    gsil = [A16.alloc((VD,), f"gsil{j}") for j in range(2)]
    o_n = [A16.alloc((VD,), f"o_n{j}") for j in range(2)]
    ogT = [A16.alloc((NC_, 128), f"ogT{j}") for j in range(2)]
    stats = A32.alloc((NT, HH, 2), "gnstats")
    sq2 = [A32.alloc((VH,), f"gnsq{j}") for j in range(2)]
    st82 = [A32.alloc((8,), f"gnst8{j}") for j in range(2)]
    og = [A32.alloc((VD,), f"og{j}") for j in range(2)]
    s_sb = [A32.alloc((D,), f"s_sb{j}") for j in range(2)]

    op("gpsimd", lambda E: E.dma_start(
        out=w_gt.ap(), in_=da["w_gt"].rearrange("(c p) m -> p c m", p=128)),
       writes=[w_gt.buf])
    op("gpsimd", lambda E: E.dma_start(
        out=w_ot.ap(), in_=da["w_ot"].rearrange("(c p) m -> p c m", p=128)),
       writes=[w_ot.buf])
    op("gpsimd", lambda E: E.dma_start(out=gnw.ap(), in_=da["gnw_rep"][:, :]),
       writes=[gnw.buf])
    op("gpsimd", lambda E: E.dma_start(out=gnb.ap(), in_=da["gnb_rep"][:, :]),
       writes=[gnb.buf])
    emit_casts(p, env, range(5, NE))

    for i in range(NT):
        for h in range(HH):
            osl = o_bf[:, i, h * VH:(h + 1) * VH]
            sq = sq2[(i * HH + h) % 2]
            st8 = st82[(i * HH + h) % 2]
            op("scalar", lambda E, osl=osl, sq=sq, st8=st8: E.activation(
                sq.ap(), osl, AF.Identity, accum_out=st8[:, 0:1]),
               reads=[o_bf.buf], writes=[sq.buf, st8.buf])
            op("scalar", lambda E, osl=osl, sq=sq, st8=st8: E.activation(
                sq.ap(), osl, AF.Square, accum_out=st8[:, 1:2]),
               reads=[o_bf.buf], writes=[sq.buf, st8.buf])
            op("vector", lambda E, i=i, h=h, st8=st8: E.tensor_scalar_mul(
                stats[:, i, h, 0:1], st8[:, 0:1], 1.0 / VH),
               reads=[st8.buf], writes=[stats.buf])
            op("vector", lambda E, st8=st8: E.tensor_scalar_mul(st8[:, 2:3], st8[:, 1:2], 1.0 / VH),
               reads=[st8.buf], writes=[st8.buf])
            op("vector", lambda E, i=i, h=h, st8=st8: E.tensor_mul(
                st8[:, 3:4], stats[:, i, h, 0:1], stats[:, i, h, 0:1]),
               reads=[stats.buf], writes=[st8.buf])
            op("vector", lambda E, st8=st8: E.tensor_sub(st8[:, 4:5], st8[:, 2:3], st8[:, 3:4]),
               reads=[st8.buf], writes=[st8.buf])
            op("scalar", lambda E, st8=st8: E.activation(st8[:, 5:6], st8[:, 4:5], AF.Ln, bias=env["epsb"][:, 0:1]),
               reads=[st8.buf, env["epsb"].buf], writes=[st8.buf])
            op("scalar", lambda E, i=i, h=h, st8=st8: E.activation(
                stats[:, i, h, 1:2], st8[:, 5:6], AF.Exp, scale=-0.5),
               reads=[st8.buf], writes=[stats.buf])

    for i in range(NT):
        t0 = i * 128
        pj = i % 2
        gs, on_, ogj, ogTj, ssb = gsil[pj], o_n[pj], og[pj], ogT[pj], s_sb[pj]
        emit_delta(p, env, dbufB, t0, 128)
        emit_xmix(p, env, zTb, 1, 4, dbufB, xmixB, t0, 128,
                  (bankof(env, 2, 1), bankof(env, 2, 0)))
        for vh in range(2):
            bap, bbs = bankof(env, 0, vh)
            for c in range(NC_):
                op("tensor", lambda E, vh=vh, c=c, bap=bap: E.matmul(
                    bap, xmixB[:, c, :], w_gt[:, c, vh * 512:(vh + 1) * 512],
                    start=(c == 0), stop=(c == NC_ - 1)),
                   reads=([xmixB.buf, w_gt.buf] if c in (0, NC_ - 1) else []),
                   writes=bbs if c == NC_ - 1 else [],
                   war=bbs if c == 0 else [], inc=(c == NC_ - 1))
            op("scalar", lambda E, vh=vh, bap=bap, gs=gs: E.activation(
                gs[:, vh * 512:(vh + 1) * 512], bap, AF.Silu),
               reads=bbs, writes=[gs.buf])
        for h in range(HH):
            op("vector", lambda E, i=i, h=h, on_=on_: E.tensor_scalar(
                on_[:, h * VH:(h + 1) * VH], o_bf[:, i, h * VH:(h + 1) * VH],
                stats[:, i, h, 0:1], stats[:, i, h, 1:2], ALU.subtract, ALU.mult),
               reads=[o_bf.buf, stats.buf], writes=[on_.buf])
        op("vector", lambda E, on_=on_: E.tensor_mul(on_.ap(), on_.ap(), gnw.ap()),
           reads=[gnw.buf, on_.buf], writes=[on_.buf])
        op("vector", lambda E, on_=on_: E.tensor_add(on_.ap(), on_.ap(), gnb.ap()),
           reads=[gnb.buf, on_.buf], writes=[on_.buf])
        op("vector", lambda E, on_=on_, gs=gs, ogj=ogj: E.tensor_mul(ogj.ap(), on_.ap(), gs.ap()),
           reads=[gs.buf, on_.buf], writes=[ogj.buf])
        tp_ap, tp_bb = (ps[3][:, 640:768], [psb[7]]) if pj == 0 else (ps[0][:, 640:768], [psb[1]])
        for c in range(NC_):
            op("tensor", lambda E, c=c, tp_ap=tp_ap, ogj=ogj: E.transpose(
                tp_ap, ogj[:, c * 128:(c + 1) * 128], env["ident"].ap()),
               reads=[ogj.buf, env["ident"].buf], writes=tp_bb, war=tp_bb)
            op("vector", lambda E, c=c, tp_ap=tp_ap, ogTj=ogTj: E.tensor_copy(ogTj[:, c, :], tp_ap),
               reads=tp_bb, writes=[ogTj.buf])
        for oh in range(2):
            bap, bbs = bankof(env, 1, oh) if pj == 0 else bankof(env, 2, oh)
            for c in range(NC_):
                op("tensor", lambda E, oh=oh, c=c, bap=bap, ogTj=ogTj: E.matmul(
                    bap, ogTj[:, c, :], w_ot[:, c, oh * 512:(oh + 1) * 512],
                    start=(c == 0), stop=(c == NC_ - 1)),
                   reads=([ogTj.buf, w_ot.buf] if c in (0, NC_ - 1) else []),
                   writes=bbs if c == NC_ - 1 else [],
                   war=bbs if c == 0 else [], inc=(c == NC_ - 1))
            op("vector", lambda E, oh=oh, bap=bap, ssb=ssb: E.tensor_copy(
                ssb[:, oh * 512:(oh + 1) * 512], bap),
               reads=bbs, writes=[ssb.buf])
        op("sync", lambda E, i=i, ssb=ssb: E.dma_start(
            out=env["s_scr"][i * 128:(i + 1) * 128, :], in_=ssb.ap()),
           reads=[ssb.buf], writes=[env["s_scr_buf"]])
        if "s" in env["dbg"]:
            op("sync", lambda E, i=i, ssb=ssb: E.dma_start(
                out=env["dbg"]["s"][i * 128:(i + 1) * 128, :], in_=ssb.ap()),
               reads=[ssb.buf], dsem="b_dbg")
    p.barrier()
    A16.free(w_gt, w_ot, gnw, gnb, *gsil, *o_n, *ogT, o_bf, env["zTb"],
             env["w_x2s"], env["musb"], env["musb2"], dbufB, xmixB)
    A32.free(stats, *sq2, *st82, *og, *s_sb, env["xbias"])


def ph_ffn(p, env):
    op, da, A16, A32 = p.op, env["da"], env["A16"], env["A32"]
    ps, psb, xnT = env["ps"], env["psb"], env["xnT"]

    Hsb = A16.alloc((32, 1024), "Hsb")
    w1b = [A16.alloc((8, NC_, 128), f"w1b{j}") for j in range(2)]
    w2b = [A16.alloc((1, 1024), f"w2b{j}") for j in range(6)]
    wrt = A16.alloc((NC_, 8), "wrt")
    b2e = A16.alloc((D,), "b2e")
    gatesT = A16.alloc((T,), "gatesT")
    b1t = A32.alloc((NE * 32,), "b1t")
    gates = A32.alloc((NT, 8), "gates")
    gf32 = A32.alloc((16,), "gf32")
    gtmp = A32.alloc((16,), "gtmp")
    acc = A32.alloc((8, D), "acc")
    xres = [A32.alloc((D,), f"xres{j}") for j in range(2)]
    sres = [A32.alloc((D,), f"sres{j}") for j in range(2)]
    gfin = A32.alloc((D,), "gfin")

    op("gpsimd", lambda E: E.dma_start(
        out=wrt.ap(), in_=da["w_routert"].rearrange("(c p) m -> p c m", p=128)),
       writes=[wrt.buf])
    op("gpsimd", lambda E: E.dma_start(out=b2e[0:NE, :], in_=da["b2ext"][:, :]),
       writes=[b2e.buf])
    op("sync", lambda E: E.dma_start(
        out=b1t.ap().rearrange("p (a b) -> p a b", a=NE), in_=da["b1t"][:, :, :]),
       writes=[b1t.buf])

    for i in range(NT):
        t0 = i * 128
        bap, bbs = bankof(env, 2, 0)
        for c in range(NC_):
            op("tensor", lambda E, c=c, bap=bap, t0=t0: E.matmul(
                bap[:, 0:8], xnT[:, c, t0:t0 + 128], wrt[:, c, :],
                start=(c == 0), stop=(c == NC_ - 1)),
               reads=([xnT.buf, wrt.buf] if c in (0, NC_ - 1) else []),
               writes=bbs if c == NC_ - 1 else [],
               war=bbs if c == 0 else [], inc=(c == NC_ - 1))
        op("vector", lambda E, bap=bap: E.reduce_max(gtmp[:, 0:1], bap[:, 0:8], axis=AX.X),
           reads=bbs, writes=[gtmp.buf])
        op("vector", lambda E: E.tensor_scalar_mul(gtmp[:, 1:2], gtmp[:, 0:1], -1.0),
           reads=[gtmp.buf], writes=[gtmp.buf])
        op("scalar", lambda E, bap=bap, i=i: E.activation(
            gates[:, i, :], bap[:, 0:8], AF.Exp, bias=gtmp[:, 1:2]),
           reads=bbs + [gtmp.buf], writes=[gates.buf])
        op("vector", lambda E, i=i: E.reduce_sum(gtmp[:, 2:3], gates[:, i, :], axis=AX.X),
           reads=[gates.buf], writes=[gtmp.buf])
        op("vector", lambda E: E.reciprocal(gtmp[:, 3:4], gtmp[:, 2:3]),
           reads=[gtmp.buf], writes=[gtmp.buf])
        op("vector", lambda E, i=i: E.tensor_scalar(
            gates[:, i, :], gates[:, i, :], gtmp[:, 3:4], None, ALU.mult),
           reads=[gtmp.buf, gates.buf], writes=[gates.buf])
        op("vector", lambda E, i=i: E.reduce_max(gtmp[:, 4:5], gates[:, i, :], axis=AX.X),
           reads=[gates.buf], writes=[gtmp.buf])
        op("vector", lambda E, i=i: E.tensor_scalar(
            gtmp[:, 8:16], gates[:, i, :], gtmp[:, 4:5], None, ALU.is_ge),
           reads=[gates.buf, gtmp.buf], writes=[gtmp.buf])
        op("vector", lambda E, i=i: E.scalar_tensor_tensor(
            gtmp[:, 8:16], gtmp[:, 8:16], -1e30, gates[:, i, :], ALU.mult, ALU.add),
           reads=[gtmp.buf, gates.buf], writes=[gtmp.buf])
        op("vector", lambda E: E.reduce_max(gtmp[:, 5:6], gtmp[:, 8:16], axis=AX.X),
           reads=[gtmp.buf], writes=[gtmp.buf])
        op("vector", lambda E, i=i: E.tensor_scalar(
            gtmp[:, 8:16], gates[:, i, :], gtmp[:, 5:6], None, ALU.is_ge),
           reads=[gates.buf, gtmp.buf], writes=[gtmp.buf])
        op("vector", lambda E, i=i: E.tensor_mul(
            gates[:, i, :], gates[:, i, :], gtmp[:, 8:16]),
           reads=[gtmp.buf, gates.buf], writes=[gates.buf])
        op("vector", lambda E, i=i: E.tensor_copy(gf32[:, 0:8], gates[:, i, :]),
           reads=[gates.buf], writes=[gf32.buf])
        op("vector", lambda E: E.memset(gf32[:, 8:16], 1.0), writes=[gf32.buf])
        tpg, tpg_bb = ps[3][:, 0:128], [psb[6]]
        op("tensor", lambda E, tpg=tpg: E.transpose(
            tpg[0:16, :], gf32.ap(), env["ident"].ap()),
           reads=[gf32.buf, env["ident"].buf], writes=tpg_bb, war=tpg_bb)
        op("vector", lambda E, tpg=tpg, t0=t0: E.tensor_copy(
            gatesT[0:9, t0:t0 + 128], tpg[0:9, :]),
           reads=tpg_bb, writes=[gatesT.buf])
    if "gates" in env["dbg"]:
        p.op("sync", lambda E: E.dma_start(
            out=env["dbg"]["gates"][:, :],
            in_=gates.ap().rearrange("p a b -> p (a b)")),
            reads=[gates.buf], dsem="b_dbg")

    pend = []

    def mk_combine_t1():
        its = []
        for tb in range(8):
            i = 8 + tb

            def it(tb=tb, i=i):
                xr, sr = xres[tb % 2], sres[tb % 2]
                op("scalar", lambda E, i=i, xr=xr: E.dma_start(
                    out=xr.ap(), in_=da["x"][i * 128:(i + 1) * 128, :]),
                   writes=[xr.buf], dsem="b_" + xr.buf.name + "_t1")
                op("scalar", lambda E, i=i, sr=sr: E.dma_start(
                    out=sr.ap(), in_=env["s_scr"][i * 128:(i + 1) * 128, :]),
                   reads=[env["s_scr_buf"]], writes=[sr.buf],
                   dsem="b_" + sr.buf.name + "_t1")
                op("scalar", lambda E, tb=tb: E.activation(
                    gfin.ap(), acc[:, tb, :], AF.Tanh, scale=SCALE / 2),
                   reads=[acc.buf], writes=[gfin.buf])
                op("vector", lambda E: E.tensor_scalar(
                    gfin.ap(), gfin.ap(), 0.5, 0.5, ALU.mult, ALU.add),
                   reads=[gfin.buf], writes=[gfin.buf])
                op("vector", lambda E, tb=tb: E.scalar_tensor_tensor(
                    gfin.ap(), acc[:, tb, :], SCALE, gfin.ap(), ALU.mult,
                    ALU.mult),
                   reads=[acc.buf, gfin.buf], writes=[gfin.buf])
                op("vector", lambda E, sr=sr: E.tensor_mul(
                    sr.ap(), sr.ap(), gfin.ap()),
                   reads=[sr.buf, gfin.buf], writes=[sr.buf])
                op("vector", lambda E, xr=xr, sr=sr: E.tensor_add(
                    sr.ap(), sr.ap(), xr.ap()),
                   reads=[xr.buf, sr.buf], writes=[sr.buf])
                op("scalar", lambda E, i=i, sr=sr: E.dma_start(
                    out=env["out"][i * 128:(i + 1) * 128, :], in_=sr.ap()),
                   reads=[sr.buf], dsem=f"b_out{2 + tb % 2}")
            its.append(it)
        return its

    for th in range(2):
        t0 = th * 1024
        for tb in range(8):
            tt = t0 + tb * 128
            for dh in range(2):
                bap, bbs = bankof(env, 2, dh)
                op("tensor", lambda E, bap=bap, tt=tt, dh=dh: E.matmul(
                    bap, gatesT[0:9, tt:tt + 128], b2e[0:9, dh * 512:(dh + 1) * 512],
                    start=True, stop=True),
                   reads=[gatesT.buf, b2e.buf], writes=bbs, war=bbs)
                op("vector", lambda E, tb=tb, dh=dh, bap=bap: E.tensor_copy(
                    acc[:, tb, dh * 512:(dh + 1) * 512], bap),
                   reads=bbs, writes=[acc.buf])
        for e in range(NE):
            idx = th * NE + e
            for q in range(4):
                wb = w1b[q % 2]
                if not (q < 2 and idx > 0):
                    qeng = "sync" if q % 2 == 0 else "gpsimd"
                    p.raw_wait(qeng, f"cast{e}", 32)
                    op(qeng, lambda E, e=e, q=q, wb=wb: E.dma_start(
                        out=wb.ap(), in_=env["w1s"][e, q * 8:(q + 1) * 8].rearrange(
                            "f (c p) m -> p f c m", p=128)),
                       writes=[wb.buf])
                for fi in range(8):
                    ffb = q * 8 + fi
                    banks = [bankof(env, (ffb % 2), tc) for tc in range(2)]
                    for c in range(NC_):
                        for tc in range(2):
                            bap, bbs = banks[tc]
                            op("tensor", lambda E, wb=wb, fi=fi, c=c, bap=bap,
                               tt=t0 + tc * 512: E.matmul(
                                bap, wb[:, fi, c, :], xnT[:, c, tt:tt + 512],
                                start=(c == 0), stop=(c == NC_ - 1)),
                               reads=([wb.buf, xnT.buf]
                                      if (c == 0 and tc == 0) or (c == NC_ - 1 and tc == 1)
                                      else []),
                               writes=bbs if c == NC_ - 1 else [],
                               war=bbs if c == 0 else [],
                               inc=(c == NC_ - 1))
                    for tc in range(2):
                        bap, bbs = banks[tc]
                        op("scalar", lambda E, e=e, ffb=ffb, tc=tc, bap=bap: E.activation(
                            Hsb[:, ffb, tc * 512:(tc + 1) * 512], bap,
                            AF.Gelu_apprx_tanh,
                            bias=b1t[:, e * 32 + ffb:e * 32 + ffb + 1]),
                           reads=bbs + [b1t.buf], writes=[Hsb.buf])
            for qq in range(2):
                for ffc in range(32):
                    pc = qq * 32 + ffc
                    wb2 = w2b[pc % 6]
                    qeng = "sync" if pc % 2 == 0 else "gpsimd"
                    p.raw_wait(qeng, f"cast{e}", 32)
                    op(qeng, lambda E, e=e, ffc=ffc, wb2=wb2: E.dma_start(
                        out=wb2.ap(), in_=env["w2s"][e, ffc * 128:(ffc + 1) * 128, :]
                        .rearrange("(f p) d -> p f d", p=128)),
                       writes=[wb2.buf])
                    for tb in range(4):
                        tbg = qq * 4 + tb
                        for dh in range(2):
                            first = (tb == 0 and dh == 0)
                            last = (tb == 3 and dh == 1)
                            op("tensor", lambda E, wb2=wb2, tb=tb, dh=dh,
                               tbg=tbg, ffc=ffc: E.matmul(
                                ps[tb][:, dh * 512:(dh + 1) * 512],
                                Hsb[:, ffc, tbg * 128:(tbg + 1) * 128],
                                wb2[:, 0, dh * 512:(dh + 1) * 512],
                                start=(ffc == 0), stop=(ffc == 31)),
                               reads=([wb2.buf, Hsb.buf] if (first or last) else []),
                               writes=[psb[2 * tb + dh]] if ffc == 31 else [],
                               war=[psb[2 * tb + dh]] if ffc == 0 else [],
                               inc=(ffc == 31 or last))
                    if pend and ffc in (4, 12, 20, 28):
                        pend.pop(0)()
                if qq == 0 and idx + 1 < 2 * NE:
                    th2, e2 = divmod(idx + 1, NE)
                    for q2, peng in ((0, "scalar"), (1, "scalar")):
                        wb = w1b[q2]
                        p.raw_wait(peng, f"cast{e2}", 32)
                        op(peng, lambda E, e2=e2, q2=q2, wb=wb: E.dma_start(
                            out=wb.ap(), in_=env["w1s"][e2, q2 * 8:(q2 + 1) * 8]
                            .rearrange("f (c p) m -> p f c m", p=128)),
                           writes=[wb.buf], dsem="b_" + wb.buf.name + "_pf")
                for tb in range(4):
                    tbg = qq * 4 + tb
                    ti = th * 8 + tbg
                    for dh in range(2):
                        sl = slice(dh * 512, (dh + 1) * 512)
                        if e < 8:
                            op("vector", lambda E, tb=tb, tbg=tbg, ti=ti, e=e, sl=sl: E.scalar_tensor_tensor(
                                acc[:, tbg, sl], ps[tb][:, sl], gates[:, ti, e:e + 1],
                                acc[:, tbg, sl], ALU.mult, ALU.add),
                               reads=[psb[2 * tb + dh], gates.buf, acc.buf],
                               writes=[acc.buf])
                        else:
                            op("vector", lambda E, tb=tb, tbg=tbg, sl=sl: E.tensor_add(
                                acc[:, tbg, sl], acc[:, tbg, sl], ps[tb][:, sl]),
                               reads=[psb[2 * tb + dh], acc.buf],
                               writes=[acc.buf])
                if th == 1 and e == NE - 1:
                    cmb = mk_combine_t1() if qq == 0 else cmb
                    pend.extend(cmb[0:4] if qq == 0 else cmb[4:8])
        accname = f"acc{th}"
        if accname in env["dbg"]:
            p.op("sync", lambda E, accname=accname: E.dma_start(
                out=env["dbg"][accname][:, :],
                in_=acc.ap().rearrange("p a b -> p (a b)")),
                reads=[acc.buf], dsem="b_dbg")
        for tb in range(8 if th == 0 else 0):
            i = th * 8 + tb
            xr, sr = xres[tb % 2], sres[tb % 2]
            op("sync", lambda E, i=i, xr=xr: E.dma_start(
                out=xr.ap(), in_=da["x"][i * 128:(i + 1) * 128, :]), writes=[xr.buf])
            op("sync", lambda E, i=i, sr=sr: E.dma_start(
                out=sr.ap(), in_=env["s_scr"][i * 128:(i + 1) * 128, :]),
               reads=[env["s_scr_buf"]], writes=[sr.buf])
            op("scalar", lambda E, tb=tb: E.activation(
                gfin.ap(), acc[:, tb, :], AF.Tanh, scale=SCALE / 2),
               reads=[acc.buf], writes=[gfin.buf])
            op("vector", lambda E: E.tensor_scalar(
                gfin.ap(), gfin.ap(), 0.5, 0.5, ALU.mult, ALU.add),
               reads=[gfin.buf], writes=[gfin.buf])
            op("vector", lambda E, tb=tb: E.scalar_tensor_tensor(
                gfin.ap(), acc[:, tb, :], SCALE, gfin.ap(), ALU.mult, ALU.mult),
               reads=[acc.buf, gfin.buf], writes=[gfin.buf])
            op("vector", lambda E, sr=sr: E.tensor_mul(sr.ap(), sr.ap(), gfin.ap()),
               reads=[sr.buf, gfin.buf], writes=[sr.buf])
            op("vector", lambda E, xr=xr, sr=sr: E.tensor_add(sr.ap(), sr.ap(), xr.ap()),
               reads=[xr.buf, sr.buf], writes=[sr.buf])
            op("sync", lambda E, i=i, sr=sr: E.dma_start(
                out=env["out"][i * 128:(i + 1) * 128, :], in_=sr.ap()),
               reads=[sr.buf], dsem=f"b_out{tb % 2}")
    while pend:
        pend.pop(0)()
    for bo in ("b_out0", "b_out1", "b_out2", "b_out3"):
        p.raw_wait("sync", bo, p.cnt.get(bo, 0))


def kernel(**inputs):
    nc = bass.Bass("TRN2", target_bir_lowering=False)
    build(nc)
    g = host_prep(inputs)
    x = np.ascontiguousarray(inputs["L0_attn_hidden_states"], dtype=np.float32)
    in_maps = []
    for c in range(8):
        m = dict(g)
        m["x"] = np.ascontiguousarray(x[c])
        in_maps.append(m)
    res = run_bass_kernel_spmd(nc, in_maps, core_ids=list(range(8)))
    return np.stack([res.results[c]["out"] for c in range(8)], axis=0)


